# version 19
# baseline (speedup 1.0000x reference)
import numpy as np
import concourse.bass as bass
import concourse.mybir as mybir

F32 = mybir.dt.float32
BF16 = mybir.dt.bfloat16
AF = mybir.ActivationFunctionType
ALU = mybir.AluOpType
AX = mybir.AxisListType

B = 2
S = 8192
D = 1024
NC = 8
TOK = 1024


class Ctr:
    def __init__(self):
        self.v = 0
    def inc(self, n=1):
        self.v += n
        return self.v



def build_l1():
    nc = bass.Bass("TRN2", target_bir_lowering=False)
    xT = nc.dram_tensor("xT", [B, 8, 128, TOK], F32, kind="ExternalInput").ap()
    wqT = nc.dram_tensor("wqT", [8, 128, D], F32, kind="ExternalInput").ap()
    wkT = nc.dram_tensor("wkT", [8, 128, D], F32, kind="ExternalInput").ap()
    wvT = nc.dram_tensor("wvT", [8, 128, D], F32, kind="ExternalInput").ap()
    qT = nc.dram_tensor("qT", [B, 8, 128, TOK], BF16, kind="ExternalOutput").ap()
    kT = nc.dram_tensor("kT", [B, 8, 128, TOK], BF16, kind="ExternalOutput").ap()
    vT = nc.dram_tensor("vT", [B, 8, 128, TOK], BF16, kind="ExternalOutput").ap()
    usum = nc.dram_tensor("usum", [128, 16], BF16, kind="ExternalOutput").ap()

    projs = [("k", wkT, kT), ("v", wvT, vT), ("q", wqT, wqT)]

    AC = {"wk0": 1, "x00": 2, "wk1": 3, "x10": 4, "x20": 5, "x30": 6,
          "wvc": 7, "wqc": 8}
    pe_c = Ctr()
    PE_AT = {}
    for pb in range(6):
        for th in range(2):
            for dc in range(8):
                pe_c.inc(); PE_AT[(pb, th, dc)] = pe_c.v
    ve_c = Ctr()
    EV = {}
    for pb in range(6):
        for th in range(2):
            for dc in range(8):
                ve_c.inc(); EV[(pb, th, dc)] = ve_c.v
                if pb in (2, 3) and th == 1:
                    ve_c.inc()
    ve_c.inc()
    VE_TOTAL = ve_c.v

    from contextlib import ExitStack
    with ExitStack() as _ctx:
        def _e(t):
            return _ctx.enter_context(t)
        x_sb = _e(nc.sbuf_tensor("x_sb", [128, B * 8 * TOK], BF16))
        x32 = _e(nc.sbuf_tensor("x32", [128, 8 * TOK], F32))
        w32a = _e(nc.sbuf_tensor("w32a", [128, 8 * D], F32))
        w32b = _e(nc.sbuf_tensor("w32b", [128, 8 * D], F32))
        w_sb0 = _e(nc.sbuf_tensor("w_sb0", [128, 8 * D], BF16))
        w_sb1 = _e(nc.sbuf_tensor("w_sb1", [128, 8 * D], BF16))
        st0 = _e(nc.sbuf_tensor("st0", [128, 8 * TOK], BF16))
        st1 = _e(nc.sbuf_tensor("st1", [128, 8 * TOK], BF16))
        us_sb = _e(nc.sbuf_tensor("us_sb", [128, 16], F32))
        usb_sb = _e(nc.sbuf_tensor("usb_sb", [128, 16], BF16))
        psb = [_e(nc.psum_tensor(f"psb{i}", [128, 512], F32)) for i in range(8)]
        g_x = [_e(nc.semaphore(f"g_x{i}")) for i in range(3)]
        s_wk0 = _e(nc.semaphore("s_wk0"))
        s_wk1 = _e(nc.semaphore("s_wk1"))
        s_wq = _e(nc.semaphore("s_wq"))
        s_w1 = _e(nc.semaphore("s_w1"))
        a_x = [_e(nc.semaphore(f"a_x{i}")) for i in range(2)]
        pe = _e(nc.semaphore("pe"))
        acts = _e(nc.semaphore("acts"))
        ev = _e(nc.semaphore("ev"))
        stA = _e(nc.semaphore("stA"))
        stB = _e(nc.semaphore("stB"))
        block = _e(nc.Block())
        w_sb = [w_sb0, w_sb1]
        stg = [st0, st1]
        stp = [stA, stB]
        XSLOT = {0: 0, 1: 1, 2: 0, 3: 1}
        XSEM = {0: a_x[0], 1: g_x[1], 2: a_x[1], 3: g_x[0]}
        XCNT = {0: 16, 1: 16, 2: 16, 3: 16}

        def xsrc(i):
            b, half = divmod(i, 2)
            return xT[b, half * 4:(half + 1) * 4].rearrange("hc p t -> p hc t")

        def xdst(i):
            s = XSLOT[i]
            return x32[:, s * 4 * TOK:(s + 1) * 4 * TOK].rearrange(
                "p (hc t) -> p hc t", hc=4)

        @block.gpsimd
        def _(g):
            g.dma_start(out=xdst(1), in_=xsrc(1)).then_inc(XSEM[1], 16)
            g.wait_ge(acts, AC["x10"])
            g.dma_start(out=xdst(3), in_=xsrc(3)).then_inc(XSEM[3], 16)

        @block.sync
        def _(sy):
            w32a3 = w32a[:, :].rearrange("p (hc d) -> p hc d", hc=8)
            w32b3 = w32b[:, :].rearrange("p (hc d) -> p hc d", hc=8)
            sy.dma_start(out=w32a[:, 0:4 * D].rearrange("p (hc d) -> p hc d", hc=4),
                         in_=wkT[0:4].rearrange("hc p d -> p hc d")).then_inc(s_wk0, 16)
            sy.dma_start(out=w32a[:, 4 * D:8 * D].rearrange("p (hc d) -> p hc d", hc=4),
                         in_=wkT[4:8].rearrange("hc p d -> p hc d")).then_inc(s_wk1, 16)
            sy.dma_start(out=w32b3,
                         in_=wvT.rearrange("hc p d -> p hc d")).then_inc(s_w1, 16)
            sy.wait_ge(acts, AC["wk1"])
            sy.dma_start(out=w32a3,
                         in_=wqT.rearrange("hc p d -> p hc d")).then_inc(s_wq, 16)
            npar = [0, 0]
            for pb in range(6):
                i, b = divmod(pb, 2)
                out = [kT, vT, qT][i]
                for h in range(4):
                    sy.wait_ge(ev, EV[(pb, 1, 2 * h + 1)])
                    sy.dma_start(out=out[b, 2 * h:2 * h + 2].rearrange("dc p t -> p dc t"),
                                 in_=stg[pb % 2][:, 2 * h * TOK:(2 * h + 2) * TOK].rearrange(
                                     "p (dc t) -> p dc t", dc=2)).then_inc(stp[pb % 2], 16)
                    npar[pb % 2] += 1
            sy.wait_ge(ev, VE_TOTAL)
            sy.dma_start(out=usum, in_=usb_sb[:, :]).then_inc(stA, 16)
            npar[0] += 1
            sy.wait_ge(stA, 16 * npar[0])
            sy.wait_ge(stB, 16 * npar[1])

        @block.scalar
        def _(s):
            sc = Ctr()

            def cp(dst, srcap):
                s.wait_ge(acts, sc.v)
                s.copy(dst, srcap).then_inc(acts, 1)
                sc.inc()

            def xcast(i):
                b, half = divmod(i, 2)
                sl = XSLOT[i]
                s.wait_ge(XSEM[i], XCNT[i])
                cp(x_sb[:, (b * 8 + half * 4) * TOK:(b * 8 + half * 4 + 4) * TOK],
                   x32[:, sl * 4 * TOK:(sl + 1) * 4 * TOK])

            s.dma_start(out=xdst(0), in_=xsrc(0)).then_inc(XSEM[0], 16)
            s.wait_ge(s_wk0, 16)
            cp(w_sb0[:, 0:4 * D], w32a[:, 0:4 * D])
            assert sc.v == AC["wk0"]
            xcast(0)
            assert sc.v == AC["x00"]
            s.wait_ge(acts, AC["x00"])
            s.dma_start(out=xdst(2), in_=xsrc(2)).then_inc(XSEM[2], 16)
            s.wait_ge(s_wk1, 16)
            cp(w_sb0[:, 4 * D:8 * D], w32a[:, 4 * D:8 * D])
            assert sc.v == AC["wk1"]
            xcast(1)
            xcast(2)
            xcast(3)
            assert sc.v == AC["x30"]
            s.wait_ge(s_w1, 16)
            cp(w_sb1[:, :], w32b[:, :])
            assert sc.v == AC["wvc"]
            s.wait_ge(pe, PE_AT[(1, 1, 7)])
            s.wait_ge(s_wq, 16)
            cp(w_sb0[:, :], w32a[:, :])
            assert sc.v == AC["wqc"]

        @block.vector
        def _(v):
            vc = Ctr()

            def op(fn, *args, **kw):
                v.wait_ge(ev, vc.v)
                fn(*args, **kw).then_inc(ev, 1)
                vc.inc()

            for pb in range(6):
                i, b = divmod(pb, 2)
                if pb >= 2:
                    v.wait_ge(stp[pb % 2], 64 * (pb // 2))
                for th in range(2):
                    for dc in range(8):
                        v.wait_ge(pe, PE_AT[(pb, th, dc)])
                        sgn = -1.0 if (i == 2 and b == 1) else 1.0
                        op(v.tensor_scalar,
                           stg[pb % 2][:, dc * TOK + th * 512:dc * TOK + th * 512 + 512],
                           psb[dc][:, :], sgn, None, ALU.mult)
                        assert vc.v == EV[(pb, th, dc)]
                        if i == 1 and th == 1:
                            op(v.reduce_sum, us_sb[:, b * 8 + dc:b * 8 + dc + 1],
                               stg[pb % 2][:, dc * TOK:(dc + 1) * TOK], AX.X)
            op(v.tensor_copy, usb_sb[:, :], us_sb[:, :])
            assert vc.v == VE_TOTAL

        @block.tensor
        def _(t):
            for pb in range(6):
                i, b = divmod(pb, 2)
                w_ok = [AC["wk0"], AC["wk1"]] if i == 0 else (
                    [AC["wvc"]] * 2 if i == 1 else [AC["wqc"]] * 2)
                x_ok = [[AC["x00"], AC["x10"]], [AC["x20"], AC["x30"]]][b]
                for th in range(2):
                    for ph in range(2):
                        t.wait_ge(acts, max(w_ok[ph], x_ok[ph]))
                        for dc in range(8):
                            if ph == 0:
                                if (pb, th) != (0, 0):
                                    ppb, pth = (pb, 0) if th == 1 else (pb - 1, 1)
                                    t.wait_ge(ev, EV[(ppb, pth, dc)])
                            last = None
                            for hc in range(4 * ph, 4 * ph + 4):
                                last = t.matmul(
                                    psb[dc][:, :],
                                    w_sb[i % 2][:, hc * D + dc * 128:hc * D + dc * 128 + 128],
                                    x_sb[:, (b * 8 + hc) * TOK + th * 512:
                                         (b * 8 + hc) * TOK + th * 512 + 512],
                                    start=(hc == 0), stop=(hc == 7),
                                )
                            if ph == 1:
                                last.then_inc(pe, 1)
    return nc



def snake_tiles(c):
    return [c, 15 - c, 16 + c, 31 - c, 32 + c, 47 - c, 48 + c, 63 - c]


def l1_host_inputs(x, Wq, Wk, Wv):
    maps = []
    wqT = np.ascontiguousarray(Wq.T).reshape(8, 128, D)
    wkT = np.ascontiguousarray(Wk.T).reshape(8, 128, D)
    wvT = np.ascontiguousarray(Wv.T).reshape(8, 128, D)
    for c in range(NC):
        xs = x[:, c * TOK:(c + 1) * TOK, :]
        xT = np.ascontiguousarray(xs.transpose(0, 2, 1)).reshape(B, 8, 128, TOK)
        maps.append({"xT": xT, "wqT": wqT, "wkT": wkT, "wvT": wvT})
    return maps


def l1_host_outputs(res):
    import ml_dtypes
    qT = np.zeros((B, D, S), dtype=ml_dtypes.bfloat16)
    kT = np.zeros((B, D, S), dtype=ml_dtypes.bfloat16)
    vT = np.zeros((B, D, S), dtype=ml_dtypes.bfloat16)
    usum = np.zeros((B, NC, D), dtype=ml_dtypes.bfloat16)
    for c in range(NC):
        r = res.results[c]
        sl = slice(c * TOK, (c + 1) * TOK)
        qT[:, :, sl] = r["qT"].reshape(B, D, TOK)
        kT[:, :, sl] = r["kT"].reshape(B, D, TOK)
        vT[:, :, sl] = r["vT"].reshape(B, D, TOK)
        u = r["usum"]
        for b in range(B):
            usum[b, c] = u[:, b * 8:(b + 1) * 8].T.reshape(D)
    return qT, kT, vT, usum



def _pieces(w):
    out = []
    o = 0
    while o < w:
        c = min(512, w - o)
        out.append((o, c))
        o += c
    return out


def plan_l2():
    pe = Ctr(); acts = Ctr(); ve = Ctr(); st = Ctr()
    P = dict(pe_S={}, pe_V={}, pe_gc={}, acts_sig={}, acts_sq={}, acts_rstd={},
             acts_b16={}, ve_ready={}, ve_ready0={}, ve_evac={}, ve_xadd={}, ve_vt={},
             ve_vteps={}, ve_nmr={}, ve_apply={}, ve_b16={}, ve_gcev={},
             sld_K={}, sld_V={}, gld_x={}, st_x={}, st_b16={}, st_gc={})

    xc = [Ctr(), Ctr()]
    for p in range(8):
        for b in range(B):
            idx = p * 2 + b
            xc[idx % 2].inc(16)
            P["gld_x"][(b, p)] = xc[idx % 2].v

    for a in range(8):
        for rc in range(8):
            P["sld_K"][(a, rc)] = 16 * (a + 1)
        P["sld_V"][(a, 0)] = 16 * a
        P["sld_V"][(a, 1)] = 16 * a

    ve.inc(); P["ve_ones"] = ve.v

    for a in range(8):
        w = (8 - a) * 128
        for rc in range(8):
            pe.inc(); P["pe_S"][(a, rc)] = pe.v
        for rc in range(8):
            acts.inc(); P["acts_sig"][(a, rc)] = acts.v
        for b in range(B):
            acts.inc(); P["acts_sq"][(b, a)] = acts.v
        acts.inc(2); P["acts_rstd"][a] = acts.v
        for rc in range(8):
            ve.inc(2)
            if rc == 7:
                P["ve_ready0"][a] = ve.v
            ve.inc(2)
            if w > 128:
                ve.inc()
        P["ve_ready"][a] = ve.v
        for b in range(B):
            for p in range(a, 8):
                pe.inc(); P["pe_V"][(a, b, p)] = pe.v
                ve.inc(); P["ve_evac"][(a, b, p)] = ve.v
                if p == a:
                    ve.inc(); P["ve_xadd"][(b, p)] = ve.v
                    ve.inc()
                    ve.inc()
                    ve.inc(2)
                    P["ve_vt"][(b, p)] = ve.v
                    if b == 1:
                        P["ve_vteps"][a] = ve.v
        for b in range(B):
            ve.inc(); P["ve_nmr"][(b, a)] = ve.v
            ve.inc(); P["ve_apply"][(b, a)] = ve.v
    for p in range(8):
        ve.inc(); P["ve_b16"][(1, p)] = ve.v
    for p in range(8):
        acts.inc(); P["acts_b16"][(0, p)] = acts.v
    for b in range(B):
        pe.inc(); P["pe_gc"][b] = pe.v
        ve.inc(); P["ve_gcev"][b] = ve.v
    for a in range(8):
        for b in range(B):
            st.inc(16); P["st_x"][(b, a)] = st.v
    for b in range(B):
        st.inc(16); P["st_b16"][b] = st.v
    for b in range(B):
        st.inc(16); P["st_gc"][b] = st.v
    P["st_total"] = st.v
    P["sig_seq"] = [P["acts_sig"][(a, rc)] for a in range(8) for rc in range(8)]
    return P


def build_l2():
    nc = bass.Bass("TRN2", target_bir_lowering=False)
    qT_in = nc.dram_tensor("qT_in", [B, 8, 128, 1024], BF16, kind="ExternalInput").ap()
    K_in = nc.dram_tensor("K_in", [8, 8, B, 128, 8, 128], BF16, kind="ExternalInput").ap()
    V_in = nc.dram_tensor("V_in", [B, 8, 8, 128, 1024], BF16, kind="ExternalInput").ap()
    us_in = nc.dram_tensor("us_in", [8, B * 1024], BF16, kind="ExternalInput").ap()
    c19_in = nc.dram_tensor("c19_in", [8, 1024], BF16, kind="ExternalInput").ap()
    m1_in = nc.dram_tensor("m1_in", [128, 2048], BF16, kind="ExternalInput").ap()
    m2_in = nc.dram_tensor("m2_in", [128, 2048], BF16, kind="ExternalInput").ap()

    xq_in = nc.dram_tensor("xq_in", [B, 8, 128, 1024], F32, kind="ExternalInput").ap()
    x1 = nc.dram_tensor("x1", [B, 8, 128, 1024], F32, kind="ExternalOutput").ap()
    x1b = nc.dram_tensor("x1b", [B, 8, 128, 1024], BF16, kind="ExternalOutput").ap()
    gcp = nc.dram_tensor("gcp", [B, 1024], F32, kind="ExternalOutput").ap()

    P = plan_l2()
    EPS = 1e-5
    RAWEPS = EPS * 1024.0 * 1024.0

    from contextlib import ExitStack
    with ExitStack() as _ctx:
        def _e(x):
            return _ctx.enter_context(x)
        qT_sb = _e(nc.sbuf_tensor("qT_sb", [128, B * 8 * 1024], BF16))
        K_rg = _e(nc.sbuf_tensor("K_rg", [128, 10 * 1024], BF16))
        V_sb = _e(nc.sbuf_tensor("V_sb", [128, 2 * 8 * 1024], BF16))
        at0 = _e(nc.sbuf_tensor("at0", [128, 8 * 1024], BF16))
        at1 = _e(nc.sbuf_tensor("at1", [128, 8 * 1024], BF16))
        acc = _e(nc.sbuf_tensor("acc", [128, B * 8 * 1024], F32))
        m1m = _e(nc.sbuf_tensor("m1m", [128, 2048], BF16))
        m2m = _e(nc.sbuf_tensor("m2m", [128, 2048], BF16))

        c19_sb = _e(nc.sbuf_tensor("c19_sb", [8, 1024], BF16))
        us_sb = _e(nc.sbuf_tensor("us_sb", [8, B * 1024], BF16))
        xres = _e(nc.sbuf_tensor("xres", [128, 2 * 1024], F32))
        sq = _e(nc.sbuf_tensor("sq", [128, 1024], F32))
        mt = _e(nc.sbuf_tensor("mt", [128, 16], F32))
        v2t = _e(nc.sbuf_tensor("v2t", [128, 16], F32))
        vt = _e(nc.sbuf_tensor("vt", [128, 16], F32))
        lt = _e(nc.sbuf_tensor("lt", [128, 16], F32))
        rt = _e(nc.sbuf_tensor("rt", [128, 16], F32))
        nmr = _e(nc.sbuf_tensor("nmr", [128, 16], F32))
        tmp1 = _e(nc.sbuf_tensor("tmp1", [128, 16], F32))
        ones = _e(nc.sbuf_tensor("ones", [128, 1], F32))
        psS0 = _e(nc.psum_tensor("psS0", [128, 1024], F32))
        psS1 = _e(nc.psum_tensor("psS1", [128, 1024], F32))
        psO0 = _e(nc.psum_tensor("psO0", [128, 1024], F32))
        psO1 = _e(nc.psum_tensor("psO1", [128, 1024], F32))
        s_Krc = [_e(nc.semaphore(f"s_K{i}")) for i in range(8)]
        s_V0 = _e(nc.semaphore("s_V0"))
        s_V1 = _e(nc.semaphore("s_V1"))
        g_V00 = _e(nc.semaphore("g_V00"))
        g_V01 = _e(nc.semaphore("g_V01"))
        g_qT0 = _e(nc.semaphore("g_qT0"))
        g_qT1 = _e(nc.semaphore("g_qT1"))
        g_cst = _e(nc.semaphore("g_cst"))
        g_x0 = _e(nc.semaphore("g_x0"))
        g_x1 = _e(nc.semaphore("g_x1"))
        pe = _e(nc.semaphore("pe"))
        acts = _e(nc.semaphore("acts"))
        ve = _e(nc.semaphore("ve"))
        st = _e(nc.semaphore("st"))
        block = _e(nc.Block())
        psS = [psS0, psS1]
        psO = [psO0, psO1]

        @block.gpsimd
        def _(g):
            g_V0p = [g_V00, g_V01]
            g_xp = [g_x0, g_x1]

            def kload(a, rc):
                grp = a * 8 + rc
                if grp >= 5:
                    pa, prc = divmod(grp - 5, 8)
                    g.wait_ge(pe, P["pe_S"][(pa, prc)])
                g.dma_start(
                    out=K_rg[:, (grp % 5) * 2048:(grp % 5 + 1) * 2048].rearrange(
                        "p (b ic rr) -> p b ic rr", b=2, ic=8),
                    in_=K_in[a, rc].rearrange("b p ic rr -> p b ic rr"),
                ).then_inc(s_Krc[rc], 16)

            def xload(p, b):
                idx = p * 2 + b
                if idx >= 2:
                    pb, bb = divmod(idx - 2, 2)
                    g.wait_ge(ve, P["ve_xadd"][(bb, pb)])
                g.dma_start(out=xres[:, (idx % 2) * 1024:(idx % 2 + 1) * 1024],
                            in_=xq_in[b, p]).then_inc(g_xp[idx % 2], 16)

            g.dma_start(
                out=V_sb[:, 0:8192].rearrange("p (jc f) -> p jc f", jc=8),
                in_=V_in[0, 0].rearrange("jc p f -> p jc f")).then_inc(g_V0p[0], 16)
            for rc in (1, 3, 5, 7):
                kload(0, rc)
            xload(0, 0)
            xload(0, 1)
            g.dma_start(
                out=V_sb[:, 8192:16384].rearrange("p (jc f) -> p jc f", jc=8),
                in_=V_in[1, 0].rearrange("jc p f -> p jc f")).then_inc(g_V0p[1], 16)
            for a in range(1, 8):
                for rc in (1, 3, 5, 7):
                    kload(a, rc)
                xload(a, 0)
                xload(a, 1)

        @block.sync
        def _(sy):
            s_Vp = [s_V0, s_V1]
            sy.dma_start(out=m1m[:, :], in_=m1_in).then_inc(g_cst, 16)
            sy.dma_start(out=m2m[:, :], in_=m2_in).then_inc(g_cst, 16)
            sy.dma_start(out=us_sb[:, :], in_=us_in).then_inc(g_cst, 16)
            sy.dma_start(out=c19_sb[:, :], in_=c19_in).then_inc(g_cst, 16)
            for a in range(8):
                for rc in (0, 2, 4, 6):
                    grp = a * 8 + rc
                    if grp >= 5:
                        pa, prc = divmod(grp - 5, 8)
                        sy.wait_ge(pe, P["pe_S"][(pa, prc)])
                    sy.dma_start(
                        out=K_rg[:, (grp % 5) * 2048:(grp % 5 + 1) * 2048].rearrange(
                            "p (b ic rr) -> p b ic rr", b=2, ic=8),
                        in_=K_in[a, rc].rearrange("b p ic rr -> p b ic rr"),
                    ).then_inc(s_Krc[rc], 16)
                if a >= 1:
                    for b in range(B):
                        sy.wait_ge(pe, P["pe_V"][(a - 1, b, 7)])
                        sy.dma_start(
                            out=V_sb[:, b * 8192:(b + 1) * 8192].rearrange(
                                "p (jc f) -> p jc f", jc=8),
                            in_=V_in[b, a].rearrange("jc p f -> p jc f")
                        ).then_inc(s_Vp[b], 16)
            sy.wait_ge(st, P["st_total"])

        @block.tensor
        def _(t):
            t.wait_ge(g_qT0, 16)
            t.wait_ge(g_cst, 16 * 4)
            s_Vp = [s_V0, s_V1]
            vgrp = 0
            for a in range(8):
                w = (8 - a) * 128
                for rc in range(8):
                    gidx = a * 8 + rc
                    if gidx >= 2:
                        t.wait_ge(acts, P["sig_seq"][gidx - 2])
                    t.wait_ge(s_Krc[rc], P["sld_K"][(a, rc)])
                    grp = a * 8 + rc
                    p2 = psS[rc % 2]
                    last = None
                    for b in range(B):
                        if gidx == 0 and b == 1:
                            t.wait_ge(g_qT1, 16)
                        for (off, pw) in _pieces(w):
                            for ic in range(8):
                                last = t.matmul(
                                    p2[:, off:off + pw],
                                    K_rg[:, (grp % 5) * 2048 + b * 1024 + ic * 128:
                                         (grp % 5) * 2048 + b * 1024 + ic * 128 + 128],
                                    qT_sb[:, (b * 8 + ic) * 1024 + a * 128 + off:
                                          (b * 8 + ic) * 1024 + a * 128 + off + pw],
                                    start=(b == 0 and ic == 0),
                                    stop=(b == 1 and ic == 7),
                                )
                    last.then_inc(pe, 1)
                for b in range(B):
                    if a == 0:
                        t.wait_ge([g_V00, g_V01][b], 16)
                    else:
                        t.wait_ge(s_Vp[b], P["sld_V"][(a, b)])
                    t.wait_ge(ve, P["ve_ready0"][a] if b == 0 else P["ve_ready"][a])
                    for p in range(a, 8):
                        if vgrp >= 2:
                            t.wait_ge(ve, _ve_evac_by_idx(P, vgrp - 2))
                        po = psO[vgrp % 2]
                        att = at0 if b == 0 else at1
                        last = None
                        for foff in (0, 512):
                            for rc in range(8):
                                last = t.matmul(
                                    po[:, foff:foff + 512],
                                    att[:, rc * 1024 + (p - a) * 128:
                                        rc * 1024 + (p - a) * 128 + 128],
                                    V_sb[:, (b * 8 + rc) * 1024 + foff:
                                         (b * 8 + rc) * 1024 + foff + 512],
                                    start=(rc == 0), stop=(rc == 7 and a != 0),
                                )
                            if a == 0:
                                last = t.matmul(
                                    po[:, foff:foff + 512],
                                    c19_sb[:, p * 128:(p + 1) * 128],
                                    us_sb[:, b * 1024 + foff:b * 1024 + foff + 512],
                                    start=False, stop=True,
                                )
                        last.then_inc(pe, 1)
                        vgrp += 1
            t.wait_ge(ve, P["ve_evac"][(7, 0, 7)])
            for b in range(B):
                if b >= 1:
                    t.wait_ge(ve, P["ve_gcev"][0])
                last = None
                for p in range(8):
                    t.wait_ge(ve, P["ve_apply"][(b, p)])
                    for foff in (0, 512):
                        last = t.matmul(
                            psO0[0:1, foff:foff + 512],
                            ones[:, 0:1],
                            acc[:, (b * 8 + p) * 1024 + foff:(b * 8 + p) * 1024 + foff + 512],
                            start=(p == 0), stop=(p == 7),
                        )
                last.then_inc(pe, 1)

        @block.scalar
        def _(s):
            s.dma_start(out=qT_sb[:, 0:8192].rearrange("p (ic m) -> p ic m", ic=8),
                        in_=qT_in[0].rearrange("ic p m -> p ic m")).then_inc(g_qT0, 16)
            s.dma_start(out=qT_sb[:, 8192:16384].rearrange("p (ic m) -> p ic m", ic=8),
                        in_=qT_in[1].rearrange("ic p m -> p ic m")).then_inc(g_qT1, 16)
            sc = Ctr()
            for a in range(8):
                w = (8 - a) * 128
                for rc in range(8):
                    s.wait_ge(acts, sc.v)
                    s.wait_ge(pe, P["pe_S"][(a, rc)])
                    s.activation(at0[:, rc * 1024:rc * 1024 + w],
                                 psS[rc % 2][:, 0:w], AF.Sigmoid,
                                 scale=1.0 / 32.0).then_inc(acts, 1)
                    sc.inc()
                for b in range(B):
                    s.wait_ge(acts, sc.v)
                    s.wait_ge(ve, P["ve_xadd"][(b, a)])
                    s.activation(sq[:, :],
                                 acc[:, (b * 8 + a) * 1024:(b * 8 + a + 1) * 1024],
                                 AF.Square).then_inc(acts, 1)
                    sc.inc()
                s.wait_ge(acts, sc.v)
                s.wait_ge(ve, P["ve_vteps"][a])
                s.activation(lt[:, 2 * a:2 * a + 2], vt[:, 2 * a:2 * a + 2],
                             AF.Ln, scale=2.0 ** -20).then_inc(acts, 1)
                sc.inc()
                s.wait_ge(acts, sc.v)
                s.activation(rt[:, 2 * a:2 * a + 2], lt[:, 2 * a:2 * a + 2],
                             AF.Exp, scale=-0.5).then_inc(acts, 1)
                sc.inc()
                for b in range(B):
                    s.wait_ge(ve, P["ve_apply"][(b, a)])
                    s.dma_start(out=x1[b, a],
                                in_=acc[:, (b * 8 + a) * 1024:(b * 8 + a + 1) * 1024]
                                ).then_inc(st, 16)
            s.wait_ge(pe, P["pe_V"][(7, 0, 7)])
            for p in range(8):
                s.wait_ge(acts, sc.v)
                s.wait_ge(ve, P["ve_apply"][(0, p)])
                s.copy(at0[:, p * 1024:(p + 1) * 1024],
                       acc[:, p * 1024:(p + 1) * 1024]).then_inc(acts, 1)
                sc.inc()
            s.wait_ge(acts, sc.v)
            s.dma_start(out=x1b[0].rearrange("p ii f -> ii p f"),
                        in_=at0[:, :].rearrange("ii (p f) -> ii p f", p=8)
                        ).then_inc(st, 16)
            s.wait_ge(ve, P["ve_b16"][(1, 7)])
            s.dma_start(out=x1b[1].rearrange("p ii f -> ii p f"),
                        in_=at1[:, :].rearrange("ii (p f) -> ii p f", p=8)
                        ).then_inc(st, 16)
            for b in range(B):
                s.wait_ge(ve, P["ve_gcev"][b])
                s.dma_start(out=gcp[b:b + 1, :],
                            in_=sq[0:1, 0:1024]).then_inc(st, 16)

        @block.vector
        def _(v):
            vc = Ctr()

            def op(fn, *args, **kw):
                v.wait_ge(ve, vc.v)
                fn(*args, **kw).then_inc(ve, 1)
                vc.inc()

            op(v.memset, ones[:, :], 1.0)
            v.wait_ge(g_cst, 16 * 4)
            g_xp = [g_x0, g_x1]
            for a in range(8):
                w = (8 - a) * 128
                par = a % 2
                for rc in range(8):
                    v.wait_ge(acts, P["acts_sig"][(a, rc)])
                    sl0 = at0[:, rc * 1024:rc * 1024 + 128]
                    msl = slice(par * 1024 + rc * 128, par * 1024 + rc * 128 + 128)
                    op(v.tensor_tensor, sl0, sl0, m1m[:, msl], ALU.mult)
                    op(v.tensor_tensor, sl0, sl0, m2m[:, msl], ALU.add)
                    if rc == 7:
                        assert vc.v == P["ve_ready0"][a]
                    op(v.tensor_tensor, at1[:, rc * 1024:rc * 1024 + 128],
                       m1m[:, msl], sl0, ALU.subtract)
                    op(v.scalar_tensor_tensor, at1[:, rc * 1024:rc * 1024 + 128],
                       m2m[:, msl], 2.0, at1[:, rc * 1024:rc * 1024 + 128],
                       ALU.mult, ALU.add)
                    if w > 128:
                        op(v.tensor_scalar, at1[:, rc * 1024 + 128:rc * 1024 + w],
                       at0[:, rc * 1024 + 128:rc * 1024 + w], -1.0, 1.0,
                       ALU.mult, ALU.add)
                assert vc.v == P["ve_ready"][a]
                for b in range(B):
                    for p in range(a, 8):
                        v.wait_ge(pe, P["pe_V"][(a, b, p)])
                        asl = acc[:, (b * 8 + p) * 1024:(b * 8 + p + 1) * 1024]
                        po = psO[_vgrp_idx(a, b, p) % 2]
                        if a == 0:
                            op(v.tensor_copy, asl, po[:, :])
                        else:
                            op(v.tensor_tensor, asl, asl, po[:, :], ALU.add)
                        assert vc.v == P["ve_evac"][(a, b, p)]
                        if p == a:
                            idx = p * 2 + b
                            v.wait_ge(g_xp[idx % 2], P["gld_x"][(b, p)])
                            op(v.tensor_tensor, asl, asl,
                               xres[:, (idx % 2) * 1024:(idx % 2 + 1) * 1024], ALU.add)
                            assert vc.v == P["ve_xadd"][(b, p)]
                            col = slice(2 * a + b, 2 * a + b + 1)
                            op(v.reduce_sum, mt[:, col], asl, AX.X)
                            v.wait_ge(acts, P["acts_sq"][(b, p)])
                            op(v.reduce_sum, v2t[:, col], sq[:, :], AX.X)
                            op(v.tensor_scalar, tmp1[:, col], mt[:, col],
                               mt[:, col], RAWEPS, ALU.mult, ALU.subtract)
                            op(v.scalar_tensor_tensor, vt[:, col], v2t[:, col],
                               1024.0, tmp1[:, col], ALU.mult, ALU.subtract)
                            assert vc.v == P["ve_vt"][(b, p)]
                            if b == 1:
                                assert vc.v == P["ve_vteps"][a]
                v.wait_ge(acts, P["acts_rstd"][a])
                for b in range(B):
                    col = slice(2 * a + b, 2 * a + b + 1)
                    op(v.tensor_scalar, nmr[:, col], mt[:, col], rt[:, col],
                       -1.0 / 1024.0, ALU.mult, ALU.mult)
                    assert vc.v == P["ve_nmr"][(b, a)]
                    asl = acc[:, (b * 8 + a) * 1024:(b * 8 + a + 1) * 1024]
                    op(v.tensor_scalar, asl, asl, rt[:, col], nmr[:, col],
                       ALU.mult, ALU.add)
                    assert vc.v == P["ve_apply"][(b, a)]
            for p in range(8):
                op(v.tensor_copy, at1[:, p * 1024:(p + 1) * 1024],
                   acc[:, (8 + p) * 1024:(9 + p) * 1024])
                assert vc.v == P["ve_b16"][(1, p)]
            for b in range(B):
                v.wait_ge(pe, P["pe_gc"][b])
                if b >= 1:
                    v.wait_ge(st, P["st_gc"][0])
                op(v.tensor_copy, sq[0:1, 0:1024], psO0[0:1, :])
                assert vc.v == P["ve_gcev"][b]
    return nc


def _vgrp_idx(a, b, p):
    i = 0
    for aa in range(a):
        i += 2 * (8 - aa)
    return i + b * (8 - a) + (p - a)


def _ve_evac_by_idx(P, vgrp):
    for (a, b, p), val in P["ve_evac"].items():
        if _vgrp_idx(a, b, p) == vgrp:
            return val
    raise KeyError(vgrp)


def l2_host_inputs(x, qT, kT, vT, usum):
    import ml_dtypes
    bf = ml_dtypes.bfloat16
    k_full = np.ascontiguousarray(qT[0:0]).reshape(0,)
    k_tok = np.ascontiguousarray(kT.transpose(0, 2, 1))
    v_tok = np.ascontiguousarray(vT.transpose(0, 2, 1))
    K_in = np.zeros((8, 8, B, 128, 8, 128), dtype=bf)
    for a in range(8):
        ks = k_tok[:, a::8, :].reshape(B, 8, 128, 8, 128)
        K_in[a] = ks.transpose(3, 0, 2, 1, 4)
    V_in = v_tok.reshape(B, 8, 8, 128, 1024)
    us_in = np.ascontiguousarray(usum.transpose(1, 0, 2)).reshape(8, B * 1024)
    c19 = np.zeros((8, 1024), dtype=bf)
    for A in range(8):
        for p in range(8):
            if A > p:
                c19[A, p * 128:(p + 1) * 128] = 1e-9
    maps = []
    for c in range(NC):
        rows = np.concatenate([np.arange(t * 128, (t + 1) * 128)
                               for t in snake_tiles(c)])
        qTc = np.ascontiguousarray(qT[:, :, rows]).reshape(B, 8, 128, 1024)
        xqc = np.ascontiguousarray(x[:, rows, :]).reshape(B, 8, 128, 1024)
        m1 = np.zeros((128, 2048), dtype=np.float32)
        for par, tm8 in ((0, c), (1, 7 - c)):
            for rc in range(8):
                slc = slice(par * 1024 + rc * 128, par * 1024 + rc * 128 + 128)
                if rc < tm8:
                    m1[:, slc] = 1.0
                elif rc == tm8:
                    jj = np.arange(128)[:, None]
                    mm = np.arange(128)[None, :]
                    m1[:, slc] = (jj <= mm).astype(np.float32)
        m2 = 1e-9 * (1.0 - m1)
        maps.append({
            "qT_in": qTc, "K_in": K_in, "V_in": V_in, "us_in": us_in,
            "c19_in": c19, "m1_in": m1.astype(bf), "m2_in": m2.astype(bf),
            "xq_in": xqc,
        })
    return maps


def l2_host_outputs(res):
    import ml_dtypes
    x1 = np.zeros((B, S, D), dtype=np.float32)
    x1bf = np.zeros((B, S, D), dtype=ml_dtypes.bfloat16)
    gcp = np.zeros((NC, B, D), dtype=np.float32)
    for c in range(NC):
        r = res.results[c]
        for pi, t in enumerate(snake_tiles(c)):
            x1[:, t * 128:(t + 1) * 128, :] = r["x1"][:, pi]
            x1bf[:, t * 128:(t + 1) * 128, :] = r["x1b"][:, pi]
        gcp[c] = r["gcp"].reshape(B, D)
    return x1, x1bf, gcp



def plan_l3():
    pe = Ctr(); acts = Ctr(); ve = Ctr(); gld = Ctr(); sld = Ctr(); st = Ctr()
    P = {}
    xp = [Ctr(), Ctr(), Ctr(), Ctr()]
    for b in range(B):
        for p in range(8):
            idx = b * 8 + p
            xp[idx % 4].inc(16)
            P["sld_x1s", b, p] = xp[idx % 4].v

    ve.inc(1); P["ve_ones"] = ve.v
    ve.inc(9); P["ve_gc16b"] = ve.v

    for oc in range(8):
        pe.inc(); P["pe_h1", 0, 0, oc] = pe.v
    for b in range(B):
        for oc in range(4):
            pe.inc(); P["pe_g1", b, oc] = pe.v
        for oc in range(2):
            pe.inc(); P["pe_g2", b, oc] = pe.v
        pe.inc(); P["pe_gf", b] = pe.v
    pe.inc(); P["pe_bc"] = pe.v
    for l in range(3):
        for b in range(B):
            if l > 0:
                for g in range(16):
                    pe.inc(); P["pe_tp", b, l, g] = pe.v
            for oc in range(8):
                if (b, l) != (0, 0):
                    pe.inc(); P["pe_h1", b, l, oc] = pe.v
            for oc in range(4):
                pe.inc(); P["pe_h2", b, l, oc] = pe.v
            for oc in range(2):
                pe.inc(); P["pe_h3", b, l, oc] = pe.v
            pe.inc(); P["pe_tf", b, l] = pe.v
    for oc in range(8):
        acts.inc(); P["acts_h1", 0, 0, oc] = acts.v
    for b in range(B):
        for oc in range(4):
            acts.inc(); P["acts_g1", b, oc] = acts.v
        for oc in range(2):
            acts.inc(); P["acts_g2", b, oc] = acts.v
    for b in range(B):
        acts.inc(); P["acts_gf", b] = acts.v
    ve.inc(); P["ve_bc"] = ve.v
    ve.inc(7); P["ve_AB"] = ve.v
    def _plan_hacts(b, l):
        for oc in range(8):
            if (b, l) != (0, 0):
                acts.inc(); P["acts_h1", b, l, oc] = acts.v
        for oc in range(4):
            acts.inc(); P["acts_h2", b, l, oc] = acts.v
        for oc in range(2):
            acts.inc(); P["acts_h3", b, l, oc] = acts.v

    def _plan_sig(b, l):
        acts.inc(); P["acts_tf", b, l] = acts.v

    for l in range(2):
        _plan_hacts(0, l); _plan_sig(0, l)
        _plan_hacts(1, l); _plan_sig(1, l)
    _plan_hacts(0, 2); _plan_sig(0, 2)
    for oc in range(8):
        acts.inc(); P["acts_h1", 1, 2, oc] = acts.v
    acts.inc(2); P["acts_rstd2", 0] = acts.v
    for oc in range(4):
        acts.inc(); P["acts_h2", 1, 2, oc] = acts.v
    for oc in range(2):
        acts.inc(); P["acts_h3", 1, 2, oc] = acts.v
    _plan_sig(1, 2)
    for p in range(8):
        acts.inc(); P["acts_sq2", 1, p] = acts.v
        if p >= 2:
            acts.inc(2); P["acts_rstd1", p - 2] = acts.v
    acts.inc(2); P["acts_rstd1", 6] = acts.v
    acts.inc(2); P["acts_rstd1", 7] = acts.v
    def _plan_CE(b, l):
        ve.inc(); P["ve_obf", b, l] = ve.v
        for g in range(16):
            ve.inc(); P["ve_tpev", b, l, g] = ve.v

    def _plan_T(b, l):
        ve.inc(2); P["ve_coup", b, l] = ve.v
        for p in range(8):
            ve.inc(3); P["ve_twc", b, l, p] = ve.v
        P["ve_twist", b, l] = ve.v

    po = Ctr()
    P["po"] = po

    def _plan_LN2(b):
        if b == 0:
            for p in range(8):
                po.inc(); P["po_s2add", 0, p] = po.v
                po.inc(); P["po_sq", 0, p] = po.v
                ve.inc()
                ve.inc(3)
                P["ve_vt", 0, p] = ve.v
            P["ve_vt2eps", 0] = ve.v
            ve.inc(2)
            P["ve_nmrb", 0] = ve.v
            for p in range(8):
                ve.inc(); P["ve_apply", 0, p] = ve.v
        else:
            for p in range(8):
                ve.inc(); P["ve_s2add1", p] = ve.v
                ve.inc(); P["ve_mred1", p] = ve.v
                if p >= 1:
                    ve.inc(3)
                    P["ve_vt", 1, p - 1] = ve.v
                    P["ve_vteps1", p - 1] = ve.v
                if p >= 2:
                    ve.inc(2)
                    P["ve_apply", 1, p - 2] = ve.v
            ve.inc(3); P["ve_vt", 1, 7] = ve.v
            P["ve_vteps1", 7] = ve.v
            ve.inc(2); P["ve_apply", 1, 6] = ve.v
            ve.inc(2); P["ve_apply", 1, 7] = ve.v

    _plan_T(0, 0)
    _plan_CE(0, 1)
    _plan_T(1, 0)
    _plan_CE(1, 1)
    _plan_T(0, 1)
    _plan_CE(0, 2)
    _plan_T(1, 1)
    _plan_CE(1, 2)
    _plan_T(0, 2)
    _plan_LN2(0)
    _plan_T(1, 2)
    _plan_LN2(1)
    st.inc(16 * 9)
    P["st_total"] = st.v
    seq = [("acts_h1", 0, 0, oc) for oc in range(8)]
    for l in range(3):
        for b in range(B):
            if (b, l) != (0, 0):
                seq += [("acts_h1", b, l, oc) for oc in range(8)]
            seq += [("acts_h2", b, l, oc) for oc in range(4)]
            seq += [("acts_h3", b, l, oc) for oc in range(2)]
    P["psh_prev"] = [P[k] for k in seq]
    return P


def build_l3():
    nc = bass.Bass("TRN2", target_bir_lowering=False)
    x1_in = nc.dram_tensor("x1_in", [B, 8, 128, 1024], F32, kind="ExternalInput").ap()
    x1T_in = nc.dram_tensor("x1T_in", [B, 8, 128, 1024], BF16, kind="ExternalInput").ap()
    gcT_in = nc.dram_tensor("gcT_in", [128, 128], F32, kind="ExternalInput").ap()
    w1_in = nc.dram_tensor("w1_in", [8, 128, 1024], F32, kind="ExternalInput").ap()
    w2_in = nc.dram_tensor("w2_in", [8, 128, 512], F32, kind="ExternalInput").ap()
    w3_in = nc.dram_tensor("w3_in", [4, 128, 256], F32, kind="ExternalInput").ap()
    w4_in = nc.dram_tensor("w4_in", [128, 2], F32, kind="ExternalInput").ap()
    gw1_in = nc.dram_tensor("gw1_in", [8, 128, 512], F32, kind="ExternalInput").ap()
    gw2_in = nc.dram_tensor("gw2_in", [4, 128, 256], F32, kind="ExternalInput").ap()
    gw3_in = nc.dram_tensor("gw3_in", [128, 2], F32, kind="ExternalInput").ap()
    sml_in = nc.dram_tensor("sml_in", [128, 20], F32, kind="ExternalInput").ap()
    b4_in = nc.dram_tensor("b4_in", [1, 1], F32, kind="ExternalInput").ap()
    gb3_in = nc.dram_tensor("gb3_in", [1, 1], F32, kind="ExternalInput").ap()
    ar_in = nc.dram_tensor("ar_in", [1, 1], F32, kind="ExternalInput").ap()
    ident_in = nc.dram_tensor("ident_in", [128, 128], BF16, kind="ExternalInput").ap()
    x2 = nc.dram_tensor("x2", [B, 8, 128, 1024], F32, kind="ExternalOutput").ap()

    P = plan_l3()
    EPS = 1e-5
    RAWEPS = EPS * 1024.0 * 1024.0
    from contextlib import ExitStack
    with ExitStack() as _ctx:
        def _e(x):
            return _ctx.enter_context(x)
        w1s = _e(nc.sbuf_tensor("w1s", [128, 8 * 1024], BF16))
        w2s = _e(nc.sbuf_tensor("w2s", [128, 8 * 512], BF16))
        w3s = _e(nc.sbuf_tensor("w3s", [128, 4 * 256], BF16))
        w4s = _e(nc.sbuf_tensor("w4s", [128, 2], BF16))
        gw1s = _e(nc.sbuf_tensor("gw1s", [128, 8 * 512], BF16))
        gw2s = _e(nc.sbuf_tensor("gw2s", [128, 4 * 256], BF16))
        gw3s = _e(nc.sbuf_tensor("gw3s", [128, 2], BF16))
        sml_sb = _e(nc.sbuf_tensor("sml_sb", [128, 20], F32))
        gb3s = _e(nc.sbuf_tensor("gb3s", [1, 1], F32))
        gcTs = _e(nc.sbuf_tensor("gcTs", [128, 128], F32))
        gc16 = _e(nc.sbuf_tensor("gc16", [128, 16], F32))
        gc16b = _e(nc.sbuf_tensor("gc16b", [128, 16], BF16))
        idents = _e(nc.sbuf_tensor("idents", [128, 128], BF16))
        sc4 = _e(nc.sbuf_tensor("sc4", [1, 4], F32))
        ones1 = _e(nc.sbuf_tensor("ones1", [1, 128], F32))
        bc = _e(nc.sbuf_tensor("bc", [128, 4], F32))
        Ab = _e(nc.sbuf_tensor("Ab", [128, 2], F32))
        Bc = _e(nc.sbuf_tensor("Bc", [128, 1], F32))
        tAB = _e(nc.sbuf_tensor("tAB", [128, 2], F32))
        g1 = _e(nc.sbuf_tensor("g1", [128, 4], BF16))
        g2 = _e(nc.sbuf_tensor("g2", [128, 2], BF16))
        out0 = _e(nc.sbuf_tensor("out0", [128, 8 * 1024], F32))
        out1 = _e(nc.sbuf_tensor("out1", [128, 8 * 1024], F32))
        outT0 = _e(nc.sbuf_tensor("outT0", [128, 8 * 1024], BF16))
        outT1 = _e(nc.sbuf_tensor("outT1", [128, 8 * 1024], BF16))
        obf = _e(nc.sbuf_tensor("obf", [128, 8 * 1024], BF16))
        h1T = _e(nc.sbuf_tensor("h1T", [128, 8 * 1024], BF16))
        h2T = _e(nc.sbuf_tensor("h2T", [128, 4 * 1024], BF16))
        h3T = _e(nc.sbuf_tensor("h3T", [128, 2 * 1024], BF16))
        x1s = _e(nc.sbuf_tensor("x1s", [128, 4 * 1024], F32))
        tfs = _e(nc.sbuf_tensor("tfs", [128, 2 * 8], F32))
        coup = _e(nc.sbuf_tensor("coup", [128, 2 * 8], F32))
        coupn = _e(nc.sbuf_tensor("coupn", [128, 2 * 8], F32))
        wtmp = _e(nc.sbuf_tensor("wtmp", [128, 1024], F32))
        sq2 = _e(nc.sbuf_tensor("sq2", [128, 2 * 1024], F32))
        mt2 = _e(nc.sbuf_tensor("mt2", [128, 16], F32))
        v2t2 = _e(nc.sbuf_tensor("v2t2", [128, 16], F32))
        vt2 = _e(nc.sbuf_tensor("vt2", [128, 16], F32))
        lt2 = _e(nc.sbuf_tensor("lt2", [128, 16], F32))
        rt2 = _e(nc.sbuf_tensor("rt2", [128, 16], F32))
        nmr2 = _e(nc.sbuf_tensor("nmr2", [128, 16], F32))
        tsm = _e(nc.sbuf_tensor("tsm", [128, 16], F32))
        psH0 = _e(nc.psum_tensor("psH0", [128, 1024], F32))
        psH1 = _e(nc.psum_tensor("psH1", [128, 1024], F32))
        psE0 = _e(nc.psum_tensor("psE0", [128, 512], BF16))
        psE1 = _e(nc.psum_tensor("psE1", [128, 512], BF16))
        psG = _e(nc.psum_tensor("psG", [128, 32], F32))
        g_w1a = _e(nc.semaphore("g_w1a"))
        g_w1b = _e(nc.semaphore("g_w1b"))
        g_sm = _e(nc.semaphore("g_sm"))
        g_gc = _e(nc.semaphore("g_gc"))
        g_w2 = _e(nc.semaphore("g_w2"))
        s_oT0 = _e(nc.semaphore("s_oT0"))
        s_oT1 = _e(nc.semaphore("s_oT1"))
        g_sm2 = _e(nc.semaphore("g_sm2"))
        st_g = _e(nc.semaphore("st_g"))
        s_ob0 = _e(nc.semaphore("s_ob0"))
        s_ob1 = _e(nc.semaphore("s_ob1"))
        s_x10 = _e(nc.semaphore("s_x10"))
        s_x11 = _e(nc.semaphore("s_x11"))
        s_x12 = _e(nc.semaphore("s_x12"))
        s_x13 = _e(nc.semaphore("s_x13"))
        po = _e(nc.semaphore("po"))
        pe = _e(nc.semaphore("pe"))
        acts = _e(nc.semaphore("acts"))
        ve = _e(nc.semaphore("ve"))
        st = _e(nc.semaphore("st"))
        block = _e(nc.Block())
        psH = [psH0, psH1]
        psE = [psE0, psE1]
        outb = [out0, out1]
        outT = [outT0, outT1]

        @block.gpsimd
        def _(g):
            g.dma_start(out=w1s[:, 0:4096].rearrange("p (o c) -> p o c", o=4),
                        in_=w1_in[0:4].rearrange("o p c -> p o c")).then_inc(g_w1a, 16)
            g.dma_start(out=w1s[:, 4096:8192].rearrange("p (o c) -> p o c", o=4),
                        in_=w1_in[4:8].rearrange("o p c -> p o c")).then_inc(g_w1b, 16)
            g.dma_start(out=gw1s[:, :].rearrange("p (hc d) -> p hc d", hc=8),
                        in_=gw1_in.rearrange("hc p d -> p hc d")).then_inc(g_gc, 16)
            g.dma_start(out=gw2s[:, :].rearrange("p (hc d) -> p hc d", hc=4),
                        in_=gw2_in.rearrange("hc p d -> p hc d")).then_inc(g_gc, 16)
            g.dma_start(out=gw3s[:, :], in_=gw3_in).then_inc(g_gc, 16)
            g.dma_start(out=idents[:, :], in_=ident_in).then_inc(g_gc, 16)
            g.dma_start(out=gcTs[:, :], in_=gcT_in).then_inc(g_gc, 16)
            g.dma_start(out=w2s[:, :].rearrange("p (hc d) -> p hc d", hc=8),
                        in_=w2_in.rearrange("hc p d -> p hc d")).then_inc(g_w2, 16)
            g.dma_start(out=w3s[:, :].rearrange("p (hc d) -> p hc d", hc=4),
                        in_=w3_in.rearrange("hc p d -> p hc d")).then_inc(g_w2, 16)
            g.dma_start(out=w4s[:, :], in_=w4_in).then_inc(g_w2, 16)
            for p in range(8):
                osl = outb[0][:, p * 1024:(p + 1) * 1024]
                g.wait_ge(ve, P["ve_twc", 0, 2, p])
                g.wait_ge([s_x10, s_x11, s_x12, s_x13][p % 4], P["sld_x1s", 0, p])
                if p > 0:
                    g.wait_ge(po, P["po_sq", 0, p - 1])
                g.tensor_tensor(osl, osl,
                                x1s[:, (p % 4) * 1024:(p % 4 + 1) * 1024],
                                ALU.add).then_inc(po, 1)
                if p >= 2:
                    g.wait_ge(ve, P["ve_vt", 0, p - 2])
                g.wait_ge(po, P["po_s2add", 0, p])
                g.tensor_tensor(sq2[:, (p % 2) * 1024:(p % 2 + 1) * 1024],
                                osl, osl, ALU.mult).then_inc(po, 1)

        @block.sync
        def _(sy):
            s_x1p = [s_x10, s_x11, s_x12, s_x13]
            sy.dma_start(out=sml_sb[:, :], in_=sml_in).then_inc(g_sm, 16)
            sy.dma_start(out=outT[0][:, :].rearrange("p (fc t) -> p fc t", fc=8),
                         in_=x1T_in[0].rearrange("fc p t -> p fc t")).then_inc(s_oT0, 16)
            sy.dma_start(out=outb[0][:, :].rearrange("ii (p f) -> ii p f", p=8),
                         in_=x1_in[0].rearrange("p ii f -> ii p f")).then_inc(s_ob0, 16)
            sy.dma_start(out=outT[1][:, :].rearrange("p (fc t) -> p fc t", fc=8),
                         in_=x1T_in[1].rearrange("fc p t -> p fc t")).then_inc(s_oT1, 16)
            sy.dma_start(out=outb[1][:, :].rearrange("ii (p f) -> ii p f", p=8),
                         in_=x1_in[1].rearrange("p ii f -> ii p f")).then_inc(s_ob1, 16)
            sy.dma_start(out=sc4[0:1, 0:1], in_=b4_in).then_inc(g_sm2, 16)
            sy.dma_start(out=sc4[0:1, 1:2], in_=ar_in).then_inc(g_sm2, 16)
            sy.dma_start(out=gb3s[0:1, 0:1], in_=gb3_in).then_inc(g_sm2, 16)

            def x1load(idx):
                b, p = divmod(idx, 8)
                if idx >= 4:
                    pb, pp = divmod(idx - 4, 8)
                    if pb == 0:
                        sy.wait_ge(po, P["po_s2add", 0, pp])
                    else:
                        sy.wait_ge(ve, P["ve_s2add1", pp])
                sy.dma_start(out=x1s[:, (idx % 4) * 1024:(idx % 4 + 1) * 1024],
                             in_=x1_in[b, p]).then_inc(s_x1p[idx % 4], 16)

            for idx in range(12):
                x1load(idx)
            sy.wait_ge(ve, P["ve_apply", 0, 7])
            sy.dma_start(out=x2[0].rearrange("p ii f -> ii p f"),
                         in_=outb[0][:, :].rearrange("ii (p f) -> ii p f", p=8)
                         ).then_inc(st, 16)
            for idx in range(12, 16):
                x1load(idx)
            for p in (0, 2, 4, 6):
                sy.wait_ge(ve, P["ve_apply", 1, p])
                sy.dma_start(out=x2[1, p],
                             in_=out1[:, p * 1024:(p + 1) * 1024]).then_inc(st, 16)
            sy.wait_ge(st, 16 * 5)
            sy.wait_ge(st_g, 16 * 4)

        @block.tensor
        def _(t):
            pshi = [0]

            def psh_wait():
                i = pshi[0]
                if i >= 2:
                    t.wait_ge(acts, P["psh_prev"][i - 2])
                pshi[0] = i + 1

            t.wait_ge(g_w1a, 16)
            t.wait_ge(s_oT0, 16)
            for oc in range(8):
                if oc == 4:
                    t.wait_ge(g_w1b, 16)
                psh_wait()
                last = None
                for ic in range(8):
                    for pc in (0, 512):
                        last = t.matmul(
                            psH[oc % 2][:, pc:pc + 512],
                            w1s[:, oc * 1024 + ic * 128:oc * 1024 + ic * 128 + 128],
                            outT[0][:, ic * 1024 + pc:ic * 1024 + pc + 512],
                            start=(ic == 0), stop=(ic == 7))
                last.then_inc(pe, 1)
            t.wait_ge(g_gc, 16 * 5)
            t.wait_ge(ve, P["ve_gc16b"])
            for b in range(B):
                for oc in range(4):
                    last = None
                    for ic in range(8):
                        last = t.matmul(psG[:, oc:oc + 1],
                                        gw1s[:, ic * 512 + oc * 128:ic * 512 + oc * 128 + 128],
                                        gc16b[:, b * 8 + ic:b * 8 + ic + 1],
                                        start=(ic == 0), stop=(ic == 7))
                    last.then_inc(pe, 1)
                for oc in range(2):
                    t.wait_ge(acts, P["acts_g1", b, 3])
                    last = None
                    for ic in range(4):
                        last = t.matmul(psG[:, 4 + oc:5 + oc],
                                        gw2s[:, ic * 256 + oc * 128:ic * 256 + oc * 128 + 128],
                                        g1[:, ic:ic + 1],
                                        start=(ic == 0), stop=(ic == 3))
                    last.then_inc(pe, 1)
                t.wait_ge(acts, P["acts_g2", b, 1])
                last = None
                for icc in range(2):
                    last = t.matmul(psG[0:1, 6 + b:7 + b],
                                    gw3s[:, icc:icc + 1], g2[:, icc:icc + 1],
                                    start=(icc == 0), stop=(icc == 1))
                last.then_inc(pe, 1)
            t.wait_ge(acts, P["acts_gf", 1])
            t.wait_ge(g_sm2, 16 * 3)
            t.wait_ge(ve, P["ve_ones"])
            t.matmul(psG[:, 8:12], ones1[0:1, :], sc4[0:1, :],
                     start=True, stop=True).then_inc(pe, 1)
            t.wait_ge(g_w2, 16 * 3)
            for l in range(3):
                for b in range(B):
                    if l > 0:
                        t.wait_ge(ve, P["ve_obf", b, l])
                        gi = 0
                        for fc in range(8):
                            for pg in range(2):
                                if gi >= 2:
                                    t.wait_ge(ve, P["ve_tpev", b, l, gi - 2])
                                last = None
                                for j in range(4):
                                    p = pg * 4 + j
                                    last = t.transpose(
                                        psE[gi % 2][:, j * 128:(j + 1) * 128],
                                        obf[:, p * 1024 + fc * 128:p * 1024 + fc * 128 + 128],
                                        idents[:, :])
                                last.then_inc(pe, 1)
                                gi += 1
                        t.wait_ge(ve, P["ve_tpev", b, l, 15])
                    else:
                        if b == 1:
                            t.wait_ge(s_oT1, 16)
                    if (b, l) != (0, 0):
                        for oc in range(8):
                            psh_wait()
                            last = None
                            for ic in range(8):
                                for pc in (0, 512):
                                    last = t.matmul(
                                        psH[oc % 2][:, pc:pc + 512],
                                        w1s[:, oc * 1024 + ic * 128:oc * 1024 + ic * 128 + 128],
                                        outT[b][:, ic * 1024 + pc:ic * 1024 + pc + 512],
                                        start=(ic == 0), stop=(ic == 7))
                            last.then_inc(pe, 1)
                    t.wait_ge(acts, P["acts_h1", b, l, 7])
                    for oc in range(4):
                        psh_wait()
                        last = None
                        for ic in range(8):
                            for pc in (0, 512):
                                last = t.matmul(
                                    psH[oc % 2][:, pc:pc + 512],
                                    w2s[:, ic * 512 + oc * 128:ic * 512 + oc * 128 + 128],
                                    h1T[:, ic * 1024 + pc:ic * 1024 + pc + 512],
                                    start=(ic == 0), stop=(ic == 7))
                        last.then_inc(pe, 1)
                    t.wait_ge(acts, P["acts_h2", b, l, 3])
                    for oc in range(2):
                        psh_wait()
                        last = None
                        for ic in range(4):
                            for pc in (0, 512):
                                last = t.matmul(
                                    psH[oc % 2][:, pc:pc + 512],
                                    w3s[:, ic * 256 + oc * 128:ic * 256 + oc * 128 + 128],
                                    h2T[:, ic * 1024 + pc:ic * 1024 + pc + 512],
                                    start=(ic == 0), stop=(ic == 3))
                        last.then_inc(pe, 1)
                    t.wait_ge(acts, P["acts_h3", b, l, 1])
                    if not (b == 0 and l == 0):
                        lb, ll = (0, l) if b == 1 else (1, l - 1)
                        t.wait_ge(acts, P["acts_tf", lb, ll])
                    last = None
                    for p in range(8):
                        for icc in range(2):
                            last = t.matmul(
                                psG[:, 12 + p:13 + p],
                                h3T[:, icc * 1024 + p * 128:icc * 1024 + p * 128 + 128],
                                w4s[:, icc:icc + 1],
                                start=(icc == 0), stop=(icc == 1))
                    last.then_inc(pe, 1)

        @block.scalar
        def _(s):
            sc = Ctr()

            def act(fn_out, fn_in, func, **kw):
                s.wait_ge(acts, sc.v)
                s.activation(fn_out, fn_in, func, **kw).then_inc(acts, 1)
                sc.inc()

            s.wait_ge(g_sm, 16)
            for oc in range(8):
                s.wait_ge(pe, P["pe_h1", 0, 0, oc])
                act(h1T[:, oc * 1024:(oc + 1) * 1024], psH[oc % 2][:, :],
                    AF.Gelu, bias=sml_sb[:, oc:oc + 1])
            for b in range(B):
                for oc in range(4):
                    s.wait_ge(pe, P["pe_g1", b, oc])
                    act(g1[:, oc:oc + 1], psG[:, oc:oc + 1], AF.Gelu,
                        bias=sml_sb[:, 14 + oc:15 + oc])
                for oc in range(2):
                    s.wait_ge(pe, P["pe_g2", b, oc])
                    act(g2[:, oc:oc + 1], psG[:, 4 + oc:5 + oc], AF.Gelu,
                        bias=sml_sb[:, 18 + oc:19 + oc])
            s.wait_ge(g_sm2, 16 * 3)
            for b in range(B):
                s.wait_ge(pe, P["pe_gf", b])
                act(sc4[0:1, 2 + b:3 + b], psG[0:1, 6 + b:7 + b], AF.Sigmoid,
                    bias=gb3s[0:1, 0:1])
            s.wait_ge(ve, P["ve_bc"])
            def hacts(b, l):
                if (b, l) != (0, 0):
                    for oc in range(8):
                        s.wait_ge(pe, P["pe_h1", b, l, oc])
                        act(h1T[:, oc * 1024:(oc + 1) * 1024], psH[oc % 2][:, :],
                            AF.Gelu, bias=sml_sb[:, oc:oc + 1])
                for oc in range(4):
                    s.wait_ge(pe, P["pe_h2", b, l, oc])
                    act(h2T[:, oc * 1024:(oc + 1) * 1024], psH[oc % 2][:, :],
                        AF.Gelu, bias=sml_sb[:, 8 + oc:9 + oc])
                for oc in range(2):
                    s.wait_ge(pe, P["pe_h3", b, l, oc])
                    act(h3T[:, oc * 1024:(oc + 1) * 1024], psH[oc % 2][:, :],
                        AF.Gelu, bias=sml_sb[:, 12 + oc:13 + oc])

            def sigact(b, l):
                s.wait_ge(pe, P["pe_tf", b, l])
                act(tfs[:, b * 8:(b + 1) * 8], psG[:, 12:20], AF.Sigmoid,
                    bias=bc[:, 0:1])

            def rstd2(b):
                s.wait_ge(ve, P["ve_vt2eps", b])
                act(lt2[:, b * 8:(b + 1) * 8], vt2[:, b * 8:(b + 1) * 8], AF.Ln,
                    scale=2.0 ** -20)
                act(rt2[:, b * 8:(b + 1) * 8], lt2[:, b * 8:(b + 1) * 8], AF.Exp,
                    scale=-0.5)

            for l in range(2):
                hacts(0, l); sigact(0, l)
                hacts(1, l); sigact(1, l)
            hacts(0, 2); sigact(0, 2)
            for oc in range(8):
                s.wait_ge(pe, P["pe_h1", 1, 2, oc])
                act(h1T[:, oc * 1024:(oc + 1) * 1024], psH[oc % 2][:, :],
                    AF.Gelu, bias=sml_sb[:, oc:oc + 1])
            rstd2(0)
            for oc in range(4):
                s.wait_ge(pe, P["pe_h2", 1, 2, oc])
                act(h2T[:, oc * 1024:(oc + 1) * 1024], psH[oc % 2][:, :],
                    AF.Gelu, bias=sml_sb[:, 8 + oc:9 + oc])
            for oc in range(2):
                s.wait_ge(pe, P["pe_h3", 1, 2, oc])
                act(h3T[:, oc * 1024:(oc + 1) * 1024], psH[oc % 2][:, :],
                    AF.Gelu, bias=sml_sb[:, 12 + oc:13 + oc])
            sigact(1, 2)
            def rstd1(p):
                s.wait_ge(ve, P["ve_vteps1", p])
                act(lt2[:, 8 + p:9 + p], vt2[:, 8 + p:9 + p], AF.Ln,
                    scale=2.0 ** -20)
                act(rt2[:, 8 + p:9 + p], lt2[:, 8 + p:9 + p], AF.Exp, scale=-0.5)

            for p in range(8):
                idx = 8 + p
                s.wait_ge(ve, P["ve_s2add1", p])
                pb2, pp2 = divmod(idx - 2, 8)
                s.wait_ge(ve, P["ve_vt", pb2, pp2])
                act(sq2[:, (idx % 2) * 1024:(idx % 2 + 1) * 1024],
                    outb[1][:, p * 1024:(p + 1) * 1024], AF.Square)
                if p >= 2:
                    rstd1(p - 2)
                if p >= 3 and p % 2 == 1:
                    pw = p - 2
                    s.wait_ge(ve, P["ve_apply", 1, pw])
                    s.dma_start(out=x2[1, pw],
                                in_=out1[:, pw * 1024:(pw + 1) * 1024]).then_inc(st_g, 16)
            rstd1(6)
            rstd1(7)
            s.wait_ge(ve, P["ve_apply", 1, 7])
            s.dma_start(out=x2[1, 7],
                        in_=out1[:, 7 * 1024:8 * 1024]).then_inc(st_g, 16)

        @block.vector
        def _(v):
            vc = Ctr()

            def op(fn, *args, **kw):
                v.wait_ge(ve, vc.v)
                fn(*args, **kw).then_inc(ve, 1)
                vc.inc()

            op(v.memset, ones1[0:1, :], 1.0)
            v.wait_ge(g_gc, 16 * 5)
            op(v.tensor_tensor, gc16[:, :], gcTs[:, 0:16], gcTs[:, 16:32], ALU.add)
            for cr in range(2, 8):
                op(v.tensor_tensor, gc16[:, :], gc16[:, :],
                   gcTs[:, cr * 16:(cr + 1) * 16], ALU.add)
            op(v.tensor_scalar, gc16[:, :], gc16[:, :], 1.0 / 8192.0, None, ALU.mult)
            op(v.tensor_copy, gc16b[:, :], gc16[:, :])
            assert vc.v == P["ve_gc16b"]
            v.wait_ge(pe, P["pe_bc"])
            op(v.tensor_copy, bc[:, :], psG[:, 8:12])
            assert vc.v == P["ve_bc"]
            for b in range(B):
                op(v.tensor_tensor, tAB[:, b:b + 1], bc[:, 1:2], bc[:, 2 + b:3 + b], ALU.mult)
            op(v.tensor_scalar, tAB[:, :], tAB[:, :], 1.4, None, ALU.mult)
            op(v.tensor_scalar, tsm[:, 0:1], bc[:, 1:2], -1.0, 0.1, ALU.mult, ALU.add)
            op(v.tensor_tensor, Ab[:, 0:1], tsm[:, 0:1], tAB[:, 0:1], ALU.add)
            op(v.tensor_tensor, Ab[:, 1:2], tsm[:, 0:1], tAB[:, 1:2], ALU.add)
            op(v.tensor_scalar, Bc[:, 0:1], bc[:, 1:2], 0.6, None, ALU.mult)
            assert vc.v == P["ve_AB"]

            def twist(b, l):
                if l == 0:
                    v.wait_ge([s_ob0, s_ob1][b], 16)
                v.wait_ge(acts, P["acts_tf", b, l])
                op(v.tensor_scalar, coup[:, b * 8:(b + 1) * 8],
                   tfs[:, b * 8:(b + 1) * 8], Bc[:, 0:1], Ab[:, b:b + 1],
                   ALU.mult, ALU.add)
                op(v.tensor_scalar, coupn[:, b * 8:(b + 1) * 8],
                   coup[:, b * 8:(b + 1) * 8], -1.0, None, ALU.mult)
                assert vc.v == P["ve_coup", b, l]
                w4 = wtmp[:, :].rearrange("p (a q) -> p a q", a=4)
                for p in range(8):
                    osl = outb[b][:, p * 1024:(p + 1) * 1024]
                    o4 = osl.rearrange("p (a q) -> p a q", a=4)
                    cp = coup[:, b * 8 + p:b * 8 + p + 1]
                    op(v.tensor_scalar, wtmp[:, :], osl, cp, None, ALU.mult)
                    op(v.scalar_tensor_tensor, o4[:, 0:4:3, :], w4[:, 2:0:-1, :],
                       1.0, o4[:, 0:4:3, :], ALU.mult, ALU.add)
                    op(v.scalar_tensor_tensor, o4[:, 1:3, :], w4[:, 3::-3, :],
                       -1.0, o4[:, 1:3, :], ALU.mult, ALU.add)
                assert vc.v == P["ve_twist", b, l]

            def ln2(b):
                assert b == 0
                bsl = slice(0, 8)
                for p in range(8):
                    col = slice(p, p + 1)
                    osl = outb[0][:, p * 1024:(p + 1) * 1024]
                    v.wait_ge(po, P["po_s2add", 0, p])
                    op(v.reduce_sum, mt2[:, col], osl, AX.X)
                    v.wait_ge(po, P["po_sq", 0, p])
                    op(v.reduce_sum, v2t2[:, col],
                       sq2[:, (p % 2) * 1024:(p % 2 + 1) * 1024], AX.X)
                    op(v.tensor_scalar, tsm[:, col], mt2[:, col], mt2[:, col],
                       RAWEPS, ALU.mult, ALU.subtract)
                    op(v.scalar_tensor_tensor, vt2[:, col], v2t2[:, col],
                       1024.0, tsm[:, col], ALU.mult, ALU.subtract)
                    assert vc.v == P["ve_vt", 0, p]
                assert vc.v == P["ve_vt2eps", 0]
                v.wait_ge(acts, P["acts_rstd2", 0])
                op(v.tensor_tensor, nmr2[:, bsl], mt2[:, bsl], rt2[:, bsl], ALU.mult)
                op(v.tensor_scalar, nmr2[:, bsl], nmr2[:, bsl], -1.0 / 1024.0, None, ALU.mult)
                assert vc.v == P["ve_nmrb", 0]
                for p in range(8):
                    col = slice(p, p + 1)
                    osl = outb[0][:, p * 1024:(p + 1) * 1024]
                    op(v.tensor_scalar, osl, osl, rt2[:, col], nmr2[:, col],
                       ALU.mult, ALU.add)
                    assert vc.v == P["ve_apply", 0, p]

            def ln2b1_head(p):
                idx = 8 + p
                col = slice(idx, idx + 1)
                osl = outb[1][:, p * 1024:(p + 1) * 1024]
                idx4 = [s_x10, s_x11, s_x12, s_x13][idx % 4]
                v.wait_ge(idx4, P["sld_x1s", 1, p])
                op(v.tensor_tensor, osl, osl,
                   x1s[:, (idx % 4) * 1024:(idx % 4 + 1) * 1024], ALU.add)
                assert vc.v == P["ve_s2add1", p]
                op(v.reduce_sum, mt2[:, col], osl, AX.X)
                assert vc.v == P["ve_mred1", p]

            def ln2b1_var(p):
                idx = 8 + p
                col = slice(idx, idx + 1)
                v.wait_ge(acts, P["acts_sq2", 1, p])
                op(v.reduce_sum, v2t2[:, col],
                   sq2[:, (idx % 2) * 1024:(idx % 2 + 1) * 1024], AX.X)
                op(v.tensor_scalar, tsm[:, col], mt2[:, col], mt2[:, col],
                   RAWEPS, ALU.mult, ALU.subtract)
                op(v.scalar_tensor_tensor, vt2[:, col], v2t2[:, col],
                   1024.0, tsm[:, col], ALU.mult, ALU.subtract)
                assert vc.v == P["ve_vt", 1, p]
                assert vc.v == P["ve_vteps1", p]

            def ln2b1_apply(p):
                idx = 8 + p
                col = slice(idx, idx + 1)
                osl = outb[1][:, p * 1024:(p + 1) * 1024]
                v.wait_ge(acts, P["acts_rstd1", p])
                op(v.tensor_scalar, nmr2[:, col], mt2[:, col], rt2[:, col],
                   -1.0 / 1024.0, ALU.mult, ALU.mult)
                op(v.tensor_scalar, osl, osl, rt2[:, col], nmr2[:, col],
                   ALU.mult, ALU.add)
                assert vc.v == P["ve_apply", 1, p]

            def ln2b1():
                for p in range(8):
                    ln2b1_head(p)
                    if p >= 1:
                        ln2b1_var(p - 1)
                    if p >= 2:
                        ln2b1_apply(p - 2)
                ln2b1_var(7)
                ln2b1_apply(6)
                ln2b1_apply(7)

            def ce(b, l):
                if (b, l) != (0, 1):
                    ob, ol = (1, l - 1) if b == 0 else (0, l)
                    v.wait_ge(pe, P["pe_tp", ob, ol, 15])
                op(v.tensor_copy, obf[:, :], outb[b][:, :])
                assert vc.v == P["ve_obf", b, l]
                for g in range(16):
                    v.wait_ge(pe, P["pe_tp", b, l, g])
                    fc, pg = divmod(g, 2)
                    op(v.tensor_copy,
                       outT[b][:, fc * 1024 + pg * 512:fc * 1024 + pg * 512 + 512],
                       psE[g % 2][:, :])
                    assert vc.v == P["ve_tpev", b, l, g]

            twist(0, 0)
            ce(0, 1)
            twist(1, 0)
            ce(1, 1)
            twist(0, 1)
            ce(0, 2)
            twist(1, 1)
            ce(1, 2)
            twist(0, 2)
            ln2(0)
            twist(1, 2)
            ln2b1()
    return nc


def l3_host_inputs(x1_full, x1bf, gcp, W):
    import ml_dtypes
    bf = ml_dtypes.bfloat16
    gcT = np.zeros((128, 128), dtype=np.float32)
    for cr in range(NC):
        for b in range(B):
            gcT[:, cr * 16 + b * 8:cr * 16 + (b + 1) * 8] = (
                gcp[cr, b].reshape(8, 128).T)
    common = {
        "gcT_in": gcT,
        "w1_in": np.ascontiguousarray(
            W["cn_w1"].reshape(8, 128, 8, 128).transpose(2, 1, 0, 3)
        ).reshape(8, 128, 1024),
        "w2_in": W["cn_w2"].reshape(8, 128, 512),
        "w3_in": W["cn_w3"].reshape(4, 128, 256),
        "w4_in": np.ascontiguousarray(W["cn_w4"].reshape(2, 128).T),
        "gw1_in": W["gc_w1"].reshape(8, 128, 512),
        "gw2_in": W["gc_w2"].reshape(4, 128, 256),
        "gw3_in": np.ascontiguousarray(W["gc_w3"].reshape(2, 128).T),
        "sml_in": np.concatenate(
            [W["cn_b1"].reshape(8, 128).T, W["cn_b2"].reshape(4, 128).T,
             W["cn_b3"].reshape(2, 128).T, W["gc_b1"].reshape(4, 128).T,
             W["gc_b2"].reshape(2, 128).T], axis=1),
        "b4_in": W["cn_b4"].reshape(1, 1),
        "gb3_in": W["gc_b3"].reshape(1, 1),
        "ar_in": W["adaptive_range"].reshape(1, 1),
        "ident_in": np.eye(128, dtype=bf),
    }
    common = {k: np.ascontiguousarray(v, dtype=v.dtype) for k, v in common.items()}
    maps = []
    for c in range(NC):
        rows = np.concatenate([np.arange(t * 128, (t + 1) * 128)
                               for t in snake_tiles(c)])
        x1c = np.ascontiguousarray(x1_full[:, rows, :]).reshape(B, 8, 128, 1024)
        x1Tc = np.ascontiguousarray(
            x1bf[:, rows, :].transpose(0, 2, 1)).reshape(B, 8, 128, 1024)
        maps.append({"x1_in": x1c, "x1T_in": x1Tc, **common})
    return maps


def l3_host_outputs(res):
    out = np.zeros((B, S, D), dtype=np.float32)
    for c in range(NC):
        r = res.results[c]
        for pi, t in enumerate(snake_tiles(c)):
            out[:, t * 128:(t + 1) * 128, :] = r["x2"][:, pi]
    return out



from concourse.bass_utils import run_bass_kernel_spmd

_NC_CACHE = {}


def _get(name, builder):
    if name not in _NC_CACHE:
        _NC_CACHE[name] = builder()
    return _NC_CACHE[name]


def kernel(L0_attn_x, L0_attn_Wq, L0_attn_Wk, L0_attn_Wv,
           L0_ffn_cn_w1, L0_ffn_cn_b1, L0_ffn_cn_w2, L0_ffn_cn_b2,
           L0_ffn_cn_w3, L0_ffn_cn_b3, L0_ffn_cn_w4, L0_ffn_cn_b4,
           L0_ffn_gc_w1, L0_ffn_gc_b1, L0_ffn_gc_w2, L0_ffn_gc_b2,
           L0_ffn_gc_w3, L0_ffn_gc_b3, L0_ffn_adaptive_range):
    x = np.asarray(L0_attn_x, dtype=np.float32)
    cores = list(range(NC))

    nc1 = _get("l1", build_l1)
    res1 = run_bass_kernel_spmd(
        nc1, l1_host_inputs(x, np.asarray(L0_attn_Wq), np.asarray(L0_attn_Wk),
                            np.asarray(L0_attn_Wv)), core_ids=cores)
    qT, kT, vT, usum = l1_host_outputs(res1)

    nc2 = _get("l2", build_l2)
    res2 = run_bass_kernel_spmd(nc2, l2_host_inputs(x, qT, kT, vT, usum),
                                core_ids=cores)
    x1, x1bf, gcp = l2_host_outputs(res2)

    W = {
        "cn_w1": np.asarray(L0_ffn_cn_w1), "cn_b1": np.asarray(L0_ffn_cn_b1),
        "cn_w2": np.asarray(L0_ffn_cn_w2), "cn_b2": np.asarray(L0_ffn_cn_b2),
        "cn_w3": np.asarray(L0_ffn_cn_w3), "cn_b3": np.asarray(L0_ffn_cn_b3),
        "cn_w4": np.asarray(L0_ffn_cn_w4), "cn_b4": np.asarray(L0_ffn_cn_b4),
        "gc_w1": np.asarray(L0_ffn_gc_w1), "gc_b1": np.asarray(L0_ffn_gc_b1),
        "gc_w2": np.asarray(L0_ffn_gc_w2), "gc_b2": np.asarray(L0_ffn_gc_b2),
        "gc_w3": np.asarray(L0_ffn_gc_w3), "gc_b3": np.asarray(L0_ffn_gc_b3),
        "adaptive_range": np.asarray(L0_ffn_adaptive_range),
    }
    nc3 = _get("l3", build_l3)
    res3 = run_bass_kernel_spmd(nc3, l3_host_inputs(x1, x1bf, gcp, W), core_ids=cores)
    return l3_host_outputs(res3)


# version 20
# speedup vs baseline: 1.0277x; 1.0277x over previous
import numpy as np
import concourse.bass as bass
import concourse.mybir as mybir

F32 = mybir.dt.float32
BF16 = mybir.dt.bfloat16
AF = mybir.ActivationFunctionType
ALU = mybir.AluOpType
AX = mybir.AxisListType

B = 2
S = 8192
D = 1024
NC = 8
TOK = 1024


class Ctr:
    def __init__(self):
        self.v = 0
    def inc(self, n=1):
        self.v += n
        return self.v



def build_l1():
    nc = bass.Bass("TRN2", target_bir_lowering=False)
    xT = nc.dram_tensor("xT", [B, 8, 128, TOK], F32, kind="ExternalInput").ap()
    wqT = nc.dram_tensor("wqT", [8, 128, D], F32, kind="ExternalInput").ap()
    wkT = nc.dram_tensor("wkT", [8, 128, D], F32, kind="ExternalInput").ap()
    wvT = nc.dram_tensor("wvT", [8, 128, D], F32, kind="ExternalInput").ap()
    qT = nc.dram_tensor("qT", [B, 8, 128, TOK], BF16, kind="ExternalOutput").ap()
    kT = nc.dram_tensor("kT", [B, 8, 128, TOK], BF16, kind="ExternalOutput").ap()
    vT = nc.dram_tensor("vT", [B, 8, 128, TOK], BF16, kind="ExternalOutput").ap()
    usum = nc.dram_tensor("usum", [128, 16], BF16, kind="ExternalOutput").ap()

    projs = [("k", wkT, kT), ("v", wvT, vT), ("q", wqT, wqT)]

    AC = {"wk0": 1, "x00": 2, "wk1": 3, "x10": 4, "x20": 5, "x30": 6,
          "wvc": 7, "wqc": 8}
    pe_c = Ctr()
    PE_AT = {}
    for pb in range(6):
        for th in range(2):
            for dc in range(8):
                pe_c.inc(); PE_AT[(pb, th, dc)] = pe_c.v
    ve_c = Ctr()
    EV = {}
    for pb in range(6):
        for th in range(2):
            for dc in range(8):
                ve_c.inc(); EV[(pb, th, dc)] = ve_c.v
                if pb in (2, 3) and th == 1:
                    ve_c.inc()
    ve_c.inc()
    VE_TOTAL = ve_c.v

    from contextlib import ExitStack
    with ExitStack() as _ctx:
        def _e(t):
            return _ctx.enter_context(t)
        x_sb = _e(nc.sbuf_tensor("x_sb", [128, B * 8 * TOK], BF16))
        x32 = _e(nc.sbuf_tensor("x32", [128, 8 * TOK], F32))
        w32a = _e(nc.sbuf_tensor("w32a", [128, 8 * D], F32))
        w32b = _e(nc.sbuf_tensor("w32b", [128, 8 * D], F32))
        w_sb0 = _e(nc.sbuf_tensor("w_sb0", [128, 8 * D], BF16))
        w_sb1 = _e(nc.sbuf_tensor("w_sb1", [128, 8 * D], BF16))
        st0 = _e(nc.sbuf_tensor("st0", [128, 8 * TOK], BF16))
        st1 = _e(nc.sbuf_tensor("st1", [128, 8 * TOK], BF16))
        us_sb = _e(nc.sbuf_tensor("us_sb", [128, 16], F32))
        usb_sb = _e(nc.sbuf_tensor("usb_sb", [128, 16], BF16))
        psb = [_e(nc.psum_tensor(f"psb{i}", [128, 512], F32)) for i in range(8)]
        g_x = [_e(nc.semaphore(f"g_x{i}")) for i in range(3)]
        s_wk0 = _e(nc.semaphore("s_wk0"))
        s_wk1 = _e(nc.semaphore("s_wk1"))
        s_wq = _e(nc.semaphore("s_wq"))
        s_w1 = _e(nc.semaphore("s_w1"))
        a_x = [_e(nc.semaphore(f"a_x{i}")) for i in range(2)]
        pe = _e(nc.semaphore("pe"))
        acts = _e(nc.semaphore("acts"))
        ev = _e(nc.semaphore("ev"))
        stA = _e(nc.semaphore("stA"))
        stB = _e(nc.semaphore("stB"))
        block = _e(nc.Block())
        w_sb = [w_sb0, w_sb1]
        stg = [st0, st1]
        stp = [stA, stB]
        XSLOT = {0: 0, 1: 1, 2: 0, 3: 1}
        XSEM = {0: a_x[0], 1: g_x[1], 2: a_x[1], 3: g_x[0]}
        XCNT = {0: 16, 1: 16, 2: 16, 3: 16}

        def xsrc(i):
            b, half = divmod(i, 2)
            return xT[b, half * 4:(half + 1) * 4].rearrange("hc p t -> p hc t")

        def xdst(i):
            s = XSLOT[i]
            return x32[:, s * 4 * TOK:(s + 1) * 4 * TOK].rearrange(
                "p (hc t) -> p hc t", hc=4)

        @block.gpsimd
        def _(g):
            g.dma_start(out=xdst(1), in_=xsrc(1)).then_inc(XSEM[1], 16)
            g.wait_ge(acts, AC["x10"])
            g.dma_start(out=xdst(3), in_=xsrc(3)).then_inc(XSEM[3], 16)

        @block.sync
        def _(sy):
            w32a3 = w32a[:, :].rearrange("p (hc d) -> p hc d", hc=8)
            w32b3 = w32b[:, :].rearrange("p (hc d) -> p hc d", hc=8)
            sy.dma_start(out=w32a[:, 0:4 * D].rearrange("p (hc d) -> p hc d", hc=4),
                         in_=wkT[0:4].rearrange("hc p d -> p hc d")).then_inc(s_wk0, 16)
            sy.dma_start(out=w32a[:, 4 * D:8 * D].rearrange("p (hc d) -> p hc d", hc=4),
                         in_=wkT[4:8].rearrange("hc p d -> p hc d")).then_inc(s_wk1, 16)
            sy.dma_start(out=w32b3,
                         in_=wvT.rearrange("hc p d -> p hc d")).then_inc(s_w1, 16)
            sy.wait_ge(acts, AC["wk1"])
            sy.dma_start(out=w32a3,
                         in_=wqT.rearrange("hc p d -> p hc d")).then_inc(s_wq, 16)
            npar = [0, 0]
            for pb in range(6):
                i, b = divmod(pb, 2)
                out = [kT, vT, qT][i]
                for h in range(4):
                    sy.wait_ge(ev, EV[(pb, 1, 2 * h + 1)])
                    sy.dma_start(out=out[b, 2 * h:2 * h + 2].rearrange("dc p t -> p dc t"),
                                 in_=stg[pb % 2][:, 2 * h * TOK:(2 * h + 2) * TOK].rearrange(
                                     "p (dc t) -> p dc t", dc=2)).then_inc(stp[pb % 2], 16)
                    npar[pb % 2] += 1
            sy.wait_ge(ev, VE_TOTAL)
            sy.dma_start(out=usum, in_=usb_sb[:, :]).then_inc(stA, 16)
            npar[0] += 1
            sy.wait_ge(stA, 16 * npar[0])
            sy.wait_ge(stB, 16 * npar[1])

        @block.scalar
        def _(s):
            sc = Ctr()

            def cp(dst, srcap):
                s.wait_ge(acts, sc.v)
                s.copy(dst, srcap).then_inc(acts, 1)
                sc.inc()

            def xcast(i):
                b, half = divmod(i, 2)
                sl = XSLOT[i]
                s.wait_ge(XSEM[i], XCNT[i])
                cp(x_sb[:, (b * 8 + half * 4) * TOK:(b * 8 + half * 4 + 4) * TOK],
                   x32[:, sl * 4 * TOK:(sl + 1) * 4 * TOK])

            s.dma_start(out=xdst(0), in_=xsrc(0)).then_inc(XSEM[0], 16)
            s.wait_ge(s_wk0, 16)
            cp(w_sb0[:, 0:4 * D], w32a[:, 0:4 * D])
            assert sc.v == AC["wk0"]
            xcast(0)
            assert sc.v == AC["x00"]
            s.wait_ge(acts, AC["x00"])
            s.dma_start(out=xdst(2), in_=xsrc(2)).then_inc(XSEM[2], 16)
            s.wait_ge(s_wk1, 16)
            cp(w_sb0[:, 4 * D:8 * D], w32a[:, 4 * D:8 * D])
            assert sc.v == AC["wk1"]
            xcast(1)
            xcast(2)
            xcast(3)
            assert sc.v == AC["x30"]
            s.wait_ge(s_w1, 16)
            cp(w_sb1[:, :], w32b[:, :])
            assert sc.v == AC["wvc"]
            s.wait_ge(pe, PE_AT[(1, 1, 7)])
            s.wait_ge(s_wq, 16)
            cp(w_sb0[:, :], w32a[:, :])
            assert sc.v == AC["wqc"]

        @block.vector
        def _(v):
            vc = Ctr()

            def op(fn, *args, **kw):
                v.wait_ge(ev, vc.v)
                fn(*args, **kw).then_inc(ev, 1)
                vc.inc()

            for pb in range(6):
                i, b = divmod(pb, 2)
                if pb >= 2:
                    v.wait_ge(stp[pb % 2], 64 * (pb // 2))
                for th in range(2):
                    for dc in range(8):
                        v.wait_ge(pe, PE_AT[(pb, th, dc)])
                        sgn = -1.0 if (i == 2 and b == 1) else 1.0
                        op(v.tensor_scalar,
                           stg[pb % 2][:, dc * TOK + th * 512:dc * TOK + th * 512 + 512],
                           psb[dc][:, :], sgn, None, ALU.mult)
                        assert vc.v == EV[(pb, th, dc)]
                        if i == 1 and th == 1:
                            op(v.reduce_sum, us_sb[:, b * 8 + dc:b * 8 + dc + 1],
                               stg[pb % 2][:, dc * TOK:(dc + 1) * TOK], AX.X)
            op(v.tensor_copy, usb_sb[:, :], us_sb[:, :])
            assert vc.v == VE_TOTAL

        @block.tensor
        def _(t):
            for pb in range(6):
                i, b = divmod(pb, 2)
                w_ok = [AC["wk0"], AC["wk1"]] if i == 0 else (
                    [AC["wvc"]] * 2 if i == 1 else [AC["wqc"]] * 2)
                x_ok = [[AC["x00"], AC["x10"]], [AC["x20"], AC["x30"]]][b]
                for th in range(2):
                    for ph in range(2):
                        t.wait_ge(acts, max(w_ok[ph], x_ok[ph]))
                        for dc in range(8):
                            if ph == 0:
                                if (pb, th) != (0, 0):
                                    ppb, pth = (pb, 0) if th == 1 else (pb - 1, 1)
                                    t.wait_ge(ev, EV[(ppb, pth, dc)])
                            last = None
                            for hc in range(4 * ph, 4 * ph + 4):
                                last = t.matmul(
                                    psb[dc][:, :],
                                    w_sb[i % 2][:, hc * D + dc * 128:hc * D + dc * 128 + 128],
                                    x_sb[:, (b * 8 + hc) * TOK + th * 512:
                                         (b * 8 + hc) * TOK + th * 512 + 512],
                                    start=(hc == 0), stop=(hc == 7),
                                )
                            if ph == 1:
                                last.then_inc(pe, 1)
    return nc



def snake_tiles(c):
    return [c, 15 - c, 16 + c, 31 - c, 32 + c, 47 - c, 48 + c, 63 - c]


def l1_host_inputs(x, Wq, Wk, Wv):
    maps = []
    wqT = np.ascontiguousarray(Wq.T).reshape(8, 128, D)
    wkT = np.ascontiguousarray(Wk.T).reshape(8, 128, D)
    wvT = np.ascontiguousarray(Wv.T).reshape(8, 128, D)
    for c in range(NC):
        xs = x[:, c * TOK:(c + 1) * TOK, :]
        xT = np.ascontiguousarray(xs.transpose(0, 2, 1)).reshape(B, 8, 128, TOK)
        maps.append({"xT": xT, "wqT": wqT, "wkT": wkT, "wvT": wvT})
    return maps


def l1_host_outputs(res):
    import ml_dtypes
    qT = np.zeros((B, D, S), dtype=ml_dtypes.bfloat16)
    kT = np.zeros((B, D, S), dtype=ml_dtypes.bfloat16)
    vT = np.zeros((B, D, S), dtype=ml_dtypes.bfloat16)
    usum = np.zeros((B, NC, D), dtype=ml_dtypes.bfloat16)
    for c in range(NC):
        r = res.results[c]
        sl = slice(c * TOK, (c + 1) * TOK)
        qT[:, :, sl] = r["qT"].reshape(B, D, TOK)
        kT[:, :, sl] = r["kT"].reshape(B, D, TOK)
        vT[:, :, sl] = r["vT"].reshape(B, D, TOK)
        u = r["usum"]
        for b in range(B):
            usum[b, c] = u[:, b * 8:(b + 1) * 8].T.reshape(D)
    return qT, kT, vT, usum



def _pieces(w):
    out = []
    o = 0
    while o < w:
        c = min(512, w - o)
        out.append((o, c))
        o += c
    return out


def plan_l2():
    pe = Ctr(); acts = Ctr(); ve = Ctr(); st = Ctr()
    P = dict(pe_S={}, pe_V={}, pe_gc={}, acts_sig={}, acts_sq={}, acts_rstd={},
             acts_b16={}, ve_ready={}, ve_ready0={}, ve_evac={}, ve_xadd={}, ve_vt={},
             ve_vteps={}, ve_nmr={}, ve_apply={}, ve_b16={}, ve_gcev={},
             sld_K={}, sld_V={}, gld_x={}, st_x={}, st_b16={}, st_gc={})

    xc = [Ctr(), Ctr()]
    for p in range(8):
        for b in range(B):
            idx = p * 2 + b
            xc[idx % 2].inc(16)
            P["gld_x"][(b, p)] = xc[idx % 2].v

    for a in range(8):
        for rc in range(8):
            P["sld_K"][(a, rc)] = 16 * (a + 1)
        P["sld_V"][(a, 0)] = 16 * a
        P["sld_V"][(a, 1)] = 16 * a

    ve.inc(); P["ve_ones"] = ve.v

    for a in range(8):
        w = (8 - a) * 128
        for rc in range(8):
            pe.inc(); P["pe_S"][(a, rc)] = pe.v
        for rc in range(8):
            acts.inc(); P["acts_sig"][(a, rc)] = acts.v
        acts.inc(2); P["acts_rstd"][a] = acts.v
        for rc in range(8):
            ve.inc(2)
            if rc == 7:
                P["ve_ready0"][a] = ve.v
            ve.inc(2)
            if w > 128:
                ve.inc()
        P["ve_ready"][a] = ve.v
        for b in range(B):
            for p in range(a, 8):
                pe.inc(); P["pe_V"][(a, b, p)] = pe.v
                ve.inc(); P["ve_evac"][(a, b, p)] = ve.v
                if p == a:
                    ve.inc(); P["ve_xadd"][(b, p)] = ve.v
                    ve.inc()
                    ve.inc(2)
                    P["ve_vt"][(b, p)] = ve.v
                    if b == 1:
                        P["ve_vteps"][a] = ve.v
        for b in range(B):
            ve.inc(); P["ve_nmr"][(b, a)] = ve.v
            ve.inc(); P["ve_apply"][(b, a)] = ve.v
    for p in range(8):
        ve.inc(); P["ve_b16"][(1, p)] = ve.v
    for p in range(8):
        acts.inc(); P["acts_b16"][(0, p)] = acts.v
    for b in range(B):
        pe.inc(); P["pe_gc"][b] = pe.v
        ve.inc(); P["ve_gcev"][b] = ve.v
    for a in range(8):
        for b in range(B):
            st.inc(16); P["st_x"][(b, a)] = st.v
    for b in range(B):
        st.inc(16); P["st_b16"][b] = st.v
    for b in range(B):
        st.inc(16); P["st_gc"][b] = st.v
    P["st_total"] = st.v
    P["sig_seq"] = [P["acts_sig"][(a, rc)] for a in range(8) for rc in range(8)]
    return P


def build_l2():
    nc = bass.Bass("TRN2", target_bir_lowering=False)
    qT_in = nc.dram_tensor("qT_in", [B, 8, 128, 1024], BF16, kind="ExternalInput").ap()
    K_in = nc.dram_tensor("K_in", [8, 8, B, 128, 8, 128], BF16, kind="ExternalInput").ap()
    V_in = nc.dram_tensor("V_in", [B, 8, 8, 128, 1024], BF16, kind="ExternalInput").ap()
    us_in = nc.dram_tensor("us_in", [8, B * 1024], BF16, kind="ExternalInput").ap()
    c19_in = nc.dram_tensor("c19_in", [8, 1024], BF16, kind="ExternalInput").ap()
    m1_in = nc.dram_tensor("m1_in", [128, 2048], BF16, kind="ExternalInput").ap()
    m2_in = nc.dram_tensor("m2_in", [128, 2048], BF16, kind="ExternalInput").ap()

    xq_in = nc.dram_tensor("xq_in", [B, 8, 128, 1024], F32, kind="ExternalInput").ap()
    x1 = nc.dram_tensor("x1", [B, 8, 128, 1024], F32, kind="ExternalOutput").ap()
    x1b = nc.dram_tensor("x1b", [B, 8, 128, 1024], BF16, kind="ExternalOutput").ap()
    gcp = nc.dram_tensor("gcp", [B, 1024], F32, kind="ExternalOutput").ap()

    P = plan_l2()
    EPS = 1e-5
    RAWEPS = EPS * 1024.0 * 1024.0

    from contextlib import ExitStack
    with ExitStack() as _ctx:
        def _e(x):
            return _ctx.enter_context(x)
        qT_sb = _e(nc.sbuf_tensor("qT_sb", [128, B * 8 * 1024], BF16))
        K_rg = _e(nc.sbuf_tensor("K_rg", [128, 10 * 1024], BF16))
        V_sb = _e(nc.sbuf_tensor("V_sb", [128, 2 * 8 * 1024], BF16))
        at0 = _e(nc.sbuf_tensor("at0", [128, 8 * 1024], BF16))
        at1 = _e(nc.sbuf_tensor("at1", [128, 8 * 1024], BF16))
        acc = _e(nc.sbuf_tensor("acc", [128, B * 8 * 1024], F32))
        m1m = _e(nc.sbuf_tensor("m1m", [128, 2048], BF16))
        m2m = _e(nc.sbuf_tensor("m2m", [128, 2048], BF16))

        c19_sb = _e(nc.sbuf_tensor("c19_sb", [8, 1024], BF16))
        us_sb = _e(nc.sbuf_tensor("us_sb", [8, B * 1024], BF16))
        xres = _e(nc.sbuf_tensor("xres", [128, 2 * 1024], F32))
        sq = _e(nc.sbuf_tensor("sq", [128, 1024], F32))
        mt = _e(nc.sbuf_tensor("mt", [128, 16], F32))
        v2t = _e(nc.sbuf_tensor("v2t", [128, 16], F32))
        vt = _e(nc.sbuf_tensor("vt", [128, 16], F32))
        lt = _e(nc.sbuf_tensor("lt", [128, 16], F32))
        rt = _e(nc.sbuf_tensor("rt", [128, 16], F32))
        nmr = _e(nc.sbuf_tensor("nmr", [128, 16], F32))
        tmp1 = _e(nc.sbuf_tensor("tmp1", [128, 16], F32))
        ones = _e(nc.sbuf_tensor("ones", [128, 1], F32))
        psS0 = _e(nc.psum_tensor("psS0", [128, 1024], F32))
        psS1 = _e(nc.psum_tensor("psS1", [128, 1024], F32))
        psO0 = _e(nc.psum_tensor("psO0", [128, 1024], F32))
        psO1 = _e(nc.psum_tensor("psO1", [128, 1024], F32))
        s_Krc = [_e(nc.semaphore(f"s_K{i}")) for i in range(8)]
        s_V0 = _e(nc.semaphore("s_V0"))
        s_V1 = _e(nc.semaphore("s_V1"))
        g_V00 = _e(nc.semaphore("g_V00"))
        g_V01 = _e(nc.semaphore("g_V01"))
        g_qT0 = _e(nc.semaphore("g_qT0"))
        g_qT1 = _e(nc.semaphore("g_qT1"))
        g_cst = _e(nc.semaphore("g_cst"))
        g_x0 = _e(nc.semaphore("g_x0"))
        g_x1 = _e(nc.semaphore("g_x1"))
        pe = _e(nc.semaphore("pe"))
        acts = _e(nc.semaphore("acts"))
        ve = _e(nc.semaphore("ve"))
        st = _e(nc.semaphore("st"))
        block = _e(nc.Block())
        psS = [psS0, psS1]
        psO = [psO0, psO1]

        @block.gpsimd
        def _(g):
            g_V0p = [g_V00, g_V01]
            g_xp = [g_x0, g_x1]

            def kload(a, rc):
                grp = a * 8 + rc
                if grp >= 5:
                    pa, prc = divmod(grp - 5, 8)
                    g.wait_ge(pe, P["pe_S"][(pa, prc)])
                g.dma_start(
                    out=K_rg[:, (grp % 5) * 2048:(grp % 5 + 1) * 2048].rearrange(
                        "p (b ic rr) -> p b ic rr", b=2, ic=8),
                    in_=K_in[a, rc].rearrange("b p ic rr -> p b ic rr"),
                ).then_inc(s_Krc[rc], 16)

            def xload(p, b):
                idx = p * 2 + b
                if idx >= 2:
                    pb, bb = divmod(idx - 2, 2)
                    g.wait_ge(ve, P["ve_xadd"][(bb, pb)])
                g.dma_start(out=xres[:, (idx % 2) * 1024:(idx % 2 + 1) * 1024],
                            in_=xq_in[b, p]).then_inc(g_xp[idx % 2], 16)

            g.dma_start(
                out=V_sb[:, 0:8192].rearrange("p (jc f) -> p jc f", jc=8),
                in_=V_in[0, 0].rearrange("jc p f -> p jc f")).then_inc(g_V0p[0], 16)
            for rc in (1, 3, 5, 7):
                kload(0, rc)
            xload(0, 0)
            xload(0, 1)
            g.dma_start(
                out=V_sb[:, 8192:16384].rearrange("p (jc f) -> p jc f", jc=8),
                in_=V_in[1, 0].rearrange("jc p f -> p jc f")).then_inc(g_V0p[1], 16)
            for a in range(1, 8):
                for rc in (1, 3, 5, 7):
                    kload(a, rc)
                xload(a, 0)
                xload(a, 1)

        @block.sync
        def _(sy):
            s_Vp = [s_V0, s_V1]
            sy.dma_start(out=m1m[:, :], in_=m1_in).then_inc(g_cst, 16)
            sy.dma_start(out=m2m[:, :], in_=m2_in).then_inc(g_cst, 16)
            sy.dma_start(out=us_sb[:, :], in_=us_in).then_inc(g_cst, 16)
            sy.dma_start(out=c19_sb[:, :], in_=c19_in).then_inc(g_cst, 16)
            for a in range(8):
                for rc in (0, 2, 4, 6):
                    grp = a * 8 + rc
                    if grp >= 5:
                        pa, prc = divmod(grp - 5, 8)
                        sy.wait_ge(pe, P["pe_S"][(pa, prc)])
                    sy.dma_start(
                        out=K_rg[:, (grp % 5) * 2048:(grp % 5 + 1) * 2048].rearrange(
                            "p (b ic rr) -> p b ic rr", b=2, ic=8),
                        in_=K_in[a, rc].rearrange("b p ic rr -> p b ic rr"),
                    ).then_inc(s_Krc[rc], 16)
                if a >= 1:
                    for b in range(B):
                        sy.wait_ge(pe, P["pe_V"][(a - 1, b, 7)])
                        sy.dma_start(
                            out=V_sb[:, b * 8192:(b + 1) * 8192].rearrange(
                                "p (jc f) -> p jc f", jc=8),
                            in_=V_in[b, a].rearrange("jc p f -> p jc f")
                        ).then_inc(s_Vp[b], 16)
            sy.wait_ge(st, P["st_total"])

        @block.tensor
        def _(t):
            t.wait_ge(g_qT0, 16)
            t.wait_ge(g_cst, 16 * 4)
            s_Vp = [s_V0, s_V1]
            vgrp = 0
            for a in range(8):
                w = (8 - a) * 128
                for rc in range(8):
                    gidx = a * 8 + rc
                    if gidx >= 2:
                        t.wait_ge(acts, P["sig_seq"][gidx - 2])
                    t.wait_ge(s_Krc[rc], P["sld_K"][(a, rc)])
                    grp = a * 8 + rc
                    p2 = psS[rc % 2]
                    last = None
                    for b in range(B):
                        if gidx == 0 and b == 1:
                            t.wait_ge(g_qT1, 16)
                        for (off, pw) in _pieces(w):
                            for ic in range(8):
                                last = t.matmul(
                                    p2[:, off:off + pw],
                                    K_rg[:, (grp % 5) * 2048 + b * 1024 + ic * 128:
                                         (grp % 5) * 2048 + b * 1024 + ic * 128 + 128],
                                    qT_sb[:, (b * 8 + ic) * 1024 + a * 128 + off:
                                          (b * 8 + ic) * 1024 + a * 128 + off + pw],
                                    start=(b == 0 and ic == 0),
                                    stop=(b == 1 and ic == 7),
                                )
                    last.then_inc(pe, 1)
                for b in range(B):
                    if a == 0:
                        t.wait_ge([g_V00, g_V01][b], 16)
                    else:
                        t.wait_ge(s_Vp[b], P["sld_V"][(a, b)])
                    t.wait_ge(ve, P["ve_ready0"][a] if b == 0 else P["ve_ready"][a])
                    for p in range(a, 8):
                        if vgrp >= 2:
                            t.wait_ge(ve, _ve_evac_by_idx(P, vgrp - 2))
                        po = psO[vgrp % 2]
                        att = at0 if b == 0 else at1
                        last = None
                        for foff in (0, 512):
                            for rc in range(8):
                                last = t.matmul(
                                    po[:, foff:foff + 512],
                                    att[:, rc * 1024 + (p - a) * 128:
                                        rc * 1024 + (p - a) * 128 + 128],
                                    V_sb[:, (b * 8 + rc) * 1024 + foff:
                                         (b * 8 + rc) * 1024 + foff + 512],
                                    start=(rc == 0), stop=(rc == 7 and a != 0),
                                )
                            if a == 0:
                                last = t.matmul(
                                    po[:, foff:foff + 512],
                                    c19_sb[:, p * 128:(p + 1) * 128],
                                    us_sb[:, b * 1024 + foff:b * 1024 + foff + 512],
                                    start=False, stop=True,
                                )
                        last.then_inc(pe, 1)
                        vgrp += 1
            t.wait_ge(ve, P["ve_evac"][(7, 0, 7)])
            for b in range(B):
                if b >= 1:
                    t.wait_ge(ve, P["ve_gcev"][0])
                last = None
                for p in range(8):
                    t.wait_ge(ve, P["ve_apply"][(b, p)])
                    for foff in (0, 512):
                        last = t.matmul(
                            psO0[0:1, foff:foff + 512],
                            ones[:, 0:1],
                            acc[:, (b * 8 + p) * 1024 + foff:(b * 8 + p) * 1024 + foff + 512],
                            start=(p == 0), stop=(p == 7),
                        )
                last.then_inc(pe, 1)

        @block.scalar
        def _(s):
            s.dma_start(out=qT_sb[:, 0:8192].rearrange("p (ic m) -> p ic m", ic=8),
                        in_=qT_in[0].rearrange("ic p m -> p ic m")).then_inc(g_qT0, 16)
            s.dma_start(out=qT_sb[:, 8192:16384].rearrange("p (ic m) -> p ic m", ic=8),
                        in_=qT_in[1].rearrange("ic p m -> p ic m")).then_inc(g_qT1, 16)
            sc = Ctr()
            for a in range(8):
                w = (8 - a) * 128
                for rc in range(8):
                    s.wait_ge(acts, sc.v)
                    s.wait_ge(pe, P["pe_S"][(a, rc)])
                    s.activation(at0[:, rc * 1024:rc * 1024 + w],
                                 psS[rc % 2][:, 0:w], AF.Sigmoid,
                                 scale=1.0 / 32.0).then_inc(acts, 1)
                    sc.inc()
                s.wait_ge(acts, sc.v)
                s.wait_ge(ve, P["ve_vteps"][a])
                s.activation(lt[:, 2 * a:2 * a + 2], vt[:, 2 * a:2 * a + 2],
                             AF.Ln, scale=2.0 ** -20).then_inc(acts, 1)
                sc.inc()
                s.wait_ge(acts, sc.v)
                s.activation(rt[:, 2 * a:2 * a + 2], lt[:, 2 * a:2 * a + 2],
                             AF.Exp, scale=-0.5).then_inc(acts, 1)
                sc.inc()
                for b in range(B):
                    s.wait_ge(ve, P["ve_apply"][(b, a)])
                    s.dma_start(out=x1[b, a],
                                in_=acc[:, (b * 8 + a) * 1024:(b * 8 + a + 1) * 1024]
                                ).then_inc(st, 16)
            s.wait_ge(pe, P["pe_V"][(7, 0, 7)])
            for p in range(8):
                s.wait_ge(acts, sc.v)
                s.wait_ge(ve, P["ve_apply"][(0, p)])
                s.copy(at0[:, p * 1024:(p + 1) * 1024],
                       acc[:, p * 1024:(p + 1) * 1024]).then_inc(acts, 1)
                sc.inc()
            s.wait_ge(acts, sc.v)
            s.dma_start(out=x1b[0].rearrange("p ii f -> ii p f"),
                        in_=at0[:, :].rearrange("ii (p f) -> ii p f", p=8)
                        ).then_inc(st, 16)
            s.wait_ge(ve, P["ve_b16"][(1, 7)])
            s.dma_start(out=x1b[1].rearrange("p ii f -> ii p f"),
                        in_=at1[:, :].rearrange("ii (p f) -> ii p f", p=8)
                        ).then_inc(st, 16)
            for b in range(B):
                s.wait_ge(ve, P["ve_gcev"][b])
                s.dma_start(out=gcp[b:b + 1, :],
                            in_=sq[0:1, 0:1024]).then_inc(st, 16)

        @block.vector
        def _(v):
            vc = Ctr()

            def op(fn, *args, **kw):
                v.wait_ge(ve, vc.v)
                fn(*args, **kw).then_inc(ve, 1)
                vc.inc()

            op(v.memset, ones[:, :], 1.0)
            v.wait_ge(g_cst, 16 * 4)
            g_xp = [g_x0, g_x1]
            for a in range(8):
                w = (8 - a) * 128
                par = a % 2
                for rc in range(8):
                    v.wait_ge(acts, P["acts_sig"][(a, rc)])
                    sl0 = at0[:, rc * 1024:rc * 1024 + 128]
                    msl = slice(par * 1024 + rc * 128, par * 1024 + rc * 128 + 128)
                    op(v.tensor_tensor, sl0, sl0, m1m[:, msl], ALU.mult)
                    op(v.tensor_tensor, sl0, sl0, m2m[:, msl], ALU.add)
                    if rc == 7:
                        assert vc.v == P["ve_ready0"][a]
                    op(v.tensor_tensor, at1[:, rc * 1024:rc * 1024 + 128],
                       m1m[:, msl], sl0, ALU.subtract)
                    op(v.scalar_tensor_tensor, at1[:, rc * 1024:rc * 1024 + 128],
                       m2m[:, msl], 2.0, at1[:, rc * 1024:rc * 1024 + 128],
                       ALU.mult, ALU.add)
                    if w > 128:
                        op(v.tensor_scalar, at1[:, rc * 1024 + 128:rc * 1024 + w],
                       at0[:, rc * 1024 + 128:rc * 1024 + w], -1.0, 1.0,
                       ALU.mult, ALU.add)
                assert vc.v == P["ve_ready"][a]
                for b in range(B):
                    for p in range(a, 8):
                        v.wait_ge(pe, P["pe_V"][(a, b, p)])
                        asl = acc[:, (b * 8 + p) * 1024:(b * 8 + p + 1) * 1024]
                        po = psO[_vgrp_idx(a, b, p) % 2]
                        if a == 0:
                            op(v.tensor_copy, asl, po[:, :])
                        else:
                            op(v.tensor_tensor, asl, asl, po[:, :], ALU.add)
                        assert vc.v == P["ve_evac"][(a, b, p)]
                        if p == a:
                            idx = p * 2 + b
                            col = slice(2 * a + b, 2 * a + b + 1)
                            v.wait_ge(g_xp[idx % 2], P["gld_x"][(b, p)])
                            op(v.scalar_tensor_tensor, asl,
                               xres[:, (idx % 2) * 1024:(idx % 2 + 1) * 1024],
                               1.0, asl, ALU.mult, ALU.add, accum_out=mt[:, col])
                            assert vc.v == P["ve_xadd"][(b, p)]
                            op(v.scalar_tensor_tensor, sq[:, :], asl, 1.0, asl,
                               ALU.mult, ALU.mult, accum_out=v2t[:, col])
                            op(v.tensor_scalar, tmp1[:, col], mt[:, col],
                               mt[:, col], RAWEPS, ALU.mult, ALU.subtract)
                            op(v.scalar_tensor_tensor, vt[:, col], v2t[:, col],
                               1024.0, tmp1[:, col], ALU.mult, ALU.subtract)
                            assert vc.v == P["ve_vt"][(b, p)]
                            if b == 1:
                                assert vc.v == P["ve_vteps"][a]
                v.wait_ge(acts, P["acts_rstd"][a])
                for b in range(B):
                    col = slice(2 * a + b, 2 * a + b + 1)
                    op(v.tensor_scalar, nmr[:, col], mt[:, col], rt[:, col],
                       -1.0 / 1024.0, ALU.mult, ALU.mult)
                    assert vc.v == P["ve_nmr"][(b, a)]
                    asl = acc[:, (b * 8 + a) * 1024:(b * 8 + a + 1) * 1024]
                    op(v.tensor_scalar, asl, asl, rt[:, col], nmr[:, col],
                       ALU.mult, ALU.add)
                    assert vc.v == P["ve_apply"][(b, a)]
            for p in range(8):
                op(v.tensor_copy, at1[:, p * 1024:(p + 1) * 1024],
                   acc[:, (8 + p) * 1024:(9 + p) * 1024])
                assert vc.v == P["ve_b16"][(1, p)]
            for b in range(B):
                v.wait_ge(pe, P["pe_gc"][b])
                if b >= 1:
                    v.wait_ge(st, P["st_gc"][0])
                op(v.tensor_copy, sq[0:1, 0:1024], psO0[0:1, :])
                assert vc.v == P["ve_gcev"][b]
    return nc


def _vgrp_idx(a, b, p):
    i = 0
    for aa in range(a):
        i += 2 * (8 - aa)
    return i + b * (8 - a) + (p - a)


def _ve_evac_by_idx(P, vgrp):
    for (a, b, p), val in P["ve_evac"].items():
        if _vgrp_idx(a, b, p) == vgrp:
            return val
    raise KeyError(vgrp)


def l2_host_inputs(x, qT, kT, vT, usum):
    import ml_dtypes
    bf = ml_dtypes.bfloat16
    k_full = np.ascontiguousarray(qT[0:0]).reshape(0,)
    k_tok = np.ascontiguousarray(kT.transpose(0, 2, 1))
    v_tok = np.ascontiguousarray(vT.transpose(0, 2, 1))
    K_in = np.zeros((8, 8, B, 128, 8, 128), dtype=bf)
    for a in range(8):
        ks = k_tok[:, a::8, :].reshape(B, 8, 128, 8, 128)
        K_in[a] = ks.transpose(3, 0, 2, 1, 4)
    V_in = v_tok.reshape(B, 8, 8, 128, 1024)
    us_in = np.ascontiguousarray(usum.transpose(1, 0, 2)).reshape(8, B * 1024)
    c19 = np.zeros((8, 1024), dtype=bf)
    for A in range(8):
        for p in range(8):
            if A > p:
                c19[A, p * 128:(p + 1) * 128] = 1e-9
    maps = []
    for c in range(NC):
        rows = np.concatenate([np.arange(t * 128, (t + 1) * 128)
                               for t in snake_tiles(c)])
        qTc = np.ascontiguousarray(qT[:, :, rows]).reshape(B, 8, 128, 1024)
        xqc = np.ascontiguousarray(x[:, rows, :]).reshape(B, 8, 128, 1024)
        m1 = np.zeros((128, 2048), dtype=np.float32)
        for par, tm8 in ((0, c), (1, 7 - c)):
            for rc in range(8):
                slc = slice(par * 1024 + rc * 128, par * 1024 + rc * 128 + 128)
                if rc < tm8:
                    m1[:, slc] = 1.0
                elif rc == tm8:
                    jj = np.arange(128)[:, None]
                    mm = np.arange(128)[None, :]
                    m1[:, slc] = (jj <= mm).astype(np.float32)
        m2 = 1e-9 * (1.0 - m1)
        maps.append({
            "qT_in": qTc, "K_in": K_in, "V_in": V_in, "us_in": us_in,
            "c19_in": c19, "m1_in": m1.astype(bf), "m2_in": m2.astype(bf),
            "xq_in": xqc,
        })
    return maps


def l2_host_outputs(res):
    import ml_dtypes
    x1 = np.zeros((B, S, D), dtype=np.float32)
    x1bf = np.zeros((B, S, D), dtype=ml_dtypes.bfloat16)
    gcp = np.zeros((NC, B, D), dtype=np.float32)
    for c in range(NC):
        r = res.results[c]
        for pi, t in enumerate(snake_tiles(c)):
            x1[:, t * 128:(t + 1) * 128, :] = r["x1"][:, pi]
            x1bf[:, t * 128:(t + 1) * 128, :] = r["x1b"][:, pi]
        gcp[c] = r["gcp"].reshape(B, D)
    return x1, x1bf, gcp



def plan_l3():
    pe = Ctr(); acts = Ctr(); ve = Ctr(); gld = Ctr(); sld = Ctr(); st = Ctr()
    P = {}
    xp = [Ctr(), Ctr(), Ctr(), Ctr()]
    for b in range(B):
        for p in range(8):
            idx = b * 8 + p
            xp[idx % 4].inc(16)
            P["sld_x1s", b, p] = xp[idx % 4].v

    ve.inc(1); P["ve_ones"] = ve.v
    ve.inc(9); P["ve_gc16b"] = ve.v

    for oc in range(8):
        pe.inc(); P["pe_h1", 0, 0, oc] = pe.v
    for b in range(B):
        for oc in range(4):
            pe.inc(); P["pe_g1", b, oc] = pe.v
        for oc in range(2):
            pe.inc(); P["pe_g2", b, oc] = pe.v
        pe.inc(); P["pe_gf", b] = pe.v
    pe.inc(); P["pe_bc"] = pe.v
    for l in range(3):
        for b in range(B):
            if l > 0:
                for g in range(16):
                    pe.inc(); P["pe_tp", b, l, g] = pe.v
            for oc in range(8):
                if (b, l) != (0, 0):
                    pe.inc(); P["pe_h1", b, l, oc] = pe.v
            for oc in range(4):
                pe.inc(); P["pe_h2", b, l, oc] = pe.v
            for oc in range(2):
                pe.inc(); P["pe_h3", b, l, oc] = pe.v
            pe.inc(); P["pe_tf", b, l] = pe.v
    for oc in range(8):
        acts.inc(); P["acts_h1", 0, 0, oc] = acts.v
    for b in range(B):
        for oc in range(4):
            acts.inc(); P["acts_g1", b, oc] = acts.v
        for oc in range(2):
            acts.inc(); P["acts_g2", b, oc] = acts.v
    for b in range(B):
        acts.inc(); P["acts_gf", b] = acts.v
    ve.inc(); P["ve_bc"] = ve.v
    ve.inc(7); P["ve_AB"] = ve.v
    def _plan_hacts(b, l):
        for oc in range(8):
            if (b, l) != (0, 0):
                acts.inc(); P["acts_h1", b, l, oc] = acts.v
        for oc in range(4):
            acts.inc(); P["acts_h2", b, l, oc] = acts.v
        for oc in range(2):
            acts.inc(); P["acts_h3", b, l, oc] = acts.v

    def _plan_sig(b, l):
        acts.inc(); P["acts_tf", b, l] = acts.v

    for l in range(2):
        _plan_hacts(0, l); _plan_sig(0, l)
        _plan_hacts(1, l); _plan_sig(1, l)
    _plan_hacts(0, 2); _plan_sig(0, 2)
    for oc in range(8):
        acts.inc(); P["acts_h1", 1, 2, oc] = acts.v
    acts.inc(2); P["acts_rstd2", 0] = acts.v
    for oc in range(4):
        acts.inc(); P["acts_h2", 1, 2, oc] = acts.v
    for oc in range(2):
        acts.inc(); P["acts_h3", 1, 2, oc] = acts.v
    _plan_sig(1, 2)
    for p in range(8):
        acts.inc(2); P["acts_rstd1", p] = acts.v
    def _plan_CE(b, l):
        ve.inc(); P["ve_obf", b, l] = ve.v
        for g in range(16):
            ve.inc(); P["ve_tpev", b, l, g] = ve.v

    def _plan_T(b, l):
        ve.inc(2); P["ve_coup", b, l] = ve.v
        for p in range(8):
            ve.inc(3); P["ve_twc", b, l, p] = ve.v
        P["ve_twist", b, l] = ve.v

    po = Ctr()
    P["po"] = po

    def _plan_LN2(b):
        if b == 0:
            for p in range(8):
                po.inc(); P["po_s2add", 0, p] = po.v
                po.inc(); P["po_sq", 0, p] = po.v
                ve.inc()
                ve.inc(3)
                P["ve_vt", 0, p] = ve.v
            P["ve_vt2eps", 0] = ve.v
            ve.inc(2)
            P["ve_nmrb", 0] = ve.v
            for p in range(8):
                ve.inc(); P["ve_apply", 0, p] = ve.v
        else:
            for p in range(8):
                ve.inc(); P["ve_s2add1", p] = ve.v
                if p >= 1:
                    ve.inc(3)
                    P["ve_vt", 1, p - 1] = ve.v
                    P["ve_vteps1", p - 1] = ve.v
                if p >= 2:
                    ve.inc(2)
                    P["ve_apply", 1, p - 2] = ve.v
            ve.inc(3); P["ve_vt", 1, 7] = ve.v
            P["ve_vteps1", 7] = ve.v
            ve.inc(2); P["ve_apply", 1, 6] = ve.v
            ve.inc(2); P["ve_apply", 1, 7] = ve.v

    _plan_T(0, 0)
    _plan_CE(0, 1)
    _plan_T(1, 0)
    _plan_CE(1, 1)
    _plan_T(0, 1)
    _plan_CE(0, 2)
    _plan_T(1, 1)
    _plan_CE(1, 2)
    _plan_T(0, 2)
    _plan_LN2(0)
    _plan_T(1, 2)
    _plan_LN2(1)
    st.inc(16 * 9)
    P["st_total"] = st.v
    seq = [("acts_h1", 0, 0, oc) for oc in range(8)]
    for l in range(3):
        for b in range(B):
            if (b, l) != (0, 0):
                seq += [("acts_h1", b, l, oc) for oc in range(8)]
            seq += [("acts_h2", b, l, oc) for oc in range(4)]
            seq += [("acts_h3", b, l, oc) for oc in range(2)]
    P["psh_prev"] = [P[k] for k in seq]
    return P


def build_l3():
    nc = bass.Bass("TRN2", target_bir_lowering=False)
    x1_in = nc.dram_tensor("x1_in", [B, 8, 128, 1024], F32, kind="ExternalInput").ap()
    x1T_in = nc.dram_tensor("x1T_in", [B, 8, 128, 1024], BF16, kind="ExternalInput").ap()
    gcT_in = nc.dram_tensor("gcT_in", [128, 128], F32, kind="ExternalInput").ap()
    w1_in = nc.dram_tensor("w1_in", [8, 128, 1024], F32, kind="ExternalInput").ap()
    w2_in = nc.dram_tensor("w2_in", [8, 128, 512], F32, kind="ExternalInput").ap()
    w3_in = nc.dram_tensor("w3_in", [4, 128, 256], F32, kind="ExternalInput").ap()
    w4_in = nc.dram_tensor("w4_in", [128, 2], F32, kind="ExternalInput").ap()
    gw1_in = nc.dram_tensor("gw1_in", [8, 128, 512], F32, kind="ExternalInput").ap()
    gw2_in = nc.dram_tensor("gw2_in", [4, 128, 256], F32, kind="ExternalInput").ap()
    gw3_in = nc.dram_tensor("gw3_in", [128, 2], F32, kind="ExternalInput").ap()
    sml_in = nc.dram_tensor("sml_in", [128, 20], F32, kind="ExternalInput").ap()
    b4_in = nc.dram_tensor("b4_in", [1, 1], F32, kind="ExternalInput").ap()
    gb3_in = nc.dram_tensor("gb3_in", [1, 1], F32, kind="ExternalInput").ap()
    ar_in = nc.dram_tensor("ar_in", [1, 1], F32, kind="ExternalInput").ap()
    ident_in = nc.dram_tensor("ident_in", [128, 128], BF16, kind="ExternalInput").ap()
    x2 = nc.dram_tensor("x2", [B, 8, 128, 1024], F32, kind="ExternalOutput").ap()

    P = plan_l3()
    EPS = 1e-5
    RAWEPS = EPS * 1024.0 * 1024.0
    from contextlib import ExitStack
    with ExitStack() as _ctx:
        def _e(x):
            return _ctx.enter_context(x)
        w1s = _e(nc.sbuf_tensor("w1s", [128, 8 * 1024], BF16))
        w2s = _e(nc.sbuf_tensor("w2s", [128, 8 * 512], BF16))
        w3s = _e(nc.sbuf_tensor("w3s", [128, 4 * 256], BF16))
        w4s = _e(nc.sbuf_tensor("w4s", [128, 2], BF16))
        gw1s = _e(nc.sbuf_tensor("gw1s", [128, 8 * 512], BF16))
        gw2s = _e(nc.sbuf_tensor("gw2s", [128, 4 * 256], BF16))
        gw3s = _e(nc.sbuf_tensor("gw3s", [128, 2], BF16))
        sml_sb = _e(nc.sbuf_tensor("sml_sb", [128, 20], F32))
        gb3s = _e(nc.sbuf_tensor("gb3s", [1, 1], F32))
        gcTs = _e(nc.sbuf_tensor("gcTs", [128, 128], F32))
        gc16 = _e(nc.sbuf_tensor("gc16", [128, 16], F32))
        gc16b = _e(nc.sbuf_tensor("gc16b", [128, 16], BF16))
        idents = _e(nc.sbuf_tensor("idents", [128, 128], BF16))
        sc4 = _e(nc.sbuf_tensor("sc4", [1, 4], F32))
        ones1 = _e(nc.sbuf_tensor("ones1", [1, 128], F32))
        bc = _e(nc.sbuf_tensor("bc", [128, 4], F32))
        Ab = _e(nc.sbuf_tensor("Ab", [128, 2], F32))
        Bc = _e(nc.sbuf_tensor("Bc", [128, 1], F32))
        tAB = _e(nc.sbuf_tensor("tAB", [128, 2], F32))
        g1 = _e(nc.sbuf_tensor("g1", [128, 4], BF16))
        g2 = _e(nc.sbuf_tensor("g2", [128, 2], BF16))
        out0 = _e(nc.sbuf_tensor("out0", [128, 8 * 1024], F32))
        out1 = _e(nc.sbuf_tensor("out1", [128, 8 * 1024], F32))
        outT0 = _e(nc.sbuf_tensor("outT0", [128, 8 * 1024], BF16))
        outT1 = _e(nc.sbuf_tensor("outT1", [128, 8 * 1024], BF16))
        obf = _e(nc.sbuf_tensor("obf", [128, 8 * 1024], BF16))
        h1T = _e(nc.sbuf_tensor("h1T", [128, 8 * 1024], BF16))
        h2T = _e(nc.sbuf_tensor("h2T", [128, 4 * 1024], BF16))
        h3T = _e(nc.sbuf_tensor("h3T", [128, 2 * 1024], BF16))
        x1s = _e(nc.sbuf_tensor("x1s", [128, 4 * 1024], F32))
        tfs = _e(nc.sbuf_tensor("tfs", [128, 2 * 8], F32))
        coup = _e(nc.sbuf_tensor("coup", [128, 2 * 8], F32))
        coupn = _e(nc.sbuf_tensor("coupn", [128, 2 * 8], F32))
        wtmp = _e(nc.sbuf_tensor("wtmp", [128, 1024], F32))
        sq2 = _e(nc.sbuf_tensor("sq2", [128, 2 * 1024], F32))
        mt2 = _e(nc.sbuf_tensor("mt2", [128, 16], F32))
        v2t2 = _e(nc.sbuf_tensor("v2t2", [128, 16], F32))
        vt2 = _e(nc.sbuf_tensor("vt2", [128, 16], F32))
        lt2 = _e(nc.sbuf_tensor("lt2", [128, 16], F32))
        rt2 = _e(nc.sbuf_tensor("rt2", [128, 16], F32))
        nmr2 = _e(nc.sbuf_tensor("nmr2", [128, 16], F32))
        tsm = _e(nc.sbuf_tensor("tsm", [128, 16], F32))
        psH0 = _e(nc.psum_tensor("psH0", [128, 1024], F32))
        psH1 = _e(nc.psum_tensor("psH1", [128, 1024], F32))
        psE0 = _e(nc.psum_tensor("psE0", [128, 512], BF16))
        psE1 = _e(nc.psum_tensor("psE1", [128, 512], BF16))
        psG = _e(nc.psum_tensor("psG", [128, 32], F32))
        g_w1a = _e(nc.semaphore("g_w1a"))
        g_w1b = _e(nc.semaphore("g_w1b"))
        g_sm = _e(nc.semaphore("g_sm"))
        g_gc = _e(nc.semaphore("g_gc"))
        g_w2 = _e(nc.semaphore("g_w2"))
        s_oT0 = _e(nc.semaphore("s_oT0"))
        s_oT1 = _e(nc.semaphore("s_oT1"))
        g_sm2 = _e(nc.semaphore("g_sm2"))
        st_g = _e(nc.semaphore("st_g"))
        s_ob0 = _e(nc.semaphore("s_ob0"))
        s_ob1 = _e(nc.semaphore("s_ob1"))
        s_x10 = _e(nc.semaphore("s_x10"))
        s_x11 = _e(nc.semaphore("s_x11"))
        s_x12 = _e(nc.semaphore("s_x12"))
        s_x13 = _e(nc.semaphore("s_x13"))
        po = _e(nc.semaphore("po"))
        pe = _e(nc.semaphore("pe"))
        acts = _e(nc.semaphore("acts"))
        ve = _e(nc.semaphore("ve"))
        st = _e(nc.semaphore("st"))
        block = _e(nc.Block())
        psH = [psH0, psH1]
        psE = [psE0, psE1]
        outb = [out0, out1]
        outT = [outT0, outT1]

        @block.gpsimd
        def _(g):
            g.dma_start(out=w1s[:, 0:4096].rearrange("p (o c) -> p o c", o=4),
                        in_=w1_in[0:4].rearrange("o p c -> p o c")).then_inc(g_w1a, 16)
            g.dma_start(out=w1s[:, 4096:8192].rearrange("p (o c) -> p o c", o=4),
                        in_=w1_in[4:8].rearrange("o p c -> p o c")).then_inc(g_w1b, 16)
            g.dma_start(out=gw1s[:, :].rearrange("p (hc d) -> p hc d", hc=8),
                        in_=gw1_in.rearrange("hc p d -> p hc d")).then_inc(g_gc, 16)
            g.dma_start(out=gw2s[:, :].rearrange("p (hc d) -> p hc d", hc=4),
                        in_=gw2_in.rearrange("hc p d -> p hc d")).then_inc(g_gc, 16)
            g.dma_start(out=gw3s[:, :], in_=gw3_in).then_inc(g_gc, 16)
            g.dma_start(out=idents[:, :], in_=ident_in).then_inc(g_gc, 16)
            g.dma_start(out=gcTs[:, :], in_=gcT_in).then_inc(g_gc, 16)
            g.dma_start(out=w2s[:, :].rearrange("p (hc d) -> p hc d", hc=8),
                        in_=w2_in.rearrange("hc p d -> p hc d")).then_inc(g_w2, 16)
            g.dma_start(out=w3s[:, :].rearrange("p (hc d) -> p hc d", hc=4),
                        in_=w3_in.rearrange("hc p d -> p hc d")).then_inc(g_w2, 16)
            g.dma_start(out=w4s[:, :], in_=w4_in).then_inc(g_w2, 16)
            for p in range(8):
                osl = outb[0][:, p * 1024:(p + 1) * 1024]
                g.wait_ge(ve, P["ve_twc", 0, 2, p])
                g.wait_ge([s_x10, s_x11, s_x12, s_x13][p % 4], P["sld_x1s", 0, p])
                if p > 0:
                    g.wait_ge(po, P["po_sq", 0, p - 1])
                g.tensor_tensor(osl, osl,
                                x1s[:, (p % 4) * 1024:(p % 4 + 1) * 1024],
                                ALU.add).then_inc(po, 1)
                if p >= 2:
                    g.wait_ge(ve, P["ve_vt", 0, p - 2])
                g.wait_ge(po, P["po_s2add", 0, p])
                g.tensor_tensor(sq2[:, (p % 2) * 1024:(p % 2 + 1) * 1024],
                                osl, osl, ALU.mult).then_inc(po, 1)

        @block.sync
        def _(sy):
            s_x1p = [s_x10, s_x11, s_x12, s_x13]
            sy.dma_start(out=sml_sb[:, :], in_=sml_in).then_inc(g_sm, 16)
            sy.dma_start(out=outT[0][:, :].rearrange("p (fc t) -> p fc t", fc=8),
                         in_=x1T_in[0].rearrange("fc p t -> p fc t")).then_inc(s_oT0, 16)
            sy.dma_start(out=outb[0][:, :].rearrange("ii (p f) -> ii p f", p=8),
                         in_=x1_in[0].rearrange("p ii f -> ii p f")).then_inc(s_ob0, 16)
            sy.dma_start(out=outT[1][:, :].rearrange("p (fc t) -> p fc t", fc=8),
                         in_=x1T_in[1].rearrange("fc p t -> p fc t")).then_inc(s_oT1, 16)
            sy.dma_start(out=outb[1][:, :].rearrange("ii (p f) -> ii p f", p=8),
                         in_=x1_in[1].rearrange("p ii f -> ii p f")).then_inc(s_ob1, 16)
            sy.dma_start(out=sc4[0:1, 0:1], in_=b4_in).then_inc(g_sm2, 16)
            sy.dma_start(out=sc4[0:1, 1:2], in_=ar_in).then_inc(g_sm2, 16)
            sy.dma_start(out=gb3s[0:1, 0:1], in_=gb3_in).then_inc(g_sm2, 16)

            def x1load(idx):
                b, p = divmod(idx, 8)
                if idx >= 4:
                    pb, pp = divmod(idx - 4, 8)
                    if pb == 0:
                        sy.wait_ge(po, P["po_s2add", 0, pp])
                    else:
                        sy.wait_ge(ve, P["ve_s2add1", pp])
                sy.dma_start(out=x1s[:, (idx % 4) * 1024:(idx % 4 + 1) * 1024],
                             in_=x1_in[b, p]).then_inc(s_x1p[idx % 4], 16)

            for idx in range(12):
                x1load(idx)
            sy.wait_ge(ve, P["ve_apply", 0, 7])
            sy.dma_start(out=x2[0].rearrange("p ii f -> ii p f"),
                         in_=outb[0][:, :].rearrange("ii (p f) -> ii p f", p=8)
                         ).then_inc(st, 16)
            for idx in range(12, 16):
                x1load(idx)
            for p in (0, 2, 4, 6):
                sy.wait_ge(ve, P["ve_apply", 1, p])
                sy.dma_start(out=x2[1, p],
                             in_=out1[:, p * 1024:(p + 1) * 1024]).then_inc(st, 16)
            sy.wait_ge(st, 16 * 5)
            sy.wait_ge(st_g, 16 * 4)

        @block.tensor
        def _(t):
            pshi = [0]

            def psh_wait():
                i = pshi[0]
                if i >= 2:
                    t.wait_ge(acts, P["psh_prev"][i - 2])
                pshi[0] = i + 1

            t.wait_ge(g_w1a, 16)
            t.wait_ge(s_oT0, 16)
            for oc in range(8):
                if oc == 4:
                    t.wait_ge(g_w1b, 16)
                psh_wait()
                last = None
                for ic in range(8):
                    for pc in (0, 512):
                        last = t.matmul(
                            psH[oc % 2][:, pc:pc + 512],
                            w1s[:, oc * 1024 + ic * 128:oc * 1024 + ic * 128 + 128],
                            outT[0][:, ic * 1024 + pc:ic * 1024 + pc + 512],
                            start=(ic == 0), stop=(ic == 7))
                last.then_inc(pe, 1)
            t.wait_ge(g_gc, 16 * 5)
            t.wait_ge(ve, P["ve_gc16b"])
            for b in range(B):
                for oc in range(4):
                    last = None
                    for ic in range(8):
                        last = t.matmul(psG[:, oc:oc + 1],
                                        gw1s[:, ic * 512 + oc * 128:ic * 512 + oc * 128 + 128],
                                        gc16b[:, b * 8 + ic:b * 8 + ic + 1],
                                        start=(ic == 0), stop=(ic == 7))
                    last.then_inc(pe, 1)
                for oc in range(2):
                    t.wait_ge(acts, P["acts_g1", b, 3])
                    last = None
                    for ic in range(4):
                        last = t.matmul(psG[:, 4 + oc:5 + oc],
                                        gw2s[:, ic * 256 + oc * 128:ic * 256 + oc * 128 + 128],
                                        g1[:, ic:ic + 1],
                                        start=(ic == 0), stop=(ic == 3))
                    last.then_inc(pe, 1)
                t.wait_ge(acts, P["acts_g2", b, 1])
                last = None
                for icc in range(2):
                    last = t.matmul(psG[0:1, 6 + b:7 + b],
                                    gw3s[:, icc:icc + 1], g2[:, icc:icc + 1],
                                    start=(icc == 0), stop=(icc == 1))
                last.then_inc(pe, 1)
            t.wait_ge(acts, P["acts_gf", 1])
            t.wait_ge(g_sm2, 16 * 3)
            t.wait_ge(ve, P["ve_ones"])
            t.matmul(psG[:, 8:12], ones1[0:1, :], sc4[0:1, :],
                     start=True, stop=True).then_inc(pe, 1)
            t.wait_ge(g_w2, 16 * 3)
            for l in range(3):
                for b in range(B):
                    if l > 0:
                        t.wait_ge(ve, P["ve_obf", b, l])
                        gi = 0
                        for fc in range(8):
                            for pg in range(2):
                                if gi >= 2:
                                    t.wait_ge(ve, P["ve_tpev", b, l, gi - 2])
                                last = None
                                for j in range(4):
                                    p = pg * 4 + j
                                    last = t.transpose(
                                        psE[gi % 2][:, j * 128:(j + 1) * 128],
                                        obf[:, p * 1024 + fc * 128:p * 1024 + fc * 128 + 128],
                                        idents[:, :])
                                last.then_inc(pe, 1)
                                gi += 1
                        t.wait_ge(ve, P["ve_tpev", b, l, 15])
                    else:
                        if b == 1:
                            t.wait_ge(s_oT1, 16)
                    if (b, l) != (0, 0):
                        for oc in range(8):
                            psh_wait()
                            last = None
                            for ic in range(8):
                                for pc in (0, 512):
                                    last = t.matmul(
                                        psH[oc % 2][:, pc:pc + 512],
                                        w1s[:, oc * 1024 + ic * 128:oc * 1024 + ic * 128 + 128],
                                        outT[b][:, ic * 1024 + pc:ic * 1024 + pc + 512],
                                        start=(ic == 0), stop=(ic == 7))
                            last.then_inc(pe, 1)
                    t.wait_ge(acts, P["acts_h1", b, l, 7])
                    for oc in range(4):
                        psh_wait()
                        last = None
                        for ic in range(8):
                            for pc in (0, 512):
                                last = t.matmul(
                                    psH[oc % 2][:, pc:pc + 512],
                                    w2s[:, ic * 512 + oc * 128:ic * 512 + oc * 128 + 128],
                                    h1T[:, ic * 1024 + pc:ic * 1024 + pc + 512],
                                    start=(ic == 0), stop=(ic == 7))
                        last.then_inc(pe, 1)
                    t.wait_ge(acts, P["acts_h2", b, l, 3])
                    for oc in range(2):
                        psh_wait()
                        last = None
                        for ic in range(4):
                            for pc in (0, 512):
                                last = t.matmul(
                                    psH[oc % 2][:, pc:pc + 512],
                                    w3s[:, ic * 256 + oc * 128:ic * 256 + oc * 128 + 128],
                                    h2T[:, ic * 1024 + pc:ic * 1024 + pc + 512],
                                    start=(ic == 0), stop=(ic == 3))
                        last.then_inc(pe, 1)
                    t.wait_ge(acts, P["acts_h3", b, l, 1])
                    if not (b == 0 and l == 0):
                        lb, ll = (0, l) if b == 1 else (1, l - 1)
                        t.wait_ge(acts, P["acts_tf", lb, ll])
                    last = None
                    for p in range(8):
                        for icc in range(2):
                            last = t.matmul(
                                psG[:, 12 + p:13 + p],
                                h3T[:, icc * 1024 + p * 128:icc * 1024 + p * 128 + 128],
                                w4s[:, icc:icc + 1],
                                start=(icc == 0), stop=(icc == 1))
                    last.then_inc(pe, 1)

        @block.scalar
        def _(s):
            sc = Ctr()

            def act(fn_out, fn_in, func, **kw):
                s.wait_ge(acts, sc.v)
                s.activation(fn_out, fn_in, func, **kw).then_inc(acts, 1)
                sc.inc()

            s.wait_ge(g_sm, 16)
            for oc in range(8):
                s.wait_ge(pe, P["pe_h1", 0, 0, oc])
                act(h1T[:, oc * 1024:(oc + 1) * 1024], psH[oc % 2][:, :],
                    AF.Gelu, bias=sml_sb[:, oc:oc + 1])
            for b in range(B):
                for oc in range(4):
                    s.wait_ge(pe, P["pe_g1", b, oc])
                    act(g1[:, oc:oc + 1], psG[:, oc:oc + 1], AF.Gelu,
                        bias=sml_sb[:, 14 + oc:15 + oc])
                for oc in range(2):
                    s.wait_ge(pe, P["pe_g2", b, oc])
                    act(g2[:, oc:oc + 1], psG[:, 4 + oc:5 + oc], AF.Gelu,
                        bias=sml_sb[:, 18 + oc:19 + oc])
            s.wait_ge(g_sm2, 16 * 3)
            for b in range(B):
                s.wait_ge(pe, P["pe_gf", b])
                act(sc4[0:1, 2 + b:3 + b], psG[0:1, 6 + b:7 + b], AF.Sigmoid,
                    bias=gb3s[0:1, 0:1])
            s.wait_ge(ve, P["ve_bc"])
            def hacts(b, l):
                if (b, l) != (0, 0):
                    for oc in range(8):
                        s.wait_ge(pe, P["pe_h1", b, l, oc])
                        act(h1T[:, oc * 1024:(oc + 1) * 1024], psH[oc % 2][:, :],
                            AF.Gelu, bias=sml_sb[:, oc:oc + 1])
                for oc in range(4):
                    s.wait_ge(pe, P["pe_h2", b, l, oc])
                    act(h2T[:, oc * 1024:(oc + 1) * 1024], psH[oc % 2][:, :],
                        AF.Gelu, bias=sml_sb[:, 8 + oc:9 + oc])
                for oc in range(2):
                    s.wait_ge(pe, P["pe_h3", b, l, oc])
                    act(h3T[:, oc * 1024:(oc + 1) * 1024], psH[oc % 2][:, :],
                        AF.Gelu, bias=sml_sb[:, 12 + oc:13 + oc])

            def sigact(b, l):
                s.wait_ge(pe, P["pe_tf", b, l])
                act(tfs[:, b * 8:(b + 1) * 8], psG[:, 12:20], AF.Sigmoid,
                    bias=bc[:, 0:1])

            def rstd2(b):
                s.wait_ge(ve, P["ve_vt2eps", b])
                act(lt2[:, b * 8:(b + 1) * 8], vt2[:, b * 8:(b + 1) * 8], AF.Ln,
                    scale=2.0 ** -20)
                act(rt2[:, b * 8:(b + 1) * 8], lt2[:, b * 8:(b + 1) * 8], AF.Exp,
                    scale=-0.5)

            for l in range(2):
                hacts(0, l); sigact(0, l)
                hacts(1, l); sigact(1, l)
            hacts(0, 2); sigact(0, 2)
            for oc in range(8):
                s.wait_ge(pe, P["pe_h1", 1, 2, oc])
                act(h1T[:, oc * 1024:(oc + 1) * 1024], psH[oc % 2][:, :],
                    AF.Gelu, bias=sml_sb[:, oc:oc + 1])
            rstd2(0)
            for oc in range(4):
                s.wait_ge(pe, P["pe_h2", 1, 2, oc])
                act(h2T[:, oc * 1024:(oc + 1) * 1024], psH[oc % 2][:, :],
                    AF.Gelu, bias=sml_sb[:, 8 + oc:9 + oc])
            for oc in range(2):
                s.wait_ge(pe, P["pe_h3", 1, 2, oc])
                act(h3T[:, oc * 1024:(oc + 1) * 1024], psH[oc % 2][:, :],
                    AF.Gelu, bias=sml_sb[:, 12 + oc:13 + oc])
            sigact(1, 2)
            def rstd1(p):
                s.wait_ge(ve, P["ve_vteps1", p])
                act(lt2[:, 8 + p:9 + p], vt2[:, 8 + p:9 + p], AF.Ln,
                    scale=2.0 ** -20)
                act(rt2[:, 8 + p:9 + p], lt2[:, 8 + p:9 + p], AF.Exp, scale=-0.5)

            for p in range(8):
                rstd1(p)
                if p >= 3 and p % 2 == 1:
                    pw = p - 2
                    s.wait_ge(ve, P["ve_apply", 1, pw])
                    s.dma_start(out=x2[1, pw],
                                in_=out1[:, pw * 1024:(pw + 1) * 1024]).then_inc(st_g, 16)
            s.wait_ge(ve, P["ve_apply", 1, 7])
            s.dma_start(out=x2[1, 7],
                        in_=out1[:, 7 * 1024:8 * 1024]).then_inc(st_g, 16)

        @block.vector
        def _(v):
            vc = Ctr()

            def op(fn, *args, **kw):
                v.wait_ge(ve, vc.v)
                fn(*args, **kw).then_inc(ve, 1)
                vc.inc()

            op(v.memset, ones1[0:1, :], 1.0)
            v.wait_ge(g_gc, 16 * 5)
            op(v.tensor_tensor, gc16[:, :], gcTs[:, 0:16], gcTs[:, 16:32], ALU.add)
            for cr in range(2, 8):
                op(v.tensor_tensor, gc16[:, :], gc16[:, :],
                   gcTs[:, cr * 16:(cr + 1) * 16], ALU.add)
            op(v.tensor_scalar, gc16[:, :], gc16[:, :], 1.0 / 8192.0, None, ALU.mult)
            op(v.tensor_copy, gc16b[:, :], gc16[:, :])
            assert vc.v == P["ve_gc16b"]
            v.wait_ge(pe, P["pe_bc"])
            op(v.tensor_copy, bc[:, :], psG[:, 8:12])
            assert vc.v == P["ve_bc"]
            for b in range(B):
                op(v.tensor_tensor, tAB[:, b:b + 1], bc[:, 1:2], bc[:, 2 + b:3 + b], ALU.mult)
            op(v.tensor_scalar, tAB[:, :], tAB[:, :], 1.4, None, ALU.mult)
            op(v.tensor_scalar, tsm[:, 0:1], bc[:, 1:2], -1.0, 0.1, ALU.mult, ALU.add)
            op(v.tensor_tensor, Ab[:, 0:1], tsm[:, 0:1], tAB[:, 0:1], ALU.add)
            op(v.tensor_tensor, Ab[:, 1:2], tsm[:, 0:1], tAB[:, 1:2], ALU.add)
            op(v.tensor_scalar, Bc[:, 0:1], bc[:, 1:2], 0.6, None, ALU.mult)
            assert vc.v == P["ve_AB"]

            def twist(b, l):
                if l == 0:
                    v.wait_ge([s_ob0, s_ob1][b], 16)
                v.wait_ge(acts, P["acts_tf", b, l])
                op(v.tensor_scalar, coup[:, b * 8:(b + 1) * 8],
                   tfs[:, b * 8:(b + 1) * 8], Bc[:, 0:1], Ab[:, b:b + 1],
                   ALU.mult, ALU.add)
                op(v.tensor_scalar, coupn[:, b * 8:(b + 1) * 8],
                   coup[:, b * 8:(b + 1) * 8], -1.0, None, ALU.mult)
                assert vc.v == P["ve_coup", b, l]
                w4 = wtmp[:, :].rearrange("p (a q) -> p a q", a=4)
                for p in range(8):
                    osl = outb[b][:, p * 1024:(p + 1) * 1024]
                    o4 = osl.rearrange("p (a q) -> p a q", a=4)
                    cp = coup[:, b * 8 + p:b * 8 + p + 1]
                    op(v.tensor_scalar, wtmp[:, :], osl, cp, None, ALU.mult)
                    op(v.scalar_tensor_tensor, o4[:, 0:4:3, :], w4[:, 2:0:-1, :],
                       1.0, o4[:, 0:4:3, :], ALU.mult, ALU.add)
                    op(v.scalar_tensor_tensor, o4[:, 1:3, :], w4[:, 3::-3, :],
                       -1.0, o4[:, 1:3, :], ALU.mult, ALU.add)
                assert vc.v == P["ve_twist", b, l]

            def ln2(b):
                assert b == 0
                bsl = slice(0, 8)
                for p in range(8):
                    col = slice(p, p + 1)
                    osl = outb[0][:, p * 1024:(p + 1) * 1024]
                    v.wait_ge(po, P["po_s2add", 0, p])
                    op(v.reduce_sum, mt2[:, col], osl, AX.X)
                    v.wait_ge(po, P["po_sq", 0, p])
                    op(v.reduce_sum, v2t2[:, col],
                       sq2[:, (p % 2) * 1024:(p % 2 + 1) * 1024], AX.X)
                    op(v.tensor_scalar, tsm[:, col], mt2[:, col], mt2[:, col],
                       RAWEPS, ALU.mult, ALU.subtract)
                    op(v.scalar_tensor_tensor, vt2[:, col], v2t2[:, col],
                       1024.0, tsm[:, col], ALU.mult, ALU.subtract)
                    assert vc.v == P["ve_vt", 0, p]
                assert vc.v == P["ve_vt2eps", 0]
                v.wait_ge(acts, P["acts_rstd2", 0])
                op(v.tensor_tensor, nmr2[:, bsl], mt2[:, bsl], rt2[:, bsl], ALU.mult)
                op(v.tensor_scalar, nmr2[:, bsl], nmr2[:, bsl], -1.0 / 1024.0, None, ALU.mult)
                assert vc.v == P["ve_nmrb", 0]
                for p in range(8):
                    col = slice(p, p + 1)
                    osl = outb[0][:, p * 1024:(p + 1) * 1024]
                    op(v.tensor_scalar, osl, osl, rt2[:, col], nmr2[:, col],
                       ALU.mult, ALU.add)
                    assert vc.v == P["ve_apply", 0, p]

            def ln2b1_head(p):
                idx = 8 + p
                col = slice(idx, idx + 1)
                osl = outb[1][:, p * 1024:(p + 1) * 1024]
                idx4 = [s_x10, s_x11, s_x12, s_x13][idx % 4]
                v.wait_ge(idx4, P["sld_x1s", 1, p])
                op(v.scalar_tensor_tensor, osl,
                   x1s[:, (idx % 4) * 1024:(idx % 4 + 1) * 1024], 1.0, osl,
                   ALU.mult, ALU.add, accum_out=mt2[:, col])
                assert vc.v == P["ve_s2add1", p]

            def ln2b1_var(p):
                idx = 8 + p
                col = slice(idx, idx + 1)
                osl = outb[1][:, p * 1024:(p + 1) * 1024]
                op(v.scalar_tensor_tensor, wtmp[:, :], osl, 1.0, osl,
                   ALU.mult, ALU.mult, accum_out=v2t2[:, col])
                op(v.tensor_scalar, tsm[:, col], mt2[:, col], mt2[:, col],
                   RAWEPS, ALU.mult, ALU.subtract)
                op(v.scalar_tensor_tensor, vt2[:, col], v2t2[:, col],
                   1024.0, tsm[:, col], ALU.mult, ALU.subtract)
                assert vc.v == P["ve_vt", 1, p]
                assert vc.v == P["ve_vteps1", p]

            def ln2b1_apply(p):
                idx = 8 + p
                col = slice(idx, idx + 1)
                osl = outb[1][:, p * 1024:(p + 1) * 1024]
                v.wait_ge(acts, P["acts_rstd1", p])
                op(v.tensor_scalar, nmr2[:, col], mt2[:, col], rt2[:, col],
                   -1.0 / 1024.0, ALU.mult, ALU.mult)
                op(v.tensor_scalar, osl, osl, rt2[:, col], nmr2[:, col],
                   ALU.mult, ALU.add)
                assert vc.v == P["ve_apply", 1, p]

            def ln2b1():
                for p in range(8):
                    ln2b1_head(p)
                    if p >= 1:
                        ln2b1_var(p - 1)
                    if p >= 2:
                        ln2b1_apply(p - 2)
                ln2b1_var(7)
                ln2b1_apply(6)
                ln2b1_apply(7)

            def ce(b, l):
                if (b, l) != (0, 1):
                    ob, ol = (1, l - 1) if b == 0 else (0, l)
                    v.wait_ge(pe, P["pe_tp", ob, ol, 15])
                op(v.tensor_copy, obf[:, :], outb[b][:, :])
                assert vc.v == P["ve_obf", b, l]
                for g in range(16):
                    v.wait_ge(pe, P["pe_tp", b, l, g])
                    fc, pg = divmod(g, 2)
                    op(v.tensor_copy,
                       outT[b][:, fc * 1024 + pg * 512:fc * 1024 + pg * 512 + 512],
                       psE[g % 2][:, :])
                    assert vc.v == P["ve_tpev", b, l, g]

            twist(0, 0)
            ce(0, 1)
            twist(1, 0)
            ce(1, 1)
            twist(0, 1)
            ce(0, 2)
            twist(1, 1)
            ce(1, 2)
            twist(0, 2)
            ln2(0)
            twist(1, 2)
            ln2b1()
    return nc


def l3_host_inputs(x1_full, x1bf, gcp, W):
    import ml_dtypes
    bf = ml_dtypes.bfloat16
    gcT = np.zeros((128, 128), dtype=np.float32)
    for cr in range(NC):
        for b in range(B):
            gcT[:, cr * 16 + b * 8:cr * 16 + (b + 1) * 8] = (
                gcp[cr, b].reshape(8, 128).T)
    common = {
        "gcT_in": gcT,
        "w1_in": np.ascontiguousarray(
            W["cn_w1"].reshape(8, 128, 8, 128).transpose(2, 1, 0, 3)
        ).reshape(8, 128, 1024),
        "w2_in": W["cn_w2"].reshape(8, 128, 512),
        "w3_in": W["cn_w3"].reshape(4, 128, 256),
        "w4_in": np.ascontiguousarray(W["cn_w4"].reshape(2, 128).T),
        "gw1_in": W["gc_w1"].reshape(8, 128, 512),
        "gw2_in": W["gc_w2"].reshape(4, 128, 256),
        "gw3_in": np.ascontiguousarray(W["gc_w3"].reshape(2, 128).T),
        "sml_in": np.concatenate(
            [W["cn_b1"].reshape(8, 128).T, W["cn_b2"].reshape(4, 128).T,
             W["cn_b3"].reshape(2, 128).T, W["gc_b1"].reshape(4, 128).T,
             W["gc_b2"].reshape(2, 128).T], axis=1),
        "b4_in": W["cn_b4"].reshape(1, 1),
        "gb3_in": W["gc_b3"].reshape(1, 1),
        "ar_in": W["adaptive_range"].reshape(1, 1),
        "ident_in": np.eye(128, dtype=bf),
    }
    common = {k: np.ascontiguousarray(v, dtype=v.dtype) for k, v in common.items()}
    maps = []
    for c in range(NC):
        rows = np.concatenate([np.arange(t * 128, (t + 1) * 128)
                               for t in snake_tiles(c)])
        x1c = np.ascontiguousarray(x1_full[:, rows, :]).reshape(B, 8, 128, 1024)
        x1Tc = np.ascontiguousarray(
            x1bf[:, rows, :].transpose(0, 2, 1)).reshape(B, 8, 128, 1024)
        maps.append({"x1_in": x1c, "x1T_in": x1Tc, **common})
    return maps


def l3_host_outputs(res):
    out = np.zeros((B, S, D), dtype=np.float32)
    for c in range(NC):
        r = res.results[c]
        for pi, t in enumerate(snake_tiles(c)):
            out[:, t * 128:(t + 1) * 128, :] = r["x2"][:, pi]
    return out



from concourse.bass_utils import run_bass_kernel_spmd

_NC_CACHE = {}


def _get(name, builder):
    if name not in _NC_CACHE:
        _NC_CACHE[name] = builder()
    return _NC_CACHE[name]


def kernel(L0_attn_x, L0_attn_Wq, L0_attn_Wk, L0_attn_Wv,
           L0_ffn_cn_w1, L0_ffn_cn_b1, L0_ffn_cn_w2, L0_ffn_cn_b2,
           L0_ffn_cn_w3, L0_ffn_cn_b3, L0_ffn_cn_w4, L0_ffn_cn_b4,
           L0_ffn_gc_w1, L0_ffn_gc_b1, L0_ffn_gc_w2, L0_ffn_gc_b2,
           L0_ffn_gc_w3, L0_ffn_gc_b3, L0_ffn_adaptive_range):
    x = np.asarray(L0_attn_x, dtype=np.float32)
    cores = list(range(NC))

    nc1 = _get("l1", build_l1)
    res1 = run_bass_kernel_spmd(
        nc1, l1_host_inputs(x, np.asarray(L0_attn_Wq), np.asarray(L0_attn_Wk),
                            np.asarray(L0_attn_Wv)), core_ids=cores)
    qT, kT, vT, usum = l1_host_outputs(res1)

    nc2 = _get("l2", build_l2)
    res2 = run_bass_kernel_spmd(nc2, l2_host_inputs(x, qT, kT, vT, usum),
                                core_ids=cores)
    x1, x1bf, gcp = l2_host_outputs(res2)

    W = {
        "cn_w1": np.asarray(L0_ffn_cn_w1), "cn_b1": np.asarray(L0_ffn_cn_b1),
        "cn_w2": np.asarray(L0_ffn_cn_w2), "cn_b2": np.asarray(L0_ffn_cn_b2),
        "cn_w3": np.asarray(L0_ffn_cn_w3), "cn_b3": np.asarray(L0_ffn_cn_b3),
        "cn_w4": np.asarray(L0_ffn_cn_w4), "cn_b4": np.asarray(L0_ffn_cn_b4),
        "gc_w1": np.asarray(L0_ffn_gc_w1), "gc_b1": np.asarray(L0_ffn_gc_b1),
        "gc_w2": np.asarray(L0_ffn_gc_w2), "gc_b2": np.asarray(L0_ffn_gc_b2),
        "gc_w3": np.asarray(L0_ffn_gc_w3), "gc_b3": np.asarray(L0_ffn_gc_b3),
        "adaptive_range": np.asarray(L0_ffn_adaptive_range),
    }
    nc3 = _get("l3", build_l3)
    res3 = run_bass_kernel_spmd(nc3, l3_host_inputs(x1, x1bf, gcp, W), core_ids=cores)
    return l3_host_outputs(res3)


# version 21
# speedup vs baseline: 1.0394x; 1.0113x over previous
import numpy as np
import concourse.bass as bass
import concourse.mybir as mybir

F32 = mybir.dt.float32
BF16 = mybir.dt.bfloat16
AF = mybir.ActivationFunctionType
ALU = mybir.AluOpType
AX = mybir.AxisListType

B = 2
S = 8192
D = 1024
NC = 8
TOK = 1024


class Ctr:
    def __init__(self):
        self.v = 0
    def inc(self, n=1):
        self.v += n
        return self.v



def build_l1():
    nc = bass.Bass("TRN2", target_bir_lowering=False)
    xT = nc.dram_tensor("xT", [B, 8, 128, TOK], F32, kind="ExternalInput").ap()
    wqT = nc.dram_tensor("wqT", [8, 128, D], F32, kind="ExternalInput").ap()
    wkT = nc.dram_tensor("wkT", [8, 128, D], F32, kind="ExternalInput").ap()
    wvT = nc.dram_tensor("wvT", [8, 128, D], F32, kind="ExternalInput").ap()
    qT = nc.dram_tensor("qT", [B, 8, 128, TOK], BF16, kind="ExternalOutput").ap()
    kT = nc.dram_tensor("kT", [B, 8, 128, TOK], BF16, kind="ExternalOutput").ap()
    vT = nc.dram_tensor("vT", [B, 8, 128, TOK], BF16, kind="ExternalOutput").ap()
    usum = nc.dram_tensor("usum", [128, 16], BF16, kind="ExternalOutput").ap()

    projs = [("k", wkT, kT), ("v", wvT, vT), ("q", wqT, wqT)]

    AC = {"wk0": 1, "x00": 2, "wk1": 3, "x10": 4, "x20": 5, "x30": 6,
          "wvc": 7, "wqc": 8}
    pe_c = Ctr()
    PE_AT = {}
    for pb in range(6):
        for th in range(2):
            for dc in range(8):
                pe_c.inc(); PE_AT[(pb, th, dc)] = pe_c.v
    ve_c = Ctr()
    EV = {}
    for pb in range(6):
        for th in range(2):
            for dc in range(8):
                ve_c.inc(); EV[(pb, th, dc)] = ve_c.v
                if pb in (2, 3) and th == 1:
                    ve_c.inc()
    ve_c.inc()
    VE_TOTAL = ve_c.v

    from contextlib import ExitStack
    with ExitStack() as _ctx:
        def _e(t):
            return _ctx.enter_context(t)
        x_sb = _e(nc.sbuf_tensor("x_sb", [128, B * 8 * TOK], BF16))
        x32 = _e(nc.sbuf_tensor("x32", [128, 8 * TOK], F32))
        w32a = _e(nc.sbuf_tensor("w32a", [128, 8 * D], F32))
        w32b = _e(nc.sbuf_tensor("w32b", [128, 8 * D], F32))
        w_sb0 = _e(nc.sbuf_tensor("w_sb0", [128, 8 * D], BF16))
        w_sb1 = _e(nc.sbuf_tensor("w_sb1", [128, 8 * D], BF16))
        st0 = _e(nc.sbuf_tensor("st0", [128, 8 * TOK], BF16))
        st1 = _e(nc.sbuf_tensor("st1", [128, 8 * TOK], BF16))
        us_sb = _e(nc.sbuf_tensor("us_sb", [128, 16], F32))
        usb_sb = _e(nc.sbuf_tensor("usb_sb", [128, 16], BF16))
        psb = [_e(nc.psum_tensor(f"psb{i}", [128, 512], F32)) for i in range(8)]
        g_x = [_e(nc.semaphore(f"g_x{i}")) for i in range(3)]
        s_wk0 = _e(nc.semaphore("s_wk0"))
        s_wk1 = _e(nc.semaphore("s_wk1"))
        s_wq = _e(nc.semaphore("s_wq"))
        s_w1 = _e(nc.semaphore("s_w1"))
        a_x = [_e(nc.semaphore(f"a_x{i}")) for i in range(2)]
        pe = _e(nc.semaphore("pe"))
        acts = _e(nc.semaphore("acts"))
        ev = _e(nc.semaphore("ev"))
        stA = _e(nc.semaphore("stA"))
        stB = _e(nc.semaphore("stB"))
        block = _e(nc.Block())
        w_sb = [w_sb0, w_sb1]
        stg = [st0, st1]
        stp = [stA, stB]
        XSLOT = {0: 0, 1: 1, 2: 0, 3: 1}
        XSEM = {0: a_x[0], 1: g_x[1], 2: a_x[1], 3: g_x[0]}
        XCNT = {0: 16, 1: 16, 2: 16, 3: 16}

        def xsrc(i):
            b, half = divmod(i, 2)
            return xT[b, half * 4:(half + 1) * 4].rearrange("hc p t -> p hc t")

        def xdst(i):
            s = XSLOT[i]
            return x32[:, s * 4 * TOK:(s + 1) * 4 * TOK].rearrange(
                "p (hc t) -> p hc t", hc=4)

        @block.gpsimd
        def _(g):
            g.dma_start(out=xdst(1), in_=xsrc(1)).then_inc(XSEM[1], 16)
            g.wait_ge(acts, AC["x10"])
            g.dma_start(out=xdst(3), in_=xsrc(3)).then_inc(XSEM[3], 16)

        @block.sync
        def _(sy):
            w32a3 = w32a[:, :].rearrange("p (hc d) -> p hc d", hc=8)
            w32b3 = w32b[:, :].rearrange("p (hc d) -> p hc d", hc=8)
            sy.dma_start(out=w32a[:, 0:4 * D].rearrange("p (hc d) -> p hc d", hc=4),
                         in_=wkT[0:4].rearrange("hc p d -> p hc d")).then_inc(s_wk0, 16)
            sy.dma_start(out=w32a[:, 4 * D:8 * D].rearrange("p (hc d) -> p hc d", hc=4),
                         in_=wkT[4:8].rearrange("hc p d -> p hc d")).then_inc(s_wk1, 16)
            sy.dma_start(out=w32b3,
                         in_=wvT.rearrange("hc p d -> p hc d")).then_inc(s_w1, 16)
            sy.wait_ge(acts, AC["wk1"])
            sy.dma_start(out=w32a3,
                         in_=wqT.rearrange("hc p d -> p hc d")).then_inc(s_wq, 16)
            npar = [0, 0]
            for pb in range(6):
                i, b = divmod(pb, 2)
                out = [kT, vT, qT][i]
                for h in range(4):
                    sy.wait_ge(ev, EV[(pb, 1, 2 * h + 1)])
                    sy.dma_start(out=out[b, 2 * h:2 * h + 2].rearrange("dc p t -> p dc t"),
                                 in_=stg[pb % 2][:, 2 * h * TOK:(2 * h + 2) * TOK].rearrange(
                                     "p (dc t) -> p dc t", dc=2)).then_inc(stp[pb % 2], 16)
                    npar[pb % 2] += 1
            sy.wait_ge(ev, VE_TOTAL)
            sy.dma_start(out=usum, in_=usb_sb[:, :]).then_inc(stA, 16)
            npar[0] += 1
            sy.wait_ge(stA, 16 * npar[0])
            sy.wait_ge(stB, 16 * npar[1])

        @block.scalar
        def _(s):
            sc = Ctr()

            def cp(dst, srcap):
                s.wait_ge(acts, sc.v)
                s.copy(dst, srcap).then_inc(acts, 1)
                sc.inc()

            def xcast(i):
                b, half = divmod(i, 2)
                sl = XSLOT[i]
                s.wait_ge(XSEM[i], XCNT[i])
                cp(x_sb[:, (b * 8 + half * 4) * TOK:(b * 8 + half * 4 + 4) * TOK],
                   x32[:, sl * 4 * TOK:(sl + 1) * 4 * TOK])

            s.dma_start(out=xdst(0), in_=xsrc(0)).then_inc(XSEM[0], 16)
            s.wait_ge(s_wk0, 16)
            cp(w_sb0[:, 0:4 * D], w32a[:, 0:4 * D])
            assert sc.v == AC["wk0"]
            xcast(0)
            assert sc.v == AC["x00"]
            s.wait_ge(acts, AC["x00"])
            s.dma_start(out=xdst(2), in_=xsrc(2)).then_inc(XSEM[2], 16)
            s.wait_ge(s_wk1, 16)
            cp(w_sb0[:, 4 * D:8 * D], w32a[:, 4 * D:8 * D])
            assert sc.v == AC["wk1"]
            xcast(1)
            xcast(2)
            xcast(3)
            assert sc.v == AC["x30"]
            s.wait_ge(s_w1, 16)
            cp(w_sb1[:, :], w32b[:, :])
            assert sc.v == AC["wvc"]
            s.wait_ge(pe, PE_AT[(1, 1, 7)])
            s.wait_ge(s_wq, 16)
            cp(w_sb0[:, :], w32a[:, :])
            assert sc.v == AC["wqc"]

        @block.vector
        def _(v):
            vc = Ctr()

            def op(fn, *args, **kw):
                v.wait_ge(ev, vc.v)
                fn(*args, **kw).then_inc(ev, 1)
                vc.inc()

            for pb in range(6):
                i, b = divmod(pb, 2)
                if pb >= 2:
                    v.wait_ge(stp[pb % 2], 64 * (pb // 2))
                for th in range(2):
                    for dc in range(8):
                        v.wait_ge(pe, PE_AT[(pb, th, dc)])
                        sgn = -1.0 if (i == 2 and b == 1) else 1.0
                        op(v.tensor_scalar,
                           stg[pb % 2][:, dc * TOK + th * 512:dc * TOK + th * 512 + 512],
                           psb[dc][:, :], sgn, None, ALU.mult)
                        assert vc.v == EV[(pb, th, dc)]
                        if i == 1 and th == 1:
                            op(v.reduce_sum, us_sb[:, b * 8 + dc:b * 8 + dc + 1],
                               stg[pb % 2][:, dc * TOK:(dc + 1) * TOK], AX.X)
            op(v.tensor_copy, usb_sb[:, :], us_sb[:, :])
            assert vc.v == VE_TOTAL

        @block.tensor
        def _(t):
            for pb in range(6):
                i, b = divmod(pb, 2)
                w_ok = [AC["wk0"], AC["wk1"]] if i == 0 else (
                    [AC["wvc"]] * 2 if i == 1 else [AC["wqc"]] * 2)
                x_ok = [[AC["x00"], AC["x10"]], [AC["x20"], AC["x30"]]][b]
                for th in range(2):
                    for ph in range(2):
                        t.wait_ge(acts, max(w_ok[ph], x_ok[ph]))
                        for dc in range(8):
                            if ph == 0:
                                if (pb, th) != (0, 0):
                                    ppb, pth = (pb, 0) if th == 1 else (pb - 1, 1)
                                    t.wait_ge(ev, EV[(ppb, pth, dc)])
                            last = None
                            for hc in range(4 * ph, 4 * ph + 4):
                                last = t.matmul(
                                    psb[dc][:, :],
                                    w_sb[i % 2][:, hc * D + dc * 128:hc * D + dc * 128 + 128],
                                    x_sb[:, (b * 8 + hc) * TOK + th * 512:
                                         (b * 8 + hc) * TOK + th * 512 + 512],
                                    start=(hc == 0), stop=(hc == 7),
                                )
                            if ph == 1:
                                last.then_inc(pe, 1)
    return nc



def snake_tiles(c):
    return [c, 15 - c, 16 + c, 31 - c, 32 + c, 47 - c, 48 + c, 63 - c]


def l1_host_inputs(x, Wq, Wk, Wv):
    maps = []
    wqT = np.ascontiguousarray(Wq.T).reshape(8, 128, D)
    wkT = np.ascontiguousarray(Wk.T).reshape(8, 128, D)
    wvT = np.ascontiguousarray(Wv.T).reshape(8, 128, D)
    for c in range(NC):
        xs = x[:, c * TOK:(c + 1) * TOK, :]
        xT = np.ascontiguousarray(xs.transpose(0, 2, 1)).reshape(B, 8, 128, TOK)
        maps.append({"xT": xT, "wqT": wqT, "wkT": wkT, "wvT": wvT})
    return maps


def l1_host_outputs(res):
    import ml_dtypes
    qT = np.zeros((B, D, S), dtype=ml_dtypes.bfloat16)
    kT = np.zeros((B, D, S), dtype=ml_dtypes.bfloat16)
    vT = np.zeros((B, D, S), dtype=ml_dtypes.bfloat16)
    usum = np.zeros((B, NC, D), dtype=ml_dtypes.bfloat16)
    for c in range(NC):
        r = res.results[c]
        sl = slice(c * TOK, (c + 1) * TOK)
        qT[:, :, sl] = r["qT"].reshape(B, D, TOK)
        kT[:, :, sl] = r["kT"].reshape(B, D, TOK)
        vT[:, :, sl] = r["vT"].reshape(B, D, TOK)
        u = r["usum"]
        for b in range(B):
            usum[b, c] = u[:, b * 8:(b + 1) * 8].T.reshape(D)
    return qT, kT, vT, usum



def _pieces(w):
    out = []
    o = 0
    while o < w:
        c = min(512, w - o)
        out.append((o, c))
        o += c
    return out


def plan_l2():
    pe = Ctr(); acts = Ctr(); ve = Ctr(); st = Ctr()
    P = dict(pe_S={}, pe_V={}, pe_gc={}, acts_sig={}, acts_sq={}, acts_rstd={},
             acts_b16={}, ve_ready={}, ve_ready0={}, ve_m0={}, ve_m1={},
             ve_evac={}, ve_xadd={}, ve_vt={},
             ve_vteps={}, ve_nmr={}, ve_apply={}, ve_b16={}, ve_gcev={},
             sld_K={}, sld_V={}, gld_x={}, st_x={}, st_b16={}, st_gc={})

    xc = [Ctr(), Ctr()]
    for p in range(8):
        for b in range(B):
            idx = p * 2 + b
            xc[idx % 2].inc(16)
            P["gld_x"][(b, p)] = xc[idx % 2].v

    for a in range(8):
        for rc in range(8):
            P["sld_K"][(a, rc)] = 16 * (a + 1)
        P["sld_V"][(a, 0)] = 16 * a
        P["sld_V"][(a, 1)] = 16 * a

    ve.inc(); P["ve_ones"] = ve.v

    for a in range(8):
        w = (8 - a) * 128
        for rc in range(8):
            pe.inc(); P["pe_S"][(a, rc)] = pe.v
        for rc in range(8):
            acts.inc(); P["acts_sig"][(a, rc)] = acts.v
        acts.inc(2); P["acts_rstd"][a] = acts.v
        for rc in range(8):
            ve.inc(2)
            P["ve_m0"][(a, rc)] = ve.v
            ve.inc(2)
            if w > 128:
                ve.inc()
            P["ve_m1"][(a, rc)] = ve.v
        P["ve_ready0"][a] = P["ve_m0"][(a, 7)]
        P["ve_ready"][a] = ve.v
        for b in range(B):
            for p in range(a, 8):
                pe.inc(); P["pe_V"][(a, b, p)] = pe.v
                ve.inc(); P["ve_evac"][(a, b, p)] = ve.v
                if p == a:
                    ve.inc(); P["ve_xadd"][(b, p)] = ve.v
                    ve.inc()
                    ve.inc(2)
                    P["ve_vt"][(b, p)] = ve.v
                    if b == 1:
                        P["ve_vteps"][a] = ve.v
        for b in range(B):
            ve.inc(); P["ve_nmr"][(b, a)] = ve.v
            ve.inc(); P["ve_apply"][(b, a)] = ve.v
    for p in range(8):
        ve.inc(); P["ve_b16"][(1, p)] = ve.v
    for p in range(8):
        acts.inc(); P["acts_b16"][(0, p)] = acts.v
    for b in range(B):
        pe.inc(); P["pe_gc"][b] = pe.v
        ve.inc(); P["ve_gcev"][b] = ve.v
    for a in range(8):
        for b in range(B):
            st.inc(16); P["st_x"][(b, a)] = st.v
    for b in range(B):
        st.inc(16); P["st_b16"][b] = st.v
    for b in range(B):
        st.inc(16); P["st_gc"][b] = st.v
    P["st_total"] = st.v
    P["sig_seq"] = [P["acts_sig"][(a, rc)] for a in range(8) for rc in range(8)]
    return P


def build_l2():
    nc = bass.Bass("TRN2", target_bir_lowering=False)
    qT_in = nc.dram_tensor("qT_in", [B, 8, 128, 1024], BF16, kind="ExternalInput").ap()
    K_in = nc.dram_tensor("K_in", [8, 8, B, 128, 8, 128], BF16, kind="ExternalInput").ap()
    V_in = nc.dram_tensor("V_in", [B, 8, 8, 128, 1024], BF16, kind="ExternalInput").ap()
    us_in = nc.dram_tensor("us_in", [8, B * 1024], BF16, kind="ExternalInput").ap()
    c19_in = nc.dram_tensor("c19_in", [8, 1024], BF16, kind="ExternalInput").ap()
    m1_in = nc.dram_tensor("m1_in", [128, 2048], BF16, kind="ExternalInput").ap()
    m2_in = nc.dram_tensor("m2_in", [128, 2048], BF16, kind="ExternalInput").ap()

    xq_in = nc.dram_tensor("xq_in", [B, 8, 128, 1024], F32, kind="ExternalInput").ap()
    x1 = nc.dram_tensor("x1", [B, 8, 128, 1024], F32, kind="ExternalOutput").ap()
    x1b = nc.dram_tensor("x1b", [B, 8, 128, 1024], BF16, kind="ExternalOutput").ap()
    gcp = nc.dram_tensor("gcp", [B, 1024], F32, kind="ExternalOutput").ap()

    P = plan_l2()
    EPS = 1e-5
    RAWEPS = EPS * 1024.0 * 1024.0

    from contextlib import ExitStack
    with ExitStack() as _ctx:
        def _e(x):
            return _ctx.enter_context(x)
        qT_sb = _e(nc.sbuf_tensor("qT_sb", [128, B * 8 * 1024], BF16))
        K_rg = _e(nc.sbuf_tensor("K_rg", [128, 10 * 1024], BF16))
        V_sb = _e(nc.sbuf_tensor("V_sb", [128, 2 * 8 * 1024], BF16))
        at0 = _e(nc.sbuf_tensor("at0", [128, 8 * 1024], BF16))
        at1 = _e(nc.sbuf_tensor("at1", [128, 8 * 1024], BF16))
        acc = _e(nc.sbuf_tensor("acc", [128, B * 8 * 1024], F32))
        m1m = _e(nc.sbuf_tensor("m1m", [128, 2048], BF16))
        m2m = _e(nc.sbuf_tensor("m2m", [128, 2048], BF16))

        c19_sb = _e(nc.sbuf_tensor("c19_sb", [8, 1024], BF16))
        us_sb = _e(nc.sbuf_tensor("us_sb", [8, B * 1024], BF16))
        xres = _e(nc.sbuf_tensor("xres", [128, 2 * 1024], F32))
        sq = _e(nc.sbuf_tensor("sq", [128, 1024], F32))
        mt = _e(nc.sbuf_tensor("mt", [128, 16], F32))
        v2t = _e(nc.sbuf_tensor("v2t", [128, 16], F32))
        vt = _e(nc.sbuf_tensor("vt", [128, 16], F32))
        lt = _e(nc.sbuf_tensor("lt", [128, 16], F32))
        rt = _e(nc.sbuf_tensor("rt", [128, 16], F32))
        nmr = _e(nc.sbuf_tensor("nmr", [128, 16], F32))
        tmp1 = _e(nc.sbuf_tensor("tmp1", [128, 16], F32))
        ones = _e(nc.sbuf_tensor("ones", [128, 1], F32))
        psS0 = _e(nc.psum_tensor("psS0", [128, 1024], F32))
        psS1 = _e(nc.psum_tensor("psS1", [128, 1024], F32))
        psO0 = _e(nc.psum_tensor("psO0", [128, 1024], F32))
        psO1 = _e(nc.psum_tensor("psO1", [128, 1024], F32))
        s_Krc = [_e(nc.semaphore(f"s_K{i}")) for i in range(8)]
        s_V0 = _e(nc.semaphore("s_V0"))
        s_V1 = _e(nc.semaphore("s_V1"))
        g_V00 = _e(nc.semaphore("g_V00"))
        g_V01 = _e(nc.semaphore("g_V01"))
        g_qT0 = _e(nc.semaphore("g_qT0"))
        g_qT1 = _e(nc.semaphore("g_qT1"))
        g_cst = _e(nc.semaphore("g_cst"))
        g_x0 = _e(nc.semaphore("g_x0"))
        g_x1 = _e(nc.semaphore("g_x1"))
        pe = _e(nc.semaphore("pe"))
        acts = _e(nc.semaphore("acts"))
        ve = _e(nc.semaphore("ve"))
        st = _e(nc.semaphore("st"))
        block = _e(nc.Block())
        psS = [psS0, psS1]
        psO = [psO0, psO1]

        @block.gpsimd
        def _(g):
            g_V0p = [g_V00, g_V01]
            g_xp = [g_x0, g_x1]

            def kload(a, rc):
                grp = a * 8 + rc
                if grp >= 5:
                    pa, prc = divmod(grp - 5, 8)
                    g.wait_ge(pe, P["pe_S"][(pa, prc)])
                g.dma_start(
                    out=K_rg[:, (grp % 5) * 2048:(grp % 5 + 1) * 2048].rearrange(
                        "p (b ic rr) -> p b ic rr", b=2, ic=8),
                    in_=K_in[a, rc].rearrange("b p ic rr -> p b ic rr"),
                ).then_inc(s_Krc[rc], 16)

            def xload(p, b):
                idx = p * 2 + b
                if idx >= 2:
                    pb, bb = divmod(idx - 2, 2)
                    g.wait_ge(ve, P["ve_xadd"][(bb, pb)])
                g.dma_start(out=xres[:, (idx % 2) * 1024:(idx % 2 + 1) * 1024],
                            in_=xq_in[b, p]).then_inc(g_xp[idx % 2], 16)

            g.dma_start(
                out=V_sb[:, 0:8192].rearrange("p (jc f) -> p jc f", jc=8),
                in_=V_in[0, 0].rearrange("jc p f -> p jc f")).then_inc(g_V0p[0], 16)
            for rc in (1, 3, 5, 7):
                kload(0, rc)
            xload(0, 0)
            xload(0, 1)
            g.dma_start(
                out=V_sb[:, 8192:16384].rearrange("p (jc f) -> p jc f", jc=8),
                in_=V_in[1, 0].rearrange("jc p f -> p jc f")).then_inc(g_V0p[1], 16)
            for a in range(1, 8):
                for rc in (1, 3, 5, 7):
                    kload(a, rc)
                xload(a, 0)
                xload(a, 1)

        @block.sync
        def _(sy):
            s_Vp = [s_V0, s_V1]
            sy.dma_start(out=m1m[:, :], in_=m1_in).then_inc(g_cst, 16)
            sy.dma_start(out=m2m[:, :], in_=m2_in).then_inc(g_cst, 16)
            sy.dma_start(out=us_sb[:, :], in_=us_in).then_inc(g_cst, 16)
            sy.dma_start(out=c19_sb[:, :], in_=c19_in).then_inc(g_cst, 16)
            for a in range(8):
                for rc in (0, 2, 4, 6):
                    grp = a * 8 + rc
                    if grp >= 5:
                        pa, prc = divmod(grp - 5, 8)
                        sy.wait_ge(pe, P["pe_S"][(pa, prc)])
                    sy.dma_start(
                        out=K_rg[:, (grp % 5) * 2048:(grp % 5 + 1) * 2048].rearrange(
                            "p (b ic rr) -> p b ic rr", b=2, ic=8),
                        in_=K_in[a, rc].rearrange("b p ic rr -> p b ic rr"),
                    ).then_inc(s_Krc[rc], 16)
                if a >= 1:
                    for b in range(B):
                        sy.wait_ge(pe, P["pe_V"][(a - 1, b, 7)])
                        sy.dma_start(
                            out=V_sb[:, b * 8192:(b + 1) * 8192].rearrange(
                                "p (jc f) -> p jc f", jc=8),
                            in_=V_in[b, a].rearrange("jc p f -> p jc f")
                        ).then_inc(s_Vp[b], 16)
            sy.wait_ge(st, P["st_total"])

        @block.tensor
        def _(t):
            t.wait_ge(g_qT0, 16)
            t.wait_ge(g_cst, 16 * 4)
            s_Vp = [s_V0, s_V1]
            vgrp = 0
            for a in range(8):
                w = (8 - a) * 128
                for rc in range(8):
                    gidx = a * 8 + rc
                    if gidx >= 2:
                        t.wait_ge(acts, P["sig_seq"][gidx - 2])
                    t.wait_ge(s_Krc[rc], P["sld_K"][(a, rc)])
                    grp = a * 8 + rc
                    p2 = psS[rc % 2]
                    last = None
                    for b in range(B):
                        if gidx == 0 and b == 1:
                            t.wait_ge(g_qT1, 16)
                        for (off, pw) in _pieces(w):
                            for ic in range(8):
                                last = t.matmul(
                                    p2[:, off:off + pw],
                                    K_rg[:, (grp % 5) * 2048 + b * 1024 + ic * 128:
                                         (grp % 5) * 2048 + b * 1024 + ic * 128 + 128],
                                    qT_sb[:, (b * 8 + ic) * 1024 + a * 128 + off:
                                          (b * 8 + ic) * 1024 + a * 128 + off + pw],
                                    start=(b == 0 and ic == 0),
                                    stop=(b == 1 and ic == 7),
                                )
                    last.then_inc(pe, 1)
                for b in range(B):
                    if a == 0:
                        t.wait_ge([g_V00, g_V01][b], 16)
                    else:
                        t.wait_ge(s_Vp[b], P["sld_V"][(a, b)])
                    mk = P["ve_m0"] if b == 0 else P["ve_m1"]
                    for p in range(a, 8):
                        if vgrp >= 2:
                            t.wait_ge(ve, _ve_evac_by_idx(P, vgrp - 2))
                        po = psO[vgrp % 2]
                        att = at0 if b == 0 else at1
                        last = None
                        for rc in range(8):
                            if p == a:
                                t.wait_ge(ve, mk[(a, rc)])
                            for foff in (0, 512):
                                last = t.matmul(
                                    po[:, foff:foff + 512],
                                    att[:, rc * 1024 + (p - a) * 128:
                                        rc * 1024 + (p - a) * 128 + 128],
                                    V_sb[:, (b * 8 + rc) * 1024 + foff:
                                         (b * 8 + rc) * 1024 + foff + 512],
                                    start=(rc == 0), stop=(rc == 7 and a != 0),
                                )
                        if a == 0:
                            for foff in (0, 512):
                                last = t.matmul(
                                    po[:, foff:foff + 512],
                                    c19_sb[:, p * 128:(p + 1) * 128],
                                    us_sb[:, b * 1024 + foff:b * 1024 + foff + 512],
                                    start=False, stop=True,
                                )
                        last.then_inc(pe, 1)
                        vgrp += 1
            t.wait_ge(ve, P["ve_evac"][(7, 0, 7)])
            for b in range(B):
                if b >= 1:
                    t.wait_ge(ve, P["ve_gcev"][0])
                last = None
                for p in range(8):
                    t.wait_ge(ve, P["ve_apply"][(b, p)])
                    for foff in (0, 512):
                        last = t.matmul(
                            psO0[0:1, foff:foff + 512],
                            ones[:, 0:1],
                            acc[:, (b * 8 + p) * 1024 + foff:(b * 8 + p) * 1024 + foff + 512],
                            start=(p == 0), stop=(p == 7),
                        )
                last.then_inc(pe, 1)

        @block.scalar
        def _(s):
            s.dma_start(out=qT_sb[:, 0:8192].rearrange("p (ic m) -> p ic m", ic=8),
                        in_=qT_in[0].rearrange("ic p m -> p ic m")).then_inc(g_qT0, 16)
            s.dma_start(out=qT_sb[:, 8192:16384].rearrange("p (ic m) -> p ic m", ic=8),
                        in_=qT_in[1].rearrange("ic p m -> p ic m")).then_inc(g_qT1, 16)
            sc = Ctr()
            for a in range(8):
                w = (8 - a) * 128
                for rc in range(8):
                    s.wait_ge(acts, sc.v)
                    s.wait_ge(pe, P["pe_S"][(a, rc)])
                    s.activation(at0[:, rc * 1024:rc * 1024 + w],
                                 psS[rc % 2][:, 0:w], AF.Sigmoid,
                                 scale=1.0 / 32.0).then_inc(acts, 1)
                    sc.inc()
                s.wait_ge(acts, sc.v)
                s.wait_ge(ve, P["ve_vteps"][a])
                s.activation(lt[:, 2 * a:2 * a + 2], vt[:, 2 * a:2 * a + 2],
                             AF.Ln, scale=2.0 ** -20).then_inc(acts, 1)
                sc.inc()
                s.wait_ge(acts, sc.v)
                s.activation(rt[:, 2 * a:2 * a + 2], lt[:, 2 * a:2 * a + 2],
                             AF.Exp, scale=-0.5).then_inc(acts, 1)
                sc.inc()
                for b in range(B):
                    s.wait_ge(ve, P["ve_apply"][(b, a)])
                    s.dma_start(out=x1[b, a],
                                in_=acc[:, (b * 8 + a) * 1024:(b * 8 + a + 1) * 1024]
                                ).then_inc(st, 16)
            s.wait_ge(pe, P["pe_V"][(7, 0, 7)])
            for p in range(8):
                s.wait_ge(acts, sc.v)
                s.wait_ge(ve, P["ve_apply"][(0, p)])
                s.copy(at0[:, p * 1024:(p + 1) * 1024],
                       acc[:, p * 1024:(p + 1) * 1024]).then_inc(acts, 1)
                sc.inc()
            s.wait_ge(acts, sc.v)
            s.dma_start(out=x1b[0].rearrange("p ii f -> ii p f"),
                        in_=at0[:, :].rearrange("ii (p f) -> ii p f", p=8)
                        ).then_inc(st, 16)
            s.wait_ge(ve, P["ve_b16"][(1, 7)])
            s.dma_start(out=x1b[1].rearrange("p ii f -> ii p f"),
                        in_=at1[:, :].rearrange("ii (p f) -> ii p f", p=8)
                        ).then_inc(st, 16)
            for b in range(B):
                s.wait_ge(ve, P["ve_gcev"][b])
                s.dma_start(out=gcp[b:b + 1, :],
                            in_=sq[0:1, 0:1024]).then_inc(st, 16)

        @block.vector
        def _(v):
            vc = Ctr()

            def op(fn, *args, **kw):
                v.wait_ge(ve, vc.v)
                fn(*args, **kw).then_inc(ve, 1)
                vc.inc()

            op(v.memset, ones[:, :], 1.0)
            v.wait_ge(g_cst, 16 * 4)
            g_xp = [g_x0, g_x1]
            for a in range(8):
                w = (8 - a) * 128
                par = a % 2
                for rc in range(8):
                    v.wait_ge(acts, P["acts_sig"][(a, rc)])
                    sl0 = at0[:, rc * 1024:rc * 1024 + 128]
                    msl = slice(par * 1024 + rc * 128, par * 1024 + rc * 128 + 128)
                    op(v.tensor_tensor, sl0, sl0, m1m[:, msl], ALU.mult)
                    op(v.tensor_tensor, sl0, sl0, m2m[:, msl], ALU.add)
                    assert vc.v == P["ve_m0"][(a, rc)]
                    op(v.tensor_tensor, at1[:, rc * 1024:rc * 1024 + 128],
                       m1m[:, msl], sl0, ALU.subtract)
                    op(v.scalar_tensor_tensor, at1[:, rc * 1024:rc * 1024 + 128],
                       m2m[:, msl], 2.0, at1[:, rc * 1024:rc * 1024 + 128],
                       ALU.mult, ALU.add)
                    if w > 128:
                        op(v.tensor_scalar, at1[:, rc * 1024 + 128:rc * 1024 + w],
                       at0[:, rc * 1024 + 128:rc * 1024 + w], -1.0, 1.0,
                       ALU.mult, ALU.add)
                assert vc.v == P["ve_ready"][a]
                for b in range(B):
                    for p in range(a, 8):
                        v.wait_ge(pe, P["pe_V"][(a, b, p)])
                        asl = acc[:, (b * 8 + p) * 1024:(b * 8 + p + 1) * 1024]
                        po = psO[_vgrp_idx(a, b, p) % 2]
                        if a == 0:
                            op(v.tensor_copy, asl, po[:, :])
                        else:
                            op(v.tensor_tensor, asl, asl, po[:, :], ALU.add)
                        assert vc.v == P["ve_evac"][(a, b, p)]
                        if p == a:
                            idx = p * 2 + b
                            col = slice(2 * a + b, 2 * a + b + 1)
                            v.wait_ge(g_xp[idx % 2], P["gld_x"][(b, p)])
                            op(v.scalar_tensor_tensor, asl,
                               xres[:, (idx % 2) * 1024:(idx % 2 + 1) * 1024],
                               1.0, asl, ALU.mult, ALU.add, accum_out=mt[:, col])
                            assert vc.v == P["ve_xadd"][(b, p)]
                            op(v.scalar_tensor_tensor, sq[:, :], asl, 1.0, asl,
                               ALU.mult, ALU.mult, accum_out=v2t[:, col])
                            op(v.tensor_scalar, tmp1[:, col], mt[:, col],
                               mt[:, col], RAWEPS, ALU.mult, ALU.subtract)
                            op(v.scalar_tensor_tensor, vt[:, col], v2t[:, col],
                               1024.0, tmp1[:, col], ALU.mult, ALU.subtract)
                            assert vc.v == P["ve_vt"][(b, p)]
                            if b == 1:
                                assert vc.v == P["ve_vteps"][a]
                v.wait_ge(acts, P["acts_rstd"][a])
                for b in range(B):
                    col = slice(2 * a + b, 2 * a + b + 1)
                    op(v.tensor_scalar, nmr[:, col], mt[:, col], rt[:, col],
                       -1.0 / 1024.0, ALU.mult, ALU.mult)
                    assert vc.v == P["ve_nmr"][(b, a)]
                    asl = acc[:, (b * 8 + a) * 1024:(b * 8 + a + 1) * 1024]
                    op(v.tensor_scalar, asl, asl, rt[:, col], nmr[:, col],
                       ALU.mult, ALU.add)
                    assert vc.v == P["ve_apply"][(b, a)]
            for p in range(8):
                op(v.tensor_copy, at1[:, p * 1024:(p + 1) * 1024],
                   acc[:, (8 + p) * 1024:(9 + p) * 1024])
                assert vc.v == P["ve_b16"][(1, p)]
            for b in range(B):
                v.wait_ge(pe, P["pe_gc"][b])
                if b >= 1:
                    v.wait_ge(st, P["st_gc"][0])
                op(v.tensor_copy, sq[0:1, 0:1024], psO0[0:1, :])
                assert vc.v == P["ve_gcev"][b]
    return nc


def _vgrp_idx(a, b, p):
    i = 0
    for aa in range(a):
        i += 2 * (8 - aa)
    return i + b * (8 - a) + (p - a)


def _ve_evac_by_idx(P, vgrp):
    for (a, b, p), val in P["ve_evac"].items():
        if _vgrp_idx(a, b, p) == vgrp:
            return val
    raise KeyError(vgrp)


def l2_host_inputs(x, qT, kT, vT, usum):
    import ml_dtypes
    bf = ml_dtypes.bfloat16
    k_full = np.ascontiguousarray(qT[0:0]).reshape(0,)
    k_tok = np.ascontiguousarray(kT.transpose(0, 2, 1))
    v_tok = np.ascontiguousarray(vT.transpose(0, 2, 1))
    K_in = np.zeros((8, 8, B, 128, 8, 128), dtype=bf)
    for a in range(8):
        ks = k_tok[:, a::8, :].reshape(B, 8, 128, 8, 128)
        K_in[a] = ks.transpose(3, 0, 2, 1, 4)
    V_in = v_tok.reshape(B, 8, 8, 128, 1024)
    us_in = np.ascontiguousarray(usum.transpose(1, 0, 2)).reshape(8, B * 1024)
    c19 = np.zeros((8, 1024), dtype=bf)
    for A in range(8):
        for p in range(8):
            if A > p:
                c19[A, p * 128:(p + 1) * 128] = 1e-9
    maps = []
    for c in range(NC):
        rows = np.concatenate([np.arange(t * 128, (t + 1) * 128)
                               for t in snake_tiles(c)])
        qTc = np.ascontiguousarray(qT[:, :, rows]).reshape(B, 8, 128, 1024)
        xqc = np.ascontiguousarray(x[:, rows, :]).reshape(B, 8, 128, 1024)
        m1 = np.zeros((128, 2048), dtype=np.float32)
        for par, tm8 in ((0, c), (1, 7 - c)):
            for rc in range(8):
                slc = slice(par * 1024 + rc * 128, par * 1024 + rc * 128 + 128)
                if rc < tm8:
                    m1[:, slc] = 1.0
                elif rc == tm8:
                    jj = np.arange(128)[:, None]
                    mm = np.arange(128)[None, :]
                    m1[:, slc] = (jj <= mm).astype(np.float32)
        m2 = 1e-9 * (1.0 - m1)
        maps.append({
            "qT_in": qTc, "K_in": K_in, "V_in": V_in, "us_in": us_in,
            "c19_in": c19, "m1_in": m1.astype(bf), "m2_in": m2.astype(bf),
            "xq_in": xqc,
        })
    return maps


def l2_host_outputs(res):
    import ml_dtypes
    x1 = np.zeros((B, S, D), dtype=np.float32)
    x1bf = np.zeros((B, S, D), dtype=ml_dtypes.bfloat16)
    gcp = np.zeros((NC, B, D), dtype=np.float32)
    for c in range(NC):
        r = res.results[c]
        for pi, t in enumerate(snake_tiles(c)):
            x1[:, t * 128:(t + 1) * 128, :] = r["x1"][:, pi]
            x1bf[:, t * 128:(t + 1) * 128, :] = r["x1b"][:, pi]
        gcp[c] = r["gcp"].reshape(B, D)
    return x1, x1bf, gcp



def plan_l3():
    pe = Ctr(); acts = Ctr(); ve = Ctr(); gld = Ctr(); sld = Ctr(); st = Ctr()
    P = {}
    xp = [Ctr(), Ctr(), Ctr(), Ctr()]
    for b in range(B):
        for p in range(8):
            idx = b * 8 + p
            xp[idx % 4].inc(16)
            P["sld_x1s", b, p] = xp[idx % 4].v

    ve.inc(1); P["ve_ones"] = ve.v
    ve.inc(9); P["ve_gc16b"] = ve.v

    for oc in range(8):
        pe.inc(); P["pe_h1", 0, 0, oc] = pe.v
    for b in range(B):
        for oc in range(4):
            pe.inc(); P["pe_g1", b, oc] = pe.v
        for oc in range(2):
            pe.inc(); P["pe_g2", b, oc] = pe.v
        pe.inc(); P["pe_gf", b] = pe.v
    pe.inc(); P["pe_bc"] = pe.v
    for l in range(3):
        for b in range(B):
            if l > 0:
                for g in range(16):
                    pe.inc(); P["pe_tp", b, l, g] = pe.v
            for oc in range(8):
                if (b, l) != (0, 0):
                    pe.inc(); P["pe_h1", b, l, oc] = pe.v
            for oc in range(4):
                pe.inc(); P["pe_h2", b, l, oc] = pe.v
            for oc in range(2):
                pe.inc(); P["pe_h3", b, l, oc] = pe.v
            pe.inc(); P["pe_tf", b, l] = pe.v
    for oc in range(8):
        acts.inc(); P["acts_h1", 0, 0, oc] = acts.v
    for b in range(B):
        for oc in range(4):
            acts.inc(); P["acts_g1", b, oc] = acts.v
        for oc in range(2):
            acts.inc(); P["acts_g2", b, oc] = acts.v
    for b in range(B):
        acts.inc(); P["acts_gf", b] = acts.v
    ve.inc(); P["ve_bc"] = ve.v
    ve.inc(7); P["ve_AB"] = ve.v
    def _plan_hacts(b, l):
        for oc in range(8):
            if (b, l) != (0, 0):
                acts.inc(); P["acts_h1", b, l, oc] = acts.v
        for oc in range(4):
            acts.inc(); P["acts_h2", b, l, oc] = acts.v
        for oc in range(2):
            acts.inc(); P["acts_h3", b, l, oc] = acts.v

    def _plan_sig(b, l):
        acts.inc(); P["acts_tf", b, l] = acts.v

    for l in range(2):
        _plan_hacts(0, l); _plan_sig(0, l)
        _plan_hacts(1, l); _plan_sig(1, l)
    _plan_hacts(0, 2); _plan_sig(0, 2)
    for oc in range(8):
        acts.inc(); P["acts_h1", 1, 2, oc] = acts.v
    acts.inc(2); P["acts_rstd2", 0] = acts.v
    for oc in range(4):
        acts.inc(); P["acts_h2", 1, 2, oc] = acts.v
    for oc in range(2):
        acts.inc(); P["acts_h3", 1, 2, oc] = acts.v
    _plan_sig(1, 2)
    for p in range(8):
        acts.inc(2); P["acts_rstd1", p] = acts.v
    def _plan_CE(b, l):
        ve.inc(); P["ve_obf", b, l] = ve.v
        for g in range(16):
            ve.inc(); P["ve_tpev", b, l, g] = ve.v

    def _plan_T(b, l):
        ve.inc(2); P["ve_coup", b, l] = ve.v
        for p in range(8):
            ve.inc(3); P["ve_twc", b, l, p] = ve.v
        P["ve_twist", b, l] = ve.v

    po = Ctr()
    P["po"] = po

    def _plan_LN2(b):
        if b == 0:
            for p in range(8):
                po.inc(); P["po_s2add", 0, p] = po.v
                po.inc(); P["po_sq", 0, p] = po.v
                ve.inc()
                ve.inc(3)
                P["ve_vt", 0, p] = ve.v
            P["ve_vt2eps", 0] = ve.v
            ve.inc(2)
            P["ve_nmrb", 0] = ve.v
            for p in range(8):
                ve.inc(); P["ve_apply", 0, p] = ve.v
        else:
            for p in range(8):
                ve.inc(); P["ve_s2add1", p] = ve.v
                if p >= 1:
                    ve.inc(3)
                    P["ve_vt", 1, p - 1] = ve.v
                    P["ve_vteps1", p - 1] = ve.v
                if p >= 2:
                    ve.inc(2)
                    P["ve_apply", 1, p - 2] = ve.v
            ve.inc(3); P["ve_vt", 1, 7] = ve.v
            P["ve_vteps1", 7] = ve.v
            ve.inc(2); P["ve_apply", 1, 6] = ve.v
            ve.inc(2); P["ve_apply", 1, 7] = ve.v

    _plan_T(0, 0)
    _plan_CE(0, 1)
    _plan_T(1, 0)
    _plan_CE(1, 1)
    _plan_T(0, 1)
    _plan_CE(0, 2)
    _plan_T(1, 1)
    _plan_CE(1, 2)
    _plan_T(0, 2)
    _plan_LN2(0)
    _plan_T(1, 2)
    _plan_LN2(1)
    st.inc(16 * 9)
    P["st_total"] = st.v
    seq = [("acts_h1", 0, 0, oc) for oc in range(8)]
    for l in range(3):
        for b in range(B):
            if (b, l) != (0, 0):
                seq += [("acts_h1", b, l, oc) for oc in range(8)]
            seq += [("acts_h2", b, l, oc) for oc in range(4)]
            seq += [("acts_h3", b, l, oc) for oc in range(2)]
    P["psh_prev"] = [P[k] for k in seq]
    return P


def build_l3():
    nc = bass.Bass("TRN2", target_bir_lowering=False)
    x1_in = nc.dram_tensor("x1_in", [B, 8, 128, 1024], F32, kind="ExternalInput").ap()
    x1T_in = nc.dram_tensor("x1T_in", [B, 8, 128, 1024], BF16, kind="ExternalInput").ap()
    gcT_in = nc.dram_tensor("gcT_in", [128, 128], F32, kind="ExternalInput").ap()
    w1_in = nc.dram_tensor("w1_in", [8, 128, 1024], F32, kind="ExternalInput").ap()
    w2_in = nc.dram_tensor("w2_in", [8, 128, 512], F32, kind="ExternalInput").ap()
    w3_in = nc.dram_tensor("w3_in", [4, 128, 256], F32, kind="ExternalInput").ap()
    w4_in = nc.dram_tensor("w4_in", [128, 2], F32, kind="ExternalInput").ap()
    gw1_in = nc.dram_tensor("gw1_in", [8, 128, 512], F32, kind="ExternalInput").ap()
    gw2_in = nc.dram_tensor("gw2_in", [4, 128, 256], F32, kind="ExternalInput").ap()
    gw3_in = nc.dram_tensor("gw3_in", [128, 2], F32, kind="ExternalInput").ap()
    sml_in = nc.dram_tensor("sml_in", [128, 20], F32, kind="ExternalInput").ap()
    b4_in = nc.dram_tensor("b4_in", [1, 1], F32, kind="ExternalInput").ap()
    gb3_in = nc.dram_tensor("gb3_in", [1, 1], F32, kind="ExternalInput").ap()
    ar_in = nc.dram_tensor("ar_in", [1, 1], F32, kind="ExternalInput").ap()
    ident_in = nc.dram_tensor("ident_in", [128, 128], BF16, kind="ExternalInput").ap()
    x2 = nc.dram_tensor("x2", [B, 8, 128, 1024], F32, kind="ExternalOutput").ap()

    P = plan_l3()
    EPS = 1e-5
    RAWEPS = EPS * 1024.0 * 1024.0
    from contextlib import ExitStack
    with ExitStack() as _ctx:
        def _e(x):
            return _ctx.enter_context(x)
        w1s = _e(nc.sbuf_tensor("w1s", [128, 8 * 1024], BF16))
        w2s = _e(nc.sbuf_tensor("w2s", [128, 8 * 512], BF16))
        w3s = _e(nc.sbuf_tensor("w3s", [128, 4 * 256], BF16))
        w4s = _e(nc.sbuf_tensor("w4s", [128, 2], BF16))
        gw1s = _e(nc.sbuf_tensor("gw1s", [128, 8 * 512], BF16))
        gw2s = _e(nc.sbuf_tensor("gw2s", [128, 4 * 256], BF16))
        gw3s = _e(nc.sbuf_tensor("gw3s", [128, 2], BF16))
        sml_sb = _e(nc.sbuf_tensor("sml_sb", [128, 20], F32))
        gb3s = _e(nc.sbuf_tensor("gb3s", [1, 1], F32))
        gcTs = _e(nc.sbuf_tensor("gcTs", [128, 128], F32))
        gc16 = _e(nc.sbuf_tensor("gc16", [128, 16], F32))
        gc16b = _e(nc.sbuf_tensor("gc16b", [128, 16], BF16))
        idents = _e(nc.sbuf_tensor("idents", [128, 128], BF16))
        sc4 = _e(nc.sbuf_tensor("sc4", [1, 4], F32))
        ones1 = _e(nc.sbuf_tensor("ones1", [1, 128], F32))
        bc = _e(nc.sbuf_tensor("bc", [128, 4], F32))
        Ab = _e(nc.sbuf_tensor("Ab", [128, 2], F32))
        Bc = _e(nc.sbuf_tensor("Bc", [128, 1], F32))
        tAB = _e(nc.sbuf_tensor("tAB", [128, 2], F32))
        g1 = _e(nc.sbuf_tensor("g1", [128, 4], BF16))
        g2 = _e(nc.sbuf_tensor("g2", [128, 2], BF16))
        out0 = _e(nc.sbuf_tensor("out0", [128, 8 * 1024], F32))
        out1 = _e(nc.sbuf_tensor("out1", [128, 8 * 1024], F32))
        outT0 = _e(nc.sbuf_tensor("outT0", [128, 8 * 1024], BF16))
        outT1 = _e(nc.sbuf_tensor("outT1", [128, 8 * 1024], BF16))
        obf = _e(nc.sbuf_tensor("obf", [128, 8 * 1024], BF16))
        h1T = _e(nc.sbuf_tensor("h1T", [128, 8 * 1024], BF16))
        h2T = _e(nc.sbuf_tensor("h2T", [128, 4 * 1024], BF16))
        h3T = _e(nc.sbuf_tensor("h3T", [128, 2 * 1024], BF16))
        x1s = _e(nc.sbuf_tensor("x1s", [128, 4 * 1024], F32))
        tfs = _e(nc.sbuf_tensor("tfs", [128, 2 * 8], F32))
        coup = _e(nc.sbuf_tensor("coup", [128, 2 * 8], F32))
        coupn = _e(nc.sbuf_tensor("coupn", [128, 2 * 8], F32))
        wtmp = _e(nc.sbuf_tensor("wtmp", [128, 1024], F32))
        sq2 = _e(nc.sbuf_tensor("sq2", [128, 2 * 1024], F32))
        mt2 = _e(nc.sbuf_tensor("mt2", [128, 16], F32))
        v2t2 = _e(nc.sbuf_tensor("v2t2", [128, 16], F32))
        vt2 = _e(nc.sbuf_tensor("vt2", [128, 16], F32))
        lt2 = _e(nc.sbuf_tensor("lt2", [128, 16], F32))
        rt2 = _e(nc.sbuf_tensor("rt2", [128, 16], F32))
        nmr2 = _e(nc.sbuf_tensor("nmr2", [128, 16], F32))
        tsm = _e(nc.sbuf_tensor("tsm", [128, 16], F32))
        psH0 = _e(nc.psum_tensor("psH0", [128, 1024], F32))
        psH1 = _e(nc.psum_tensor("psH1", [128, 1024], F32))
        psE0 = _e(nc.psum_tensor("psE0", [128, 512], BF16))
        psE1 = _e(nc.psum_tensor("psE1", [128, 512], BF16))
        psG = _e(nc.psum_tensor("psG", [128, 32], F32))
        g_w1a = _e(nc.semaphore("g_w1a"))
        g_w1b = _e(nc.semaphore("g_w1b"))
        g_sm = _e(nc.semaphore("g_sm"))
        g_gc = _e(nc.semaphore("g_gc"))
        g_w2 = _e(nc.semaphore("g_w2"))
        s_oT0 = _e(nc.semaphore("s_oT0"))
        s_oT1 = _e(nc.semaphore("s_oT1"))
        g_sm2 = _e(nc.semaphore("g_sm2"))
        st_g = _e(nc.semaphore("st_g"))
        s_ob0 = _e(nc.semaphore("s_ob0"))
        s_ob1 = _e(nc.semaphore("s_ob1"))
        s_x10 = _e(nc.semaphore("s_x10"))
        s_x11 = _e(nc.semaphore("s_x11"))
        s_x12 = _e(nc.semaphore("s_x12"))
        s_x13 = _e(nc.semaphore("s_x13"))
        po = _e(nc.semaphore("po"))
        pe = _e(nc.semaphore("pe"))
        acts = _e(nc.semaphore("acts"))
        ve = _e(nc.semaphore("ve"))
        st = _e(nc.semaphore("st"))
        block = _e(nc.Block())
        psH = [psH0, psH1]
        psE = [psE0, psE1]
        outb = [out0, out1]
        outT = [outT0, outT1]

        @block.gpsimd
        def _(g):
            g.dma_start(out=w1s[:, 0:4096].rearrange("p (o c) -> p o c", o=4),
                        in_=w1_in[0:4].rearrange("o p c -> p o c")).then_inc(g_w1a, 16)
            g.dma_start(out=w1s[:, 4096:8192].rearrange("p (o c) -> p o c", o=4),
                        in_=w1_in[4:8].rearrange("o p c -> p o c")).then_inc(g_w1b, 16)
            g.dma_start(out=gw1s[:, :].rearrange("p (hc d) -> p hc d", hc=8),
                        in_=gw1_in.rearrange("hc p d -> p hc d")).then_inc(g_gc, 16)
            g.dma_start(out=gw2s[:, :].rearrange("p (hc d) -> p hc d", hc=4),
                        in_=gw2_in.rearrange("hc p d -> p hc d")).then_inc(g_gc, 16)
            g.dma_start(out=gw3s[:, :], in_=gw3_in).then_inc(g_gc, 16)
            g.dma_start(out=idents[:, :], in_=ident_in).then_inc(g_gc, 16)
            g.dma_start(out=gcTs[:, :], in_=gcT_in).then_inc(g_gc, 16)
            g.dma_start(out=w2s[:, :].rearrange("p (hc d) -> p hc d", hc=8),
                        in_=w2_in.rearrange("hc p d -> p hc d")).then_inc(g_w2, 16)
            g.dma_start(out=w3s[:, :].rearrange("p (hc d) -> p hc d", hc=4),
                        in_=w3_in.rearrange("hc p d -> p hc d")).then_inc(g_w2, 16)
            g.dma_start(out=w4s[:, :], in_=w4_in).then_inc(g_w2, 16)
            for p in range(8):
                osl = outb[0][:, p * 1024:(p + 1) * 1024]
                g.wait_ge(ve, P["ve_twc", 0, 2, p])
                g.wait_ge([s_x10, s_x11, s_x12, s_x13][p % 4], P["sld_x1s", 0, p])
                if p > 0:
                    g.wait_ge(po, P["po_sq", 0, p - 1])
                g.tensor_tensor(osl, osl,
                                x1s[:, (p % 4) * 1024:(p % 4 + 1) * 1024],
                                ALU.add).then_inc(po, 1)
                if p >= 2:
                    g.wait_ge(ve, P["ve_vt", 0, p - 2])
                g.wait_ge(po, P["po_s2add", 0, p])
                g.tensor_tensor(sq2[:, (p % 2) * 1024:(p % 2 + 1) * 1024],
                                osl, osl, ALU.mult).then_inc(po, 1)

        @block.sync
        def _(sy):
            s_x1p = [s_x10, s_x11, s_x12, s_x13]
            sy.dma_start(out=sml_sb[:, :], in_=sml_in).then_inc(g_sm, 16)
            sy.dma_start(out=outT[0][:, :].rearrange("p (fc t) -> p fc t", fc=8),
                         in_=x1T_in[0].rearrange("fc p t -> p fc t")).then_inc(s_oT0, 16)
            sy.dma_start(out=outb[0][:, :].rearrange("ii (p f) -> ii p f", p=8),
                         in_=x1_in[0].rearrange("p ii f -> ii p f")).then_inc(s_ob0, 16)
            sy.dma_start(out=outT[1][:, :].rearrange("p (fc t) -> p fc t", fc=8),
                         in_=x1T_in[1].rearrange("fc p t -> p fc t")).then_inc(s_oT1, 16)
            sy.dma_start(out=outb[1][:, :].rearrange("ii (p f) -> ii p f", p=8),
                         in_=x1_in[1].rearrange("p ii f -> ii p f")).then_inc(s_ob1, 16)
            sy.dma_start(out=sc4[0:1, 0:1], in_=b4_in).then_inc(g_sm2, 16)
            sy.dma_start(out=sc4[0:1, 1:2], in_=ar_in).then_inc(g_sm2, 16)
            sy.dma_start(out=gb3s[0:1, 0:1], in_=gb3_in).then_inc(g_sm2, 16)

            def x1load(idx):
                b, p = divmod(idx, 8)
                if idx >= 4:
                    pb, pp = divmod(idx - 4, 8)
                    if pb == 0:
                        sy.wait_ge(po, P["po_s2add", 0, pp])
                    else:
                        sy.wait_ge(ve, P["ve_s2add1", pp])
                sy.dma_start(out=x1s[:, (idx % 4) * 1024:(idx % 4 + 1) * 1024],
                             in_=x1_in[b, p]).then_inc(s_x1p[idx % 4], 16)

            for idx in range(12):
                x1load(idx)
            sy.wait_ge(ve, P["ve_apply", 0, 7])
            sy.dma_start(out=x2[0].rearrange("p ii f -> ii p f"),
                         in_=outb[0][:, :].rearrange("ii (p f) -> ii p f", p=8)
                         ).then_inc(st, 16)
            for idx in range(12, 16):
                x1load(idx)
            for p in (0, 2, 4, 6):
                sy.wait_ge(ve, P["ve_apply", 1, p])
                sy.dma_start(out=x2[1, p],
                             in_=out1[:, p * 1024:(p + 1) * 1024]).then_inc(st, 16)
            sy.wait_ge(st, 16 * 5)
            sy.wait_ge(st_g, 16 * 4)

        @block.tensor
        def _(t):
            pshi = [0]

            def psh_wait():
                i = pshi[0]
                if i >= 2:
                    t.wait_ge(acts, P["psh_prev"][i - 2])
                pshi[0] = i + 1

            t.wait_ge(g_w1a, 16)
            t.wait_ge(s_oT0, 16)
            for oc in range(8):
                if oc == 4:
                    t.wait_ge(g_w1b, 16)
                psh_wait()
                last = None
                for ic in range(8):
                    for pc in (0, 512):
                        last = t.matmul(
                            psH[oc % 2][:, pc:pc + 512],
                            w1s[:, oc * 1024 + ic * 128:oc * 1024 + ic * 128 + 128],
                            outT[0][:, ic * 1024 + pc:ic * 1024 + pc + 512],
                            start=(ic == 0), stop=(ic == 7))
                last.then_inc(pe, 1)
            t.wait_ge(g_gc, 16 * 5)
            t.wait_ge(ve, P["ve_gc16b"])
            for b in range(B):
                for oc in range(4):
                    last = None
                    for ic in range(8):
                        last = t.matmul(psG[:, oc:oc + 1],
                                        gw1s[:, ic * 512 + oc * 128:ic * 512 + oc * 128 + 128],
                                        gc16b[:, b * 8 + ic:b * 8 + ic + 1],
                                        start=(ic == 0), stop=(ic == 7))
                    last.then_inc(pe, 1)
                for oc in range(2):
                    t.wait_ge(acts, P["acts_g1", b, 3])
                    last = None
                    for ic in range(4):
                        last = t.matmul(psG[:, 4 + oc:5 + oc],
                                        gw2s[:, ic * 256 + oc * 128:ic * 256 + oc * 128 + 128],
                                        g1[:, ic:ic + 1],
                                        start=(ic == 0), stop=(ic == 3))
                    last.then_inc(pe, 1)
                t.wait_ge(acts, P["acts_g2", b, 1])
                last = None
                for icc in range(2):
                    last = t.matmul(psG[0:1, 6 + b:7 + b],
                                    gw3s[:, icc:icc + 1], g2[:, icc:icc + 1],
                                    start=(icc == 0), stop=(icc == 1))
                last.then_inc(pe, 1)
            t.wait_ge(acts, P["acts_gf", 1])
            t.wait_ge(g_sm2, 16 * 3)
            t.wait_ge(ve, P["ve_ones"])
            t.matmul(psG[:, 8:12], ones1[0:1, :], sc4[0:1, :],
                     start=True, stop=True).then_inc(pe, 1)
            t.wait_ge(g_w2, 16 * 3)
            for l in range(3):
                for b in range(B):
                    if l > 0:
                        t.wait_ge(ve, P["ve_obf", b, l])
                        gi = 0
                        for fc in range(8):
                            for pg in range(2):
                                if gi >= 2:
                                    t.wait_ge(ve, P["ve_tpev", b, l, gi - 2])
                                last = None
                                for j in range(4):
                                    p = pg * 4 + j
                                    last = t.transpose(
                                        psE[gi % 2][:, j * 128:(j + 1) * 128],
                                        obf[:, p * 1024 + fc * 128:p * 1024 + fc * 128 + 128],
                                        idents[:, :])
                                last.then_inc(pe, 1)
                                gi += 1
                        t.wait_ge(ve, P["ve_tpev", b, l, 15])
                    else:
                        if b == 1:
                            t.wait_ge(s_oT1, 16)
                    if (b, l) != (0, 0):
                        for oc in range(8):
                            psh_wait()
                            last = None
                            for ic in range(8):
                                for pc in (0, 512):
                                    last = t.matmul(
                                        psH[oc % 2][:, pc:pc + 512],
                                        w1s[:, oc * 1024 + ic * 128:oc * 1024 + ic * 128 + 128],
                                        outT[b][:, ic * 1024 + pc:ic * 1024 + pc + 512],
                                        start=(ic == 0), stop=(ic == 7))
                            last.then_inc(pe, 1)
                    t.wait_ge(acts, P["acts_h1", b, l, 7])
                    for oc in range(4):
                        psh_wait()
                        last = None
                        for ic in range(8):
                            for pc in (0, 512):
                                last = t.matmul(
                                    psH[oc % 2][:, pc:pc + 512],
                                    w2s[:, ic * 512 + oc * 128:ic * 512 + oc * 128 + 128],
                                    h1T[:, ic * 1024 + pc:ic * 1024 + pc + 512],
                                    start=(ic == 0), stop=(ic == 7))
                        last.then_inc(pe, 1)
                    t.wait_ge(acts, P["acts_h2", b, l, 3])
                    for oc in range(2):
                        psh_wait()
                        last = None
                        for ic in range(4):
                            for pc in (0, 512):
                                last = t.matmul(
                                    psH[oc % 2][:, pc:pc + 512],
                                    w3s[:, ic * 256 + oc * 128:ic * 256 + oc * 128 + 128],
                                    h2T[:, ic * 1024 + pc:ic * 1024 + pc + 512],
                                    start=(ic == 0), stop=(ic == 3))
                        last.then_inc(pe, 1)
                    t.wait_ge(acts, P["acts_h3", b, l, 1])
                    if not (b == 0 and l == 0):
                        lb, ll = (0, l) if b == 1 else (1, l - 1)
                        t.wait_ge(acts, P["acts_tf", lb, ll])
                    last = None
                    for p in range(8):
                        for icc in range(2):
                            last = t.matmul(
                                psG[:, 12 + p:13 + p],
                                h3T[:, icc * 1024 + p * 128:icc * 1024 + p * 128 + 128],
                                w4s[:, icc:icc + 1],
                                start=(icc == 0), stop=(icc == 1))
                    last.then_inc(pe, 1)

        @block.scalar
        def _(s):
            sc = Ctr()

            def act(fn_out, fn_in, func, **kw):
                s.wait_ge(acts, sc.v)
                s.activation(fn_out, fn_in, func, **kw).then_inc(acts, 1)
                sc.inc()

            s.wait_ge(g_sm, 16)
            for oc in range(8):
                s.wait_ge(pe, P["pe_h1", 0, 0, oc])
                act(h1T[:, oc * 1024:(oc + 1) * 1024], psH[oc % 2][:, :],
                    AF.Gelu, bias=sml_sb[:, oc:oc + 1])
            for b in range(B):
                for oc in range(4):
                    s.wait_ge(pe, P["pe_g1", b, oc])
                    act(g1[:, oc:oc + 1], psG[:, oc:oc + 1], AF.Gelu,
                        bias=sml_sb[:, 14 + oc:15 + oc])
                for oc in range(2):
                    s.wait_ge(pe, P["pe_g2", b, oc])
                    act(g2[:, oc:oc + 1], psG[:, 4 + oc:5 + oc], AF.Gelu,
                        bias=sml_sb[:, 18 + oc:19 + oc])
            s.wait_ge(g_sm2, 16 * 3)
            for b in range(B):
                s.wait_ge(pe, P["pe_gf", b])
                act(sc4[0:1, 2 + b:3 + b], psG[0:1, 6 + b:7 + b], AF.Sigmoid,
                    bias=gb3s[0:1, 0:1])
            s.wait_ge(ve, P["ve_bc"])
            def hacts(b, l):
                if (b, l) != (0, 0):
                    for oc in range(8):
                        s.wait_ge(pe, P["pe_h1", b, l, oc])
                        act(h1T[:, oc * 1024:(oc + 1) * 1024], psH[oc % 2][:, :],
                            AF.Gelu, bias=sml_sb[:, oc:oc + 1])
                for oc in range(4):
                    s.wait_ge(pe, P["pe_h2", b, l, oc])
                    act(h2T[:, oc * 1024:(oc + 1) * 1024], psH[oc % 2][:, :],
                        AF.Gelu, bias=sml_sb[:, 8 + oc:9 + oc])
                for oc in range(2):
                    s.wait_ge(pe, P["pe_h3", b, l, oc])
                    act(h3T[:, oc * 1024:(oc + 1) * 1024], psH[oc % 2][:, :],
                        AF.Gelu, bias=sml_sb[:, 12 + oc:13 + oc])

            def sigact(b, l):
                s.wait_ge(pe, P["pe_tf", b, l])
                act(tfs[:, b * 8:(b + 1) * 8], psG[:, 12:20], AF.Sigmoid,
                    bias=bc[:, 0:1])

            def rstd2(b):
                s.wait_ge(ve, P["ve_vt2eps", b])
                act(lt2[:, b * 8:(b + 1) * 8], vt2[:, b * 8:(b + 1) * 8], AF.Ln,
                    scale=2.0 ** -20)
                act(rt2[:, b * 8:(b + 1) * 8], lt2[:, b * 8:(b + 1) * 8], AF.Exp,
                    scale=-0.5)

            for l in range(2):
                hacts(0, l); sigact(0, l)
                hacts(1, l); sigact(1, l)
            hacts(0, 2); sigact(0, 2)
            for oc in range(8):
                s.wait_ge(pe, P["pe_h1", 1, 2, oc])
                act(h1T[:, oc * 1024:(oc + 1) * 1024], psH[oc % 2][:, :],
                    AF.Gelu, bias=sml_sb[:, oc:oc + 1])
            rstd2(0)
            for oc in range(4):
                s.wait_ge(pe, P["pe_h2", 1, 2, oc])
                act(h2T[:, oc * 1024:(oc + 1) * 1024], psH[oc % 2][:, :],
                    AF.Gelu, bias=sml_sb[:, 8 + oc:9 + oc])
            for oc in range(2):
                s.wait_ge(pe, P["pe_h3", 1, 2, oc])
                act(h3T[:, oc * 1024:(oc + 1) * 1024], psH[oc % 2][:, :],
                    AF.Gelu, bias=sml_sb[:, 12 + oc:13 + oc])
            sigact(1, 2)
            def rstd1(p):
                s.wait_ge(ve, P["ve_vteps1", p])
                act(lt2[:, 8 + p:9 + p], vt2[:, 8 + p:9 + p], AF.Ln,
                    scale=2.0 ** -20)
                act(rt2[:, 8 + p:9 + p], lt2[:, 8 + p:9 + p], AF.Exp, scale=-0.5)

            for p in range(8):
                rstd1(p)
                if p >= 3 and p % 2 == 1:
                    pw = p - 2
                    s.wait_ge(ve, P["ve_apply", 1, pw])
                    s.dma_start(out=x2[1, pw],
                                in_=out1[:, pw * 1024:(pw + 1) * 1024]).then_inc(st_g, 16)
            s.wait_ge(ve, P["ve_apply", 1, 7])
            s.dma_start(out=x2[1, 7],
                        in_=out1[:, 7 * 1024:8 * 1024]).then_inc(st_g, 16)

        @block.vector
        def _(v):
            vc = Ctr()

            def op(fn, *args, **kw):
                v.wait_ge(ve, vc.v)
                fn(*args, **kw).then_inc(ve, 1)
                vc.inc()

            op(v.memset, ones1[0:1, :], 1.0)
            v.wait_ge(g_gc, 16 * 5)
            op(v.tensor_tensor, gc16[:, :], gcTs[:, 0:16], gcTs[:, 16:32], ALU.add)
            for cr in range(2, 8):
                op(v.tensor_tensor, gc16[:, :], gc16[:, :],
                   gcTs[:, cr * 16:(cr + 1) * 16], ALU.add)
            op(v.tensor_scalar, gc16[:, :], gc16[:, :], 1.0 / 8192.0, None, ALU.mult)
            op(v.tensor_copy, gc16b[:, :], gc16[:, :])
            assert vc.v == P["ve_gc16b"]
            v.wait_ge(pe, P["pe_bc"])
            op(v.tensor_copy, bc[:, :], psG[:, 8:12])
            assert vc.v == P["ve_bc"]
            for b in range(B):
                op(v.tensor_tensor, tAB[:, b:b + 1], bc[:, 1:2], bc[:, 2 + b:3 + b], ALU.mult)
            op(v.tensor_scalar, tAB[:, :], tAB[:, :], 1.4, None, ALU.mult)
            op(v.tensor_scalar, tsm[:, 0:1], bc[:, 1:2], -1.0, 0.1, ALU.mult, ALU.add)
            op(v.tensor_tensor, Ab[:, 0:1], tsm[:, 0:1], tAB[:, 0:1], ALU.add)
            op(v.tensor_tensor, Ab[:, 1:2], tsm[:, 0:1], tAB[:, 1:2], ALU.add)
            op(v.tensor_scalar, Bc[:, 0:1], bc[:, 1:2], 0.6, None, ALU.mult)
            assert vc.v == P["ve_AB"]

            def twist(b, l):
                if l == 0:
                    v.wait_ge([s_ob0, s_ob1][b], 16)
                v.wait_ge(acts, P["acts_tf", b, l])
                op(v.tensor_scalar, coup[:, b * 8:(b + 1) * 8],
                   tfs[:, b * 8:(b + 1) * 8], Bc[:, 0:1], Ab[:, b:b + 1],
                   ALU.mult, ALU.add)
                op(v.tensor_scalar, coupn[:, b * 8:(b + 1) * 8],
                   coup[:, b * 8:(b + 1) * 8], -1.0, None, ALU.mult)
                assert vc.v == P["ve_coup", b, l]
                w4 = wtmp[:, :].rearrange("p (a q) -> p a q", a=4)
                for p in range(8):
                    osl = outb[b][:, p * 1024:(p + 1) * 1024]
                    o4 = osl.rearrange("p (a q) -> p a q", a=4)
                    cp = coup[:, b * 8 + p:b * 8 + p + 1]
                    op(v.tensor_scalar, wtmp[:, :], osl, cp, None, ALU.mult)
                    op(v.scalar_tensor_tensor, o4[:, 0:4:3, :], w4[:, 2:0:-1, :],
                       1.0, o4[:, 0:4:3, :], ALU.mult, ALU.add)
                    op(v.scalar_tensor_tensor, o4[:, 1:3, :], w4[:, 3::-3, :],
                       -1.0, o4[:, 1:3, :], ALU.mult, ALU.add)
                assert vc.v == P["ve_twist", b, l]

            def ln2(b):
                assert b == 0
                bsl = slice(0, 8)
                for p in range(8):
                    col = slice(p, p + 1)
                    osl = outb[0][:, p * 1024:(p + 1) * 1024]
                    v.wait_ge(po, P["po_s2add", 0, p])
                    op(v.reduce_sum, mt2[:, col], osl, AX.X)
                    v.wait_ge(po, P["po_sq", 0, p])
                    op(v.reduce_sum, v2t2[:, col],
                       sq2[:, (p % 2) * 1024:(p % 2 + 1) * 1024], AX.X)
                    op(v.tensor_scalar, tsm[:, col], mt2[:, col], mt2[:, col],
                       RAWEPS, ALU.mult, ALU.subtract)
                    op(v.scalar_tensor_tensor, vt2[:, col], v2t2[:, col],
                       1024.0, tsm[:, col], ALU.mult, ALU.subtract)
                    assert vc.v == P["ve_vt", 0, p]
                assert vc.v == P["ve_vt2eps", 0]
                v.wait_ge(acts, P["acts_rstd2", 0])
                op(v.tensor_tensor, nmr2[:, bsl], mt2[:, bsl], rt2[:, bsl], ALU.mult)
                op(v.tensor_scalar, nmr2[:, bsl], nmr2[:, bsl], -1.0 / 1024.0, None, ALU.mult)
                assert vc.v == P["ve_nmrb", 0]
                for p in range(8):
                    col = slice(p, p + 1)
                    osl = outb[0][:, p * 1024:(p + 1) * 1024]
                    op(v.tensor_scalar, osl, osl, rt2[:, col], nmr2[:, col],
                       ALU.mult, ALU.add)
                    assert vc.v == P["ve_apply", 0, p]

            def ln2b1_head(p):
                idx = 8 + p
                col = slice(idx, idx + 1)
                osl = outb[1][:, p * 1024:(p + 1) * 1024]
                idx4 = [s_x10, s_x11, s_x12, s_x13][idx % 4]
                v.wait_ge(idx4, P["sld_x1s", 1, p])
                op(v.scalar_tensor_tensor, osl,
                   x1s[:, (idx % 4) * 1024:(idx % 4 + 1) * 1024], 1.0, osl,
                   ALU.mult, ALU.add, accum_out=mt2[:, col])
                assert vc.v == P["ve_s2add1", p]

            def ln2b1_var(p):
                idx = 8 + p
                col = slice(idx, idx + 1)
                osl = outb[1][:, p * 1024:(p + 1) * 1024]
                op(v.scalar_tensor_tensor, wtmp[:, :], osl, 1.0, osl,
                   ALU.mult, ALU.mult, accum_out=v2t2[:, col])
                op(v.tensor_scalar, tsm[:, col], mt2[:, col], mt2[:, col],
                   RAWEPS, ALU.mult, ALU.subtract)
                op(v.scalar_tensor_tensor, vt2[:, col], v2t2[:, col],
                   1024.0, tsm[:, col], ALU.mult, ALU.subtract)
                assert vc.v == P["ve_vt", 1, p]
                assert vc.v == P["ve_vteps1", p]

            def ln2b1_apply(p):
                idx = 8 + p
                col = slice(idx, idx + 1)
                osl = outb[1][:, p * 1024:(p + 1) * 1024]
                v.wait_ge(acts, P["acts_rstd1", p])
                op(v.tensor_scalar, nmr2[:, col], mt2[:, col], rt2[:, col],
                   -1.0 / 1024.0, ALU.mult, ALU.mult)
                op(v.tensor_scalar, osl, osl, rt2[:, col], nmr2[:, col],
                   ALU.mult, ALU.add)
                assert vc.v == P["ve_apply", 1, p]

            def ln2b1():
                for p in range(8):
                    ln2b1_head(p)
                    if p >= 1:
                        ln2b1_var(p - 1)
                    if p >= 2:
                        ln2b1_apply(p - 2)
                ln2b1_var(7)
                ln2b1_apply(6)
                ln2b1_apply(7)

            def ce(b, l):
                if (b, l) != (0, 1):
                    ob, ol = (1, l - 1) if b == 0 else (0, l)
                    v.wait_ge(pe, P["pe_tp", ob, ol, 15])
                op(v.tensor_copy, obf[:, :], outb[b][:, :])
                assert vc.v == P["ve_obf", b, l]
                for g in range(16):
                    v.wait_ge(pe, P["pe_tp", b, l, g])
                    fc, pg = divmod(g, 2)
                    op(v.tensor_copy,
                       outT[b][:, fc * 1024 + pg * 512:fc * 1024 + pg * 512 + 512],
                       psE[g % 2][:, :])
                    assert vc.v == P["ve_tpev", b, l, g]

            twist(0, 0)
            ce(0, 1)
            twist(1, 0)
            ce(1, 1)
            twist(0, 1)
            ce(0, 2)
            twist(1, 1)
            ce(1, 2)
            twist(0, 2)
            ln2(0)
            twist(1, 2)
            ln2b1()
    return nc


def l3_host_inputs(x1_full, x1bf, gcp, W):
    import ml_dtypes
    bf = ml_dtypes.bfloat16
    gcT = np.zeros((128, 128), dtype=np.float32)
    for cr in range(NC):
        for b in range(B):
            gcT[:, cr * 16 + b * 8:cr * 16 + (b + 1) * 8] = (
                gcp[cr, b].reshape(8, 128).T)
    common = {
        "gcT_in": gcT,
        "w1_in": np.ascontiguousarray(
            W["cn_w1"].reshape(8, 128, 8, 128).transpose(2, 1, 0, 3)
        ).reshape(8, 128, 1024),
        "w2_in": W["cn_w2"].reshape(8, 128, 512),
        "w3_in": W["cn_w3"].reshape(4, 128, 256),
        "w4_in": np.ascontiguousarray(W["cn_w4"].reshape(2, 128).T),
        "gw1_in": W["gc_w1"].reshape(8, 128, 512),
        "gw2_in": W["gc_w2"].reshape(4, 128, 256),
        "gw3_in": np.ascontiguousarray(W["gc_w3"].reshape(2, 128).T),
        "sml_in": np.concatenate(
            [W["cn_b1"].reshape(8, 128).T, W["cn_b2"].reshape(4, 128).T,
             W["cn_b3"].reshape(2, 128).T, W["gc_b1"].reshape(4, 128).T,
             W["gc_b2"].reshape(2, 128).T], axis=1),
        "b4_in": W["cn_b4"].reshape(1, 1),
        "gb3_in": W["gc_b3"].reshape(1, 1),
        "ar_in": W["adaptive_range"].reshape(1, 1),
        "ident_in": np.eye(128, dtype=bf),
    }
    common = {k: np.ascontiguousarray(v, dtype=v.dtype) for k, v in common.items()}
    maps = []
    for c in range(NC):
        rows = np.concatenate([np.arange(t * 128, (t + 1) * 128)
                               for t in snake_tiles(c)])
        x1c = np.ascontiguousarray(x1_full[:, rows, :]).reshape(B, 8, 128, 1024)
        x1Tc = np.ascontiguousarray(
            x1bf[:, rows, :].transpose(0, 2, 1)).reshape(B, 8, 128, 1024)
        maps.append({"x1_in": x1c, "x1T_in": x1Tc, **common})
    return maps


def l3_host_outputs(res):
    out = np.zeros((B, S, D), dtype=np.float32)
    for c in range(NC):
        r = res.results[c]
        for pi, t in enumerate(snake_tiles(c)):
            out[:, t * 128:(t + 1) * 128, :] = r["x2"][:, pi]
    return out



from concourse.bass_utils import run_bass_kernel_spmd

_NC_CACHE = {}


def _get(name, builder):
    if name not in _NC_CACHE:
        _NC_CACHE[name] = builder()
    return _NC_CACHE[name]


def kernel(L0_attn_x, L0_attn_Wq, L0_attn_Wk, L0_attn_Wv,
           L0_ffn_cn_w1, L0_ffn_cn_b1, L0_ffn_cn_w2, L0_ffn_cn_b2,
           L0_ffn_cn_w3, L0_ffn_cn_b3, L0_ffn_cn_w4, L0_ffn_cn_b4,
           L0_ffn_gc_w1, L0_ffn_gc_b1, L0_ffn_gc_w2, L0_ffn_gc_b2,
           L0_ffn_gc_w3, L0_ffn_gc_b3, L0_ffn_adaptive_range):
    x = np.asarray(L0_attn_x, dtype=np.float32)
    cores = list(range(NC))

    nc1 = _get("l1", build_l1)
    res1 = run_bass_kernel_spmd(
        nc1, l1_host_inputs(x, np.asarray(L0_attn_Wq), np.asarray(L0_attn_Wk),
                            np.asarray(L0_attn_Wv)), core_ids=cores)
    qT, kT, vT, usum = l1_host_outputs(res1)

    nc2 = _get("l2", build_l2)
    res2 = run_bass_kernel_spmd(nc2, l2_host_inputs(x, qT, kT, vT, usum),
                                core_ids=cores)
    x1, x1bf, gcp = l2_host_outputs(res2)

    W = {
        "cn_w1": np.asarray(L0_ffn_cn_w1), "cn_b1": np.asarray(L0_ffn_cn_b1),
        "cn_w2": np.asarray(L0_ffn_cn_w2), "cn_b2": np.asarray(L0_ffn_cn_b2),
        "cn_w3": np.asarray(L0_ffn_cn_w3), "cn_b3": np.asarray(L0_ffn_cn_b3),
        "cn_w4": np.asarray(L0_ffn_cn_w4), "cn_b4": np.asarray(L0_ffn_cn_b4),
        "gc_w1": np.asarray(L0_ffn_gc_w1), "gc_b1": np.asarray(L0_ffn_gc_b1),
        "gc_w2": np.asarray(L0_ffn_gc_w2), "gc_b2": np.asarray(L0_ffn_gc_b2),
        "gc_w3": np.asarray(L0_ffn_gc_w3), "gc_b3": np.asarray(L0_ffn_gc_b3),
        "adaptive_range": np.asarray(L0_ffn_adaptive_range),
    }
    nc3 = _get("l3", build_l3)
    res3 = run_bass_kernel_spmd(nc3, l3_host_inputs(x1, x1bf, gcp, W), core_ids=cores)
    return l3_host_outputs(res3)


# version 23
# speedup vs baseline: 1.0434x; 1.0039x over previous
import numpy as np
import concourse.bass as bass
import concourse.mybir as mybir

F32 = mybir.dt.float32
BF16 = mybir.dt.bfloat16
AF = mybir.ActivationFunctionType
ALU = mybir.AluOpType
AX = mybir.AxisListType

B = 2
S = 8192
D = 1024
NC = 8
TOK = 1024


class Ctr:
    def __init__(self):
        self.v = 0
    def inc(self, n=1):
        self.v += n
        return self.v



def build_l1():
    nc = bass.Bass("TRN2", target_bir_lowering=False)
    xT = nc.dram_tensor("xT", [B, 8, 128, TOK], F32, kind="ExternalInput").ap()
    wqT = nc.dram_tensor("wqT", [8, 128, D], F32, kind="ExternalInput").ap()
    wkT = nc.dram_tensor("wkT", [8, 128, D], F32, kind="ExternalInput").ap()
    wvT = nc.dram_tensor("wvT", [8, 128, D], F32, kind="ExternalInput").ap()
    qT = nc.dram_tensor("qT", [B, 8, 128, TOK], BF16, kind="ExternalOutput").ap()
    kT = nc.dram_tensor("kT", [B, 8, 128, TOK], BF16, kind="ExternalOutput").ap()
    vT = nc.dram_tensor("vT", [B, 8, 128, TOK], BF16, kind="ExternalOutput").ap()
    usum = nc.dram_tensor("usum", [128, 16], BF16, kind="ExternalOutput").ap()

    projs = [("k", wkT, kT), ("v", wvT, vT), ("q", wqT, wqT)]

    AC = {"wk0": 1, "x00": 2, "wk1": 3, "x10": 4, "x20": 5, "x30": 6,
          "wvc": 7, "wqc": 8}
    pe_c = Ctr()
    PE_AT = {}
    for pb in range(6):
        for th in range(2):
            for dc in range(8):
                pe_c.inc(); PE_AT[(pb, th, dc)] = pe_c.v
    ve_c = Ctr()
    ve_c.inc()
    VE_DM = ve_c.v
    EV = {}
    for pb in range(6):
        for th in range(2):
            for dc in range(8):
                ve_c.inc(); EV[(pb, th, dc)] = ve_c.v
                if pb in (2, 3) and th == 1:
                    ve_c.inc()
    ve_c.inc()
    VE_TOTAL = ve_c.v

    from contextlib import ExitStack
    with ExitStack() as _ctx:
        def _e(t):
            return _ctx.enter_context(t)
        x_sb = _e(nc.sbuf_tensor("x_sb", [128, B * 8 * TOK], BF16))
        x32 = _e(nc.sbuf_tensor("x32", [128, 8 * TOK], F32))
        w32a = _e(nc.sbuf_tensor("w32a", [128, 8 * D], F32))
        w32b = _e(nc.sbuf_tensor("w32b", [128, 8 * D], F32))
        w_sb0 = _e(nc.sbuf_tensor("w_sb0", [128, 8 * D], BF16))
        w_sb1 = _e(nc.sbuf_tensor("w_sb1", [128, 8 * D], BF16))
        st0 = _e(nc.sbuf_tensor("st0", [128, 8 * TOK], BF16))
        st1 = _e(nc.sbuf_tensor("st1", [128, 8 * TOK], BF16))
        us_sb = _e(nc.sbuf_tensor("us_sb", [128, 16], F32))
        usb_sb = _e(nc.sbuf_tensor("usb_sb", [128, 16], BF16))
        psb = [_e(nc.psum_tensor(f"psb{i}", [128, 512], F32)) for i in range(8)]
        g_x = [_e(nc.semaphore(f"g_x{i}")) for i in range(3)]
        s_wk0 = _e(nc.semaphore("s_wk0"))
        s_wk1 = _e(nc.semaphore("s_wk1"))
        s_wq = _e(nc.semaphore("s_wq"))
        s_w1 = _e(nc.semaphore("s_w1"))
        a_x = [_e(nc.semaphore(f"a_x{i}")) for i in range(2)]
        pe = _e(nc.semaphore("pe"))
        acts = _e(nc.semaphore("acts"))
        ev = _e(nc.semaphore("ev"))
        stA = _e(nc.semaphore("stA"))
        stB = _e(nc.semaphore("stB"))
        block = _e(nc.Block())
        w_sb = [w_sb0, w_sb1]
        stg = [st0, st1]
        stp = [stA, stB]
        XSLOT = {0: 0, 1: 1, 2: 0, 3: 1}
        XSEM = {0: a_x[0], 1: g_x[1], 2: a_x[1], 3: g_x[0]}
        XCNT = {0: 16, 1: 16, 2: 16, 3: 16}

        def xsrc(i):
            b, half = divmod(i, 2)
            return xT[b, half * 4:(half + 1) * 4].rearrange("hc p t -> p hc t")

        def xdst(i):
            s = XSLOT[i]
            return x32[:, s * 4 * TOK:(s + 1) * 4 * TOK].rearrange(
                "p (hc t) -> p hc t", hc=4)

        @block.gpsimd
        def _(g):
            g.dma_start(out=xdst(1), in_=xsrc(1)).then_inc(XSEM[1], 16)
            g.wait_ge(acts, AC["x10"])
            g.dma_start(out=xdst(3), in_=xsrc(3)).then_inc(XSEM[3], 16)

        @block.sync
        def _(sy):
            w32a3 = w32a[:, :].rearrange("p (hc d) -> p hc d", hc=8)
            w32b3 = w32b[:, :].rearrange("p (hc d) -> p hc d", hc=8)
            sy.dma_start(out=w32a[:, 0:4 * D].rearrange("p (hc d) -> p hc d", hc=4),
                         in_=wkT[0:4].rearrange("hc p d -> p hc d")).then_inc(s_wk0, 16)
            sy.dma_start(out=w32a[:, 4 * D:8 * D].rearrange("p (hc d) -> p hc d", hc=4),
                         in_=wkT[4:8].rearrange("hc p d -> p hc d")).then_inc(s_wk1, 16)
            sy.dma_start(out=w32b3,
                         in_=wvT.rearrange("hc p d -> p hc d")).then_inc(s_w1, 16)
            sy.wait_ge(acts, AC["wk1"])
            sy.dma_start(out=w32a3,
                         in_=wqT.rearrange("hc p d -> p hc d")).then_inc(s_wq, 16)
            npar = [0, 0]
            for pb in range(6):
                i, b = divmod(pb, 2)
                out = [kT, vT, qT][i]
                for h in range(4):
                    sy.wait_ge(ev, EV[(pb, 1, 2 * h + 1)])
                    sy.dma_start(out=out[b, 2 * h:2 * h + 2].rearrange("dc p t -> p dc t"),
                                 in_=stg[pb % 2][:, 2 * h * TOK:(2 * h + 2) * TOK].rearrange(
                                     "p (dc t) -> p dc t", dc=2)).then_inc(stp[pb % 2], 16)
                    npar[pb % 2] += 1
            sy.wait_ge(ev, VE_TOTAL)
            sy.dma_start(out=usum, in_=usb_sb[:, :]).then_inc(stA, 16)
            npar[0] += 1
            sy.wait_ge(stA, 16 * npar[0])
            sy.wait_ge(stB, 16 * npar[1])

        @block.scalar
        def _(s):
            sc = Ctr()

            def cp(dst, srcap):
                s.wait_ge(acts, sc.v)
                s.copy(dst, srcap).then_inc(acts, 1)
                sc.inc()

            def xcast(i):
                b, half = divmod(i, 2)
                sl = XSLOT[i]
                s.wait_ge(XSEM[i], XCNT[i])
                cp(x_sb[:, (b * 8 + half * 4) * TOK:(b * 8 + half * 4 + 4) * TOK],
                   x32[:, sl * 4 * TOK:(sl + 1) * 4 * TOK])

            s.dma_start(out=xdst(0), in_=xsrc(0)).then_inc(XSEM[0], 16)
            s.wait_ge(s_wk0, 16)
            cp(w_sb0[:, 0:4 * D], w32a[:, 0:4 * D])
            assert sc.v == AC["wk0"]
            xcast(0)
            assert sc.v == AC["x00"]
            s.wait_ge(acts, AC["x00"])
            s.dma_start(out=xdst(2), in_=xsrc(2)).then_inc(XSEM[2], 16)
            s.wait_ge(s_wk1, 16)
            cp(w_sb0[:, 4 * D:8 * D], w32a[:, 4 * D:8 * D])
            assert sc.v == AC["wk1"]
            xcast(1)
            xcast(2)
            xcast(3)
            assert sc.v == AC["x30"]
            s.wait_ge(s_w1, 16)
            cp(w_sb1[:, :], w32b[:, :])
            assert sc.v == AC["wvc"]
            s.wait_ge(pe, PE_AT[(1, 1, 7)])
            s.wait_ge(s_wq, 16)
            cp(w_sb0[:, :], w32a[:, :])
            assert sc.v == AC["wqc"]

        @block.vector
        def _(v):
            vc = Ctr()

            def op(fn, *args, **kw):
                v.wait_ge(ev, vc.v)
                fn(*args, **kw).then_inc(ev, 1)
                vc.inc()

            op(v.memset, st0[:, 0:512], 0.0)
            assert vc.v == VE_DM
            for pb in range(6):
                i, b = divmod(pb, 2)
                if pb >= 2:
                    v.wait_ge(stp[pb % 2], 64 * (pb // 2))
                for th in range(2):
                    for dc in range(8):
                        v.wait_ge(pe, PE_AT[(pb, th, dc)])
                        sgn = -1.0 if (i == 2 and b == 1) else 1.0
                        op(v.tensor_scalar,
                           stg[pb % 2][:, dc * TOK + th * 512:dc * TOK + th * 512 + 512],
                           psb[dc][:, :], sgn, None, ALU.mult)
                        assert vc.v == EV[(pb, th, dc)]
                        if i == 1 and th == 1:
                            op(v.reduce_sum, us_sb[:, b * 8 + dc:b * 8 + dc + 1],
                               stg[pb % 2][:, dc * TOK:(dc + 1) * TOK], AX.X)
            op(v.tensor_copy, usb_sb[:, :], us_sb[:, :])
            assert vc.v == VE_TOTAL

        @block.tensor
        def _(t):
            t.wait_ge(ev, VE_DM)
            for _wu in range(135):
                t.matmul(psb[0][:, :], st0[:, 0:128], st0[:, 0:512],
                         start=True, stop=True)
            for pb in range(6):
                i, b = divmod(pb, 2)
                w_ok = [AC["wk0"], AC["wk1"]] if i == 0 else (
                    [AC["wvc"]] * 2 if i == 1 else [AC["wqc"]] * 2)
                x_ok = [[AC["x00"], AC["x10"]], [AC["x20"], AC["x30"]]][b]
                for th in range(2):
                    for ph in range(2):
                        t.wait_ge(acts, max(w_ok[ph], x_ok[ph]))
                        for dc in range(8):
                            if ph == 0:
                                if (pb, th) != (0, 0):
                                    ppb, pth = (pb, 0) if th == 1 else (pb - 1, 1)
                                    t.wait_ge(ev, EV[(ppb, pth, dc)])
                            last = None
                            for hc in range(4 * ph, 4 * ph + 4):
                                last = t.matmul(
                                    psb[dc][:, :],
                                    w_sb[i % 2][:, hc * D + dc * 128:hc * D + dc * 128 + 128],
                                    x_sb[:, (b * 8 + hc) * TOK + th * 512:
                                         (b * 8 + hc) * TOK + th * 512 + 512],
                                    start=(hc == 0), stop=(hc == 7),
                                )
                            if ph == 1:
                                last.then_inc(pe, 1)
    return nc



def snake_tiles(c):
    return [c, 15 - c, 16 + c, 31 - c, 32 + c, 47 - c, 48 + c, 63 - c]


def l1_host_inputs(x, Wq, Wk, Wv):
    maps = []
    wqT = np.ascontiguousarray(Wq.T).reshape(8, 128, D)
    wkT = np.ascontiguousarray(Wk.T).reshape(8, 128, D)
    wvT = np.ascontiguousarray(Wv.T).reshape(8, 128, D)
    for c in range(NC):
        xs = x[:, c * TOK:(c + 1) * TOK, :]
        xT = np.ascontiguousarray(xs.transpose(0, 2, 1)).reshape(B, 8, 128, TOK)
        maps.append({"xT": xT, "wqT": wqT, "wkT": wkT, "wvT": wvT})
    return maps


def l1_host_outputs(res):
    import ml_dtypes
    qT = np.zeros((B, D, S), dtype=ml_dtypes.bfloat16)
    kT = np.zeros((B, D, S), dtype=ml_dtypes.bfloat16)
    vT = np.zeros((B, D, S), dtype=ml_dtypes.bfloat16)
    usum = np.zeros((B, NC, D), dtype=ml_dtypes.bfloat16)
    for c in range(NC):
        r = res.results[c]
        sl = slice(c * TOK, (c + 1) * TOK)
        qT[:, :, sl] = r["qT"].reshape(B, D, TOK)
        kT[:, :, sl] = r["kT"].reshape(B, D, TOK)
        vT[:, :, sl] = r["vT"].reshape(B, D, TOK)
        u = r["usum"]
        for b in range(B):
            usum[b, c] = u[:, b * 8:(b + 1) * 8].T.reshape(D)
    return qT, kT, vT, usum



def _pieces(w):
    out = []
    o = 0
    while o < w:
        c = min(512, w - o)
        out.append((o, c))
        o += c
    return out


def plan_l2():
    pe = Ctr(); acts = Ctr(); ve = Ctr(); st = Ctr()
    P = dict(pe_S={}, pe_V={}, pe_gc={}, acts_sig={}, acts_sq={}, acts_rstd={},
             acts_b16={}, ve_ready={}, ve_ready0={}, ve_m0={}, ve_m1={},
             ve_evac={}, ve_xadd={}, ve_vt={},
             ve_vteps={}, ve_nmr={}, ve_apply={}, ve_b16={}, ve_gcev={},
             sld_K={}, sld_V={}, gld_x={}, st_x={}, st_b16={}, st_gc={})

    xc = [Ctr(), Ctr()]
    for p in range(8):
        for b in range(B):
            idx = p * 2 + b
            xc[idx % 2].inc(16)
            P["gld_x"][(b, p)] = xc[idx % 2].v

    for a in range(8):
        for rc in range(8):
            P["sld_K"][(a, rc)] = 16 * (a + 1)
        P["sld_V"][(a, 0)] = 16 * a
        P["sld_V"][(a, 1)] = 16 * a

    ve.inc(); P["ve_dm"] = ve.v
    ve.inc(); P["ve_ones"] = ve.v

    for a in range(8):
        w = (8 - a) * 128
        for rc in range(8):
            pe.inc(); P["pe_S"][(a, rc)] = pe.v
        for rc in range(8):
            acts.inc(); P["acts_sig"][(a, rc)] = acts.v
        acts.inc(2); P["acts_rstd"][a] = acts.v
        for rc in range(8):
            ve.inc(2)
            P["ve_m0"][(a, rc)] = ve.v
            ve.inc(2)
            if w > 128:
                ve.inc()
            P["ve_m1"][(a, rc)] = ve.v
        P["ve_ready0"][a] = P["ve_m0"][(a, 7)]
        P["ve_ready"][a] = ve.v
        for b in range(B):
            for p in range(a, 8):
                pe.inc(); P["pe_V"][(a, b, p)] = pe.v
                ve.inc(); P["ve_evac"][(a, b, p)] = ve.v
                if p == a:
                    ve.inc(); P["ve_xadd"][(b, p)] = ve.v
                    ve.inc()
                    ve.inc(2)
                    P["ve_vt"][(b, p)] = ve.v
                    if b == 1:
                        P["ve_vteps"][a] = ve.v
        for b in range(B):
            ve.inc(); P["ve_nmr"][(b, a)] = ve.v
            ve.inc(); P["ve_apply"][(b, a)] = ve.v
    for p in range(8):
        ve.inc(); P["ve_b16"][(1, p)] = ve.v
    for p in range(8):
        acts.inc(); P["acts_b16"][(0, p)] = acts.v
    for b in range(B):
        pe.inc(); P["pe_gc"][b] = pe.v
        ve.inc(); P["ve_gcev"][b] = ve.v
    for a in range(8):
        for b in range(B):
            st.inc(16); P["st_x"][(b, a)] = st.v
    for b in range(B):
        st.inc(16); P["st_b16"][b] = st.v
    for b in range(B):
        st.inc(16); P["st_gc"][b] = st.v
    P["st_total"] = st.v
    P["sig_seq"] = [P["acts_sig"][(a, rc)] for a in range(8) for rc in range(8)]
    return P


def build_l2():
    nc = bass.Bass("TRN2", target_bir_lowering=False)
    qT_in = nc.dram_tensor("qT_in", [B, 8, 128, 1024], BF16, kind="ExternalInput").ap()
    K_in = nc.dram_tensor("K_in", [8, 8, B, 128, 8, 128], BF16, kind="ExternalInput").ap()
    V_in = nc.dram_tensor("V_in", [B, 8, 8, 128, 1024], BF16, kind="ExternalInput").ap()
    us_in = nc.dram_tensor("us_in", [8, B * 1024], BF16, kind="ExternalInput").ap()
    c19_in = nc.dram_tensor("c19_in", [8, 1024], BF16, kind="ExternalInput").ap()
    m1_in = nc.dram_tensor("m1_in", [128, 2048], BF16, kind="ExternalInput").ap()
    m2_in = nc.dram_tensor("m2_in", [128, 2048], BF16, kind="ExternalInput").ap()

    xq_in = nc.dram_tensor("xq_in", [B, 8, 128, 1024], F32, kind="ExternalInput").ap()
    x1 = nc.dram_tensor("x1", [B, 8, 128, 1024], F32, kind="ExternalOutput").ap()
    x1b = nc.dram_tensor("x1b", [B, 8, 128, 1024], BF16, kind="ExternalOutput").ap()
    gcp = nc.dram_tensor("gcp", [B, 1024], F32, kind="ExternalOutput").ap()

    P = plan_l2()
    EPS = 1e-5
    RAWEPS = EPS * 1024.0 * 1024.0

    from contextlib import ExitStack
    with ExitStack() as _ctx:
        def _e(x):
            return _ctx.enter_context(x)
        qT_sb = _e(nc.sbuf_tensor("qT_sb", [128, B * 8 * 1024], BF16))
        K_rg = _e(nc.sbuf_tensor("K_rg", [128, 10 * 1024], BF16))
        V_sb = _e(nc.sbuf_tensor("V_sb", [128, 2 * 8 * 1024], BF16))
        at0 = _e(nc.sbuf_tensor("at0", [128, 8 * 1024], BF16))
        at1 = _e(nc.sbuf_tensor("at1", [128, 8 * 1024], BF16))
        acc = _e(nc.sbuf_tensor("acc", [128, B * 8 * 1024], F32))
        m1m = _e(nc.sbuf_tensor("m1m", [128, 2048], BF16))
        m2m = _e(nc.sbuf_tensor("m2m", [128, 2048], BF16))

        c19_sb = _e(nc.sbuf_tensor("c19_sb", [8, 1024], BF16))
        us_sb = _e(nc.sbuf_tensor("us_sb", [8, B * 1024], BF16))
        xres = _e(nc.sbuf_tensor("xres", [128, 2 * 1024], F32))
        sq = _e(nc.sbuf_tensor("sq", [128, 1024], F32))
        mt = _e(nc.sbuf_tensor("mt", [128, 16], F32))
        v2t = _e(nc.sbuf_tensor("v2t", [128, 16], F32))
        vt = _e(nc.sbuf_tensor("vt", [128, 16], F32))
        lt = _e(nc.sbuf_tensor("lt", [128, 16], F32))
        rt = _e(nc.sbuf_tensor("rt", [128, 16], F32))
        nmr = _e(nc.sbuf_tensor("nmr", [128, 16], F32))
        tmp1 = _e(nc.sbuf_tensor("tmp1", [128, 16], F32))
        ones = _e(nc.sbuf_tensor("ones", [128, 1], F32))
        psS0 = _e(nc.psum_tensor("psS0", [128, 1024], F32))
        psS1 = _e(nc.psum_tensor("psS1", [128, 1024], F32))
        psO0 = _e(nc.psum_tensor("psO0", [128, 1024], F32))
        psO1 = _e(nc.psum_tensor("psO1", [128, 1024], F32))
        s_Krc = [_e(nc.semaphore(f"s_K{i}")) for i in range(8)]
        s_V0 = _e(nc.semaphore("s_V0"))
        s_V1 = _e(nc.semaphore("s_V1"))
        g_V00 = _e(nc.semaphore("g_V00"))
        g_V01 = _e(nc.semaphore("g_V01"))
        g_qT0 = _e(nc.semaphore("g_qT0"))
        g_qT1 = _e(nc.semaphore("g_qT1"))
        g_cst = _e(nc.semaphore("g_cst"))
        g_x0 = _e(nc.semaphore("g_x0"))
        g_x1 = _e(nc.semaphore("g_x1"))
        pe = _e(nc.semaphore("pe"))
        acts = _e(nc.semaphore("acts"))
        ve = _e(nc.semaphore("ve"))
        st = _e(nc.semaphore("st"))
        block = _e(nc.Block())
        psS = [psS0, psS1]
        psO = [psO0, psO1]

        @block.gpsimd
        def _(g):
            g_V0p = [g_V00, g_V01]
            g_xp = [g_x0, g_x1]

            def kload(a, rc):
                grp = a * 8 + rc
                if grp >= 5:
                    pa, prc = divmod(grp - 5, 8)
                    g.wait_ge(pe, P["pe_S"][(pa, prc)])
                g.dma_start(
                    out=K_rg[:, (grp % 5) * 2048:(grp % 5 + 1) * 2048].rearrange(
                        "p (b ic rr) -> p b ic rr", b=2, ic=8),
                    in_=K_in[a, rc].rearrange("b p ic rr -> p b ic rr"),
                ).then_inc(s_Krc[rc], 16)

            def xload(p, b):
                idx = p * 2 + b
                if idx >= 2:
                    pb, bb = divmod(idx - 2, 2)
                    g.wait_ge(ve, P["ve_xadd"][(bb, pb)])
                g.dma_start(out=xres[:, (idx % 2) * 1024:(idx % 2 + 1) * 1024],
                            in_=xq_in[b, p]).then_inc(g_xp[idx % 2], 16)

            g.dma_start(
                out=V_sb[:, 0:8192].rearrange("p (jc f) -> p jc f", jc=8),
                in_=V_in[0, 0].rearrange("jc p f -> p jc f")).then_inc(g_V0p[0], 16)
            for rc in (1, 3, 5, 7):
                kload(0, rc)
            xload(0, 0)
            xload(0, 1)
            g.dma_start(
                out=V_sb[:, 8192:16384].rearrange("p (jc f) -> p jc f", jc=8),
                in_=V_in[1, 0].rearrange("jc p f -> p jc f")).then_inc(g_V0p[1], 16)
            for a in range(1, 8):
                for rc in (1, 3, 5, 7):
                    kload(a, rc)
                xload(a, 0)
                xload(a, 1)

        @block.sync
        def _(sy):
            s_Vp = [s_V0, s_V1]
            sy.dma_start(out=m1m[:, :], in_=m1_in).then_inc(g_cst, 16)
            sy.dma_start(out=m2m[:, :], in_=m2_in).then_inc(g_cst, 16)
            sy.dma_start(out=us_sb[:, :], in_=us_in).then_inc(g_cst, 16)
            sy.dma_start(out=c19_sb[:, :], in_=c19_in).then_inc(g_cst, 16)
            for a in range(8):
                for rc in (0, 2, 4, 6):
                    grp = a * 8 + rc
                    if grp >= 5:
                        pa, prc = divmod(grp - 5, 8)
                        sy.wait_ge(pe, P["pe_S"][(pa, prc)])
                    sy.dma_start(
                        out=K_rg[:, (grp % 5) * 2048:(grp % 5 + 1) * 2048].rearrange(
                            "p (b ic rr) -> p b ic rr", b=2, ic=8),
                        in_=K_in[a, rc].rearrange("b p ic rr -> p b ic rr"),
                    ).then_inc(s_Krc[rc], 16)
                if a >= 1:
                    for b in range(B):
                        sy.wait_ge(pe, P["pe_V"][(a - 1, b, 7)])
                        sy.dma_start(
                            out=V_sb[:, b * 8192:(b + 1) * 8192].rearrange(
                                "p (jc f) -> p jc f", jc=8),
                            in_=V_in[b, a].rearrange("jc p f -> p jc f")
                        ).then_inc(s_Vp[b], 16)
            sy.wait_ge(st, P["st_total"])

        @block.tensor
        def _(t):
            t.wait_ge(ve, P["ve_dm"])
            for _wu in range(110):
                t.matmul(psO0[:, 0:512], at0[:, 0:128], at0[:, 0:512],
                         start=True, stop=True)
            t.wait_ge(g_qT0, 16)
            t.wait_ge(g_cst, 16 * 4)
            s_Vp = [s_V0, s_V1]
            vgrp = 0
            for a in range(8):
                w = (8 - a) * 128
                for rc in range(8):
                    gidx = a * 8 + rc
                    if gidx >= 2:
                        t.wait_ge(acts, P["sig_seq"][gidx - 2])
                    t.wait_ge(s_Krc[rc], P["sld_K"][(a, rc)])
                    grp = a * 8 + rc
                    p2 = psS[rc % 2]
                    last = None
                    for b in range(B):
                        if gidx == 0 and b == 1:
                            t.wait_ge(g_qT1, 16)
                        for (off, pw) in _pieces(w):
                            for ic in range(8):
                                last = t.matmul(
                                    p2[:, off:off + pw],
                                    K_rg[:, (grp % 5) * 2048 + b * 1024 + ic * 128:
                                         (grp % 5) * 2048 + b * 1024 + ic * 128 + 128],
                                    qT_sb[:, (b * 8 + ic) * 1024 + a * 128 + off:
                                          (b * 8 + ic) * 1024 + a * 128 + off + pw],
                                    start=(b == 0 and ic == 0),
                                    stop=(b == 1 and ic == 7),
                                )
                    last.then_inc(pe, 1)
                for b in range(B):
                    if a == 0:
                        t.wait_ge([g_V00, g_V01][b], 16)
                    else:
                        t.wait_ge(s_Vp[b], P["sld_V"][(a, b)])
                    mk = P["ve_m0"] if b == 0 else P["ve_m1"]
                    for p in range(a, 8):
                        if vgrp >= 2:
                            t.wait_ge(ve, _ve_evac_by_idx(P, vgrp - 2))
                        po = psO[vgrp % 2]
                        att = at0 if b == 0 else at1
                        last = None
                        for rc in range(8):
                            if p == a:
                                t.wait_ge(ve, mk[(a, rc)])
                            for foff in (0, 512):
                                last = t.matmul(
                                    po[:, foff:foff + 512],
                                    att[:, rc * 1024 + (p - a) * 128:
                                        rc * 1024 + (p - a) * 128 + 128],
                                    V_sb[:, (b * 8 + rc) * 1024 + foff:
                                         (b * 8 + rc) * 1024 + foff + 512],
                                    start=(rc == 0), stop=(rc == 7 and a != 0),
                                )
                        if a == 0:
                            for foff in (0, 512):
                                last = t.matmul(
                                    po[:, foff:foff + 512],
                                    c19_sb[:, p * 128:(p + 1) * 128],
                                    us_sb[:, b * 1024 + foff:b * 1024 + foff + 512],
                                    start=False, stop=True,
                                )
                        last.then_inc(pe, 1)
                        vgrp += 1
            t.wait_ge(ve, P["ve_evac"][(7, 0, 7)])
            for b in range(B):
                if b >= 1:
                    t.wait_ge(ve, P["ve_gcev"][0])
                last = None
                for p in range(8):
                    t.wait_ge(ve, P["ve_apply"][(b, p)])
                    for foff in (0, 512):
                        last = t.matmul(
                            psO0[0:1, foff:foff + 512],
                            ones[:, 0:1],
                            acc[:, (b * 8 + p) * 1024 + foff:(b * 8 + p) * 1024 + foff + 512],
                            start=(p == 0), stop=(p == 7),
                        )
                last.then_inc(pe, 1)

        @block.scalar
        def _(s):
            s.dma_start(out=qT_sb[:, 0:8192].rearrange("p (ic m) -> p ic m", ic=8),
                        in_=qT_in[0].rearrange("ic p m -> p ic m")).then_inc(g_qT0, 16)
            s.dma_start(out=qT_sb[:, 8192:16384].rearrange("p (ic m) -> p ic m", ic=8),
                        in_=qT_in[1].rearrange("ic p m -> p ic m")).then_inc(g_qT1, 16)
            sc = Ctr()
            for a in range(8):
                w = (8 - a) * 128
                for rc in range(8):
                    s.wait_ge(acts, sc.v)
                    s.wait_ge(pe, P["pe_S"][(a, rc)])
                    s.activation(at0[:, rc * 1024:rc * 1024 + w],
                                 psS[rc % 2][:, 0:w], AF.Sigmoid,
                                 scale=1.0 / 32.0).then_inc(acts, 1)
                    sc.inc()
                s.wait_ge(acts, sc.v)
                s.wait_ge(ve, P["ve_vteps"][a])
                s.activation(lt[:, 2 * a:2 * a + 2], vt[:, 2 * a:2 * a + 2],
                             AF.Ln, scale=2.0 ** -20).then_inc(acts, 1)
                sc.inc()
                s.wait_ge(acts, sc.v)
                s.activation(rt[:, 2 * a:2 * a + 2], lt[:, 2 * a:2 * a + 2],
                             AF.Exp, scale=-0.5).then_inc(acts, 1)
                sc.inc()
                for b in range(B):
                    s.wait_ge(ve, P["ve_apply"][(b, a)])
                    s.dma_start(out=x1[b, a],
                                in_=acc[:, (b * 8 + a) * 1024:(b * 8 + a + 1) * 1024]
                                ).then_inc(st, 16)
            s.wait_ge(pe, P["pe_V"][(7, 0, 7)])
            for p in range(8):
                s.wait_ge(acts, sc.v)
                s.wait_ge(ve, P["ve_apply"][(0, p)])
                s.copy(at0[:, p * 1024:(p + 1) * 1024],
                       acc[:, p * 1024:(p + 1) * 1024]).then_inc(acts, 1)
                sc.inc()
            s.wait_ge(acts, sc.v)
            s.dma_start(out=x1b[0].rearrange("p ii f -> ii p f"),
                        in_=at0[:, :].rearrange("ii (p f) -> ii p f", p=8)
                        ).then_inc(st, 16)
            s.wait_ge(ve, P["ve_b16"][(1, 7)])
            s.dma_start(out=x1b[1].rearrange("p ii f -> ii p f"),
                        in_=at1[:, :].rearrange("ii (p f) -> ii p f", p=8)
                        ).then_inc(st, 16)
            for b in range(B):
                s.wait_ge(ve, P["ve_gcev"][b])
                s.dma_start(out=gcp[b:b + 1, :],
                            in_=sq[0:1, 0:1024]).then_inc(st, 16)

        @block.vector
        def _(v):
            vc = Ctr()

            def op(fn, *args, **kw):
                v.wait_ge(ve, vc.v)
                fn(*args, **kw).then_inc(ve, 1)
                vc.inc()

            op(v.memset, at0[:, 0:512], 0.0)
            op(v.memset, ones[:, :], 1.0)
            v.wait_ge(g_cst, 16 * 4)
            g_xp = [g_x0, g_x1]
            for a in range(8):
                w = (8 - a) * 128
                par = a % 2
                for rc in range(8):
                    v.wait_ge(acts, P["acts_sig"][(a, rc)])
                    sl0 = at0[:, rc * 1024:rc * 1024 + 128]
                    msl = slice(par * 1024 + rc * 128, par * 1024 + rc * 128 + 128)
                    op(v.tensor_tensor, sl0, sl0, m1m[:, msl], ALU.mult)
                    op(v.tensor_tensor, sl0, sl0, m2m[:, msl], ALU.add)
                    assert vc.v == P["ve_m0"][(a, rc)]
                    op(v.tensor_tensor, at1[:, rc * 1024:rc * 1024 + 128],
                       m1m[:, msl], sl0, ALU.subtract)
                    op(v.scalar_tensor_tensor, at1[:, rc * 1024:rc * 1024 + 128],
                       m2m[:, msl], 2.0, at1[:, rc * 1024:rc * 1024 + 128],
                       ALU.mult, ALU.add)
                    if w > 128:
                        op(v.tensor_scalar, at1[:, rc * 1024 + 128:rc * 1024 + w],
                       at0[:, rc * 1024 + 128:rc * 1024 + w], -1.0, 1.0,
                       ALU.mult, ALU.add)
                assert vc.v == P["ve_ready"][a]
                for b in range(B):
                    for p in range(a, 8):
                        v.wait_ge(pe, P["pe_V"][(a, b, p)])
                        asl = acc[:, (b * 8 + p) * 1024:(b * 8 + p + 1) * 1024]
                        po = psO[_vgrp_idx(a, b, p) % 2]
                        if a == 0:
                            op(v.tensor_copy, asl, po[:, :])
                        else:
                            op(v.tensor_tensor, asl, asl, po[:, :], ALU.add)
                        assert vc.v == P["ve_evac"][(a, b, p)]
                        if p == a:
                            idx = p * 2 + b
                            col = slice(2 * a + b, 2 * a + b + 1)
                            v.wait_ge(g_xp[idx % 2], P["gld_x"][(b, p)])
                            op(v.scalar_tensor_tensor, asl,
                               xres[:, (idx % 2) * 1024:(idx % 2 + 1) * 1024],
                               1.0, asl, ALU.mult, ALU.add, accum_out=mt[:, col])
                            assert vc.v == P["ve_xadd"][(b, p)]
                            op(v.scalar_tensor_tensor, sq[:, :], asl, 1.0, asl,
                               ALU.mult, ALU.mult, accum_out=v2t[:, col])
                            op(v.tensor_scalar, tmp1[:, col], mt[:, col],
                               mt[:, col], RAWEPS, ALU.mult, ALU.subtract)
                            op(v.scalar_tensor_tensor, vt[:, col], v2t[:, col],
                               1024.0, tmp1[:, col], ALU.mult, ALU.subtract)
                            assert vc.v == P["ve_vt"][(b, p)]
                            if b == 1:
                                assert vc.v == P["ve_vteps"][a]
                v.wait_ge(acts, P["acts_rstd"][a])
                for b in range(B):
                    col = slice(2 * a + b, 2 * a + b + 1)
                    op(v.tensor_scalar, nmr[:, col], mt[:, col], rt[:, col],
                       -1.0 / 1024.0, ALU.mult, ALU.mult)
                    assert vc.v == P["ve_nmr"][(b, a)]
                    asl = acc[:, (b * 8 + a) * 1024:(b * 8 + a + 1) * 1024]
                    op(v.tensor_scalar, asl, asl, rt[:, col], nmr[:, col],
                       ALU.mult, ALU.add)
                    assert vc.v == P["ve_apply"][(b, a)]
            for p in range(8):
                op(v.tensor_copy, at1[:, p * 1024:(p + 1) * 1024],
                   acc[:, (8 + p) * 1024:(9 + p) * 1024])
                assert vc.v == P["ve_b16"][(1, p)]
            for b in range(B):
                v.wait_ge(pe, P["pe_gc"][b])
                if b >= 1:
                    v.wait_ge(st, P["st_gc"][0])
                op(v.tensor_copy, sq[0:1, 0:1024], psO0[0:1, :])
                assert vc.v == P["ve_gcev"][b]
    return nc


def _vgrp_idx(a, b, p):
    i = 0
    for aa in range(a):
        i += 2 * (8 - aa)
    return i + b * (8 - a) + (p - a)


def _ve_evac_by_idx(P, vgrp):
    for (a, b, p), val in P["ve_evac"].items():
        if _vgrp_idx(a, b, p) == vgrp:
            return val
    raise KeyError(vgrp)


def l2_host_inputs(x, qT, kT, vT, usum):
    import ml_dtypes
    bf = ml_dtypes.bfloat16
    k_full = np.ascontiguousarray(qT[0:0]).reshape(0,)
    k_tok = np.ascontiguousarray(kT.transpose(0, 2, 1))
    v_tok = np.ascontiguousarray(vT.transpose(0, 2, 1))
    K_in = np.zeros((8, 8, B, 128, 8, 128), dtype=bf)
    for a in range(8):
        ks = k_tok[:, a::8, :].reshape(B, 8, 128, 8, 128)
        K_in[a] = ks.transpose(3, 0, 2, 1, 4)
    V_in = v_tok.reshape(B, 8, 8, 128, 1024)
    us_in = np.ascontiguousarray(usum.transpose(1, 0, 2)).reshape(8, B * 1024)
    c19 = np.zeros((8, 1024), dtype=bf)
    for A in range(8):
        for p in range(8):
            if A > p:
                c19[A, p * 128:(p + 1) * 128] = 1e-9
    maps = []
    for c in range(NC):
        rows = np.concatenate([np.arange(t * 128, (t + 1) * 128)
                               for t in snake_tiles(c)])
        qTc = np.ascontiguousarray(qT[:, :, rows]).reshape(B, 8, 128, 1024)
        xqc = np.ascontiguousarray(x[:, rows, :]).reshape(B, 8, 128, 1024)
        m1 = np.zeros((128, 2048), dtype=np.float32)
        for par, tm8 in ((0, c), (1, 7 - c)):
            for rc in range(8):
                slc = slice(par * 1024 + rc * 128, par * 1024 + rc * 128 + 128)
                if rc < tm8:
                    m1[:, slc] = 1.0
                elif rc == tm8:
                    jj = np.arange(128)[:, None]
                    mm = np.arange(128)[None, :]
                    m1[:, slc] = (jj <= mm).astype(np.float32)
        m2 = 1e-9 * (1.0 - m1)
        maps.append({
            "qT_in": qTc, "K_in": K_in, "V_in": V_in, "us_in": us_in,
            "c19_in": c19, "m1_in": m1.astype(bf), "m2_in": m2.astype(bf),
            "xq_in": xqc,
        })
    return maps


def l2_host_outputs(res):
    import ml_dtypes
    x1 = np.zeros((B, S, D), dtype=np.float32)
    x1bf = np.zeros((B, S, D), dtype=ml_dtypes.bfloat16)
    gcp = np.zeros((NC, B, D), dtype=np.float32)
    for c in range(NC):
        r = res.results[c]
        for pi, t in enumerate(snake_tiles(c)):
            x1[:, t * 128:(t + 1) * 128, :] = r["x1"][:, pi]
            x1bf[:, t * 128:(t + 1) * 128, :] = r["x1b"][:, pi]
        gcp[c] = r["gcp"].reshape(B, D)
    return x1, x1bf, gcp



def plan_l3():
    pe = Ctr(); acts = Ctr(); ve = Ctr(); gld = Ctr(); sld = Ctr(); st = Ctr()
    P = {}
    xp = [Ctr(), Ctr(), Ctr(), Ctr()]
    for b in range(B):
        for p in range(8):
            idx = b * 8 + p
            xp[idx % 4].inc(16)
            P["sld_x1s", b, p] = xp[idx % 4].v

    ve.inc(1); P["ve_dm"] = ve.v
    ve.inc(1); P["ve_ones"] = ve.v
    ve.inc(9); P["ve_gc16b"] = ve.v

    for oc in range(8):
        pe.inc(); P["pe_h1", 0, 0, oc] = pe.v
    for b in range(B):
        for oc in range(4):
            pe.inc(); P["pe_g1", b, oc] = pe.v
        for oc in range(2):
            pe.inc(); P["pe_g2", b, oc] = pe.v
        pe.inc(); P["pe_gf", b] = pe.v
    pe.inc(); P["pe_bc"] = pe.v
    for l in range(3):
        for b in range(B):
            if l > 0:
                for g in range(16):
                    pe.inc(); P["pe_tp", b, l, g] = pe.v
            for oc in range(8):
                if (b, l) != (0, 0):
                    pe.inc(); P["pe_h1", b, l, oc] = pe.v
            for oc in range(4):
                pe.inc(); P["pe_h2", b, l, oc] = pe.v
            for oc in range(2):
                pe.inc(); P["pe_h3", b, l, oc] = pe.v
            pe.inc(); P["pe_tf", b, l] = pe.v
    for oc in range(8):
        acts.inc(); P["acts_h1", 0, 0, oc] = acts.v
    for b in range(B):
        for oc in range(4):
            acts.inc(); P["acts_g1", b, oc] = acts.v
        for oc in range(2):
            acts.inc(); P["acts_g2", b, oc] = acts.v
    for b in range(B):
        acts.inc(); P["acts_gf", b] = acts.v
    ve.inc(); P["ve_bc"] = ve.v
    ve.inc(7); P["ve_AB"] = ve.v
    def _plan_hacts(b, l):
        for oc in range(8):
            if (b, l) != (0, 0):
                acts.inc(); P["acts_h1", b, l, oc] = acts.v
        for oc in range(4):
            acts.inc(); P["acts_h2", b, l, oc] = acts.v
        for oc in range(2):
            acts.inc(); P["acts_h3", b, l, oc] = acts.v

    def _plan_sig(b, l):
        acts.inc(); P["acts_tf", b, l] = acts.v

    for l in range(2):
        _plan_hacts(0, l); _plan_sig(0, l)
        _plan_hacts(1, l); _plan_sig(1, l)
    _plan_hacts(0, 2); _plan_sig(0, 2)
    for oc in range(8):
        acts.inc(); P["acts_h1", 1, 2, oc] = acts.v
    acts.inc(2); P["acts_rstd2", 0] = acts.v
    for oc in range(4):
        acts.inc(); P["acts_h2", 1, 2, oc] = acts.v
    for oc in range(2):
        acts.inc(); P["acts_h3", 1, 2, oc] = acts.v
    _plan_sig(1, 2)
    for p in range(8):
        acts.inc(2); P["acts_rstd1", p] = acts.v
    def _plan_CE(b, l):
        ve.inc(); P["ve_obf", b, l] = ve.v
        for g in range(16):
            ve.inc(); P["ve_tpev", b, l, g] = ve.v

    def _plan_T(b, l):
        ve.inc(2); P["ve_coup", b, l] = ve.v
        for p in range(8):
            ve.inc(3); P["ve_twc", b, l, p] = ve.v
        P["ve_twist", b, l] = ve.v

    po = Ctr()
    P["po"] = po

    def _plan_LN2(b):
        if b == 0:
            for p in range(8):
                po.inc(); P["po_s2add", 0, p] = po.v
                po.inc(); P["po_sq", 0, p] = po.v
                ve.inc()
                ve.inc(3)
                P["ve_vt", 0, p] = ve.v
            P["ve_vt2eps", 0] = ve.v
            ve.inc(2)
            P["ve_nmrb", 0] = ve.v
            for p in range(8):
                ve.inc(); P["ve_apply", 0, p] = ve.v
        else:
            for p in range(8):
                ve.inc(); P["ve_s2add1", p] = ve.v
                if p >= 1:
                    ve.inc(3)
                    P["ve_vt", 1, p - 1] = ve.v
                    P["ve_vteps1", p - 1] = ve.v
                if p >= 2:
                    ve.inc(2)
                    P["ve_apply", 1, p - 2] = ve.v
            ve.inc(3); P["ve_vt", 1, 7] = ve.v
            P["ve_vteps1", 7] = ve.v
            ve.inc(2); P["ve_apply", 1, 6] = ve.v
            ve.inc(2); P["ve_apply", 1, 7] = ve.v

    _plan_T(0, 0)
    _plan_CE(0, 1)
    _plan_T(1, 0)
    _plan_CE(1, 1)
    _plan_T(0, 1)
    _plan_CE(0, 2)
    _plan_T(1, 1)
    _plan_CE(1, 2)
    _plan_T(0, 2)
    _plan_LN2(0)
    _plan_T(1, 2)
    _plan_LN2(1)
    st.inc(16 * 9)
    P["st_total"] = st.v
    seq = [("acts_h1", 0, 0, oc) for oc in range(8)]
    for l in range(3):
        for b in range(B):
            if (b, l) != (0, 0):
                seq += [("acts_h1", b, l, oc) for oc in range(8)]
            seq += [("acts_h2", b, l, oc) for oc in range(4)]
            seq += [("acts_h3", b, l, oc) for oc in range(2)]
    P["psh_prev"] = [P[k] for k in seq]
    return P


def build_l3():
    nc = bass.Bass("TRN2", target_bir_lowering=False)
    x1_in = nc.dram_tensor("x1_in", [B, 8, 128, 1024], F32, kind="ExternalInput").ap()
    x1T_in = nc.dram_tensor("x1T_in", [B, 8, 128, 1024], BF16, kind="ExternalInput").ap()
    gcT_in = nc.dram_tensor("gcT_in", [128, 128], F32, kind="ExternalInput").ap()
    w1_in = nc.dram_tensor("w1_in", [8, 128, 1024], F32, kind="ExternalInput").ap()
    w2_in = nc.dram_tensor("w2_in", [8, 128, 512], F32, kind="ExternalInput").ap()
    w3_in = nc.dram_tensor("w3_in", [4, 128, 256], F32, kind="ExternalInput").ap()
    w4_in = nc.dram_tensor("w4_in", [128, 2], F32, kind="ExternalInput").ap()
    gw1_in = nc.dram_tensor("gw1_in", [8, 128, 512], F32, kind="ExternalInput").ap()
    gw2_in = nc.dram_tensor("gw2_in", [4, 128, 256], F32, kind="ExternalInput").ap()
    gw3_in = nc.dram_tensor("gw3_in", [128, 2], F32, kind="ExternalInput").ap()
    sml_in = nc.dram_tensor("sml_in", [128, 20], F32, kind="ExternalInput").ap()
    b4_in = nc.dram_tensor("b4_in", [1, 1], F32, kind="ExternalInput").ap()
    gb3_in = nc.dram_tensor("gb3_in", [1, 1], F32, kind="ExternalInput").ap()
    ar_in = nc.dram_tensor("ar_in", [1, 1], F32, kind="ExternalInput").ap()
    ident_in = nc.dram_tensor("ident_in", [128, 128], BF16, kind="ExternalInput").ap()
    x2 = nc.dram_tensor("x2", [B, 8, 128, 1024], F32, kind="ExternalOutput").ap()

    P = plan_l3()
    EPS = 1e-5
    RAWEPS = EPS * 1024.0 * 1024.0
    from contextlib import ExitStack
    with ExitStack() as _ctx:
        def _e(x):
            return _ctx.enter_context(x)
        w1s = _e(nc.sbuf_tensor("w1s", [128, 8 * 1024], BF16))
        w2s = _e(nc.sbuf_tensor("w2s", [128, 8 * 512], BF16))
        w3s = _e(nc.sbuf_tensor("w3s", [128, 4 * 256], BF16))
        w4s = _e(nc.sbuf_tensor("w4s", [128, 2], BF16))
        gw1s = _e(nc.sbuf_tensor("gw1s", [128, 8 * 512], BF16))
        gw2s = _e(nc.sbuf_tensor("gw2s", [128, 4 * 256], BF16))
        gw3s = _e(nc.sbuf_tensor("gw3s", [128, 2], BF16))
        sml_sb = _e(nc.sbuf_tensor("sml_sb", [128, 20], F32))
        gb3s = _e(nc.sbuf_tensor("gb3s", [1, 1], F32))
        gcTs = _e(nc.sbuf_tensor("gcTs", [128, 128], F32))
        gc16 = _e(nc.sbuf_tensor("gc16", [128, 16], F32))
        gc16b = _e(nc.sbuf_tensor("gc16b", [128, 16], BF16))
        idents = _e(nc.sbuf_tensor("idents", [128, 128], BF16))
        sc4 = _e(nc.sbuf_tensor("sc4", [1, 4], F32))
        ones1 = _e(nc.sbuf_tensor("ones1", [1, 128], F32))
        bc = _e(nc.sbuf_tensor("bc", [128, 4], F32))
        Ab = _e(nc.sbuf_tensor("Ab", [128, 2], F32))
        Bc = _e(nc.sbuf_tensor("Bc", [128, 1], F32))
        tAB = _e(nc.sbuf_tensor("tAB", [128, 2], F32))
        g1 = _e(nc.sbuf_tensor("g1", [128, 4], BF16))
        g2 = _e(nc.sbuf_tensor("g2", [128, 2], BF16))
        out0 = _e(nc.sbuf_tensor("out0", [128, 8 * 1024], F32))
        out1 = _e(nc.sbuf_tensor("out1", [128, 8 * 1024], F32))
        outT0 = _e(nc.sbuf_tensor("outT0", [128, 8 * 1024], BF16))
        outT1 = _e(nc.sbuf_tensor("outT1", [128, 8 * 1024], BF16))
        obf = _e(nc.sbuf_tensor("obf", [128, 8 * 1024], BF16))
        h1T = _e(nc.sbuf_tensor("h1T", [128, 8 * 1024], BF16))
        h2T = _e(nc.sbuf_tensor("h2T", [128, 4 * 1024], BF16))
        h3T = _e(nc.sbuf_tensor("h3T", [128, 2 * 1024], BF16))
        x1s = _e(nc.sbuf_tensor("x1s", [128, 4 * 1024], F32))
        tfs = _e(nc.sbuf_tensor("tfs", [128, 2 * 8], F32))
        coup = _e(nc.sbuf_tensor("coup", [128, 2 * 8], F32))
        coupn = _e(nc.sbuf_tensor("coupn", [128, 2 * 8], F32))
        wtmp = _e(nc.sbuf_tensor("wtmp", [128, 1024], F32))
        sq2 = _e(nc.sbuf_tensor("sq2", [128, 2 * 1024], F32))
        mt2 = _e(nc.sbuf_tensor("mt2", [128, 16], F32))
        v2t2 = _e(nc.sbuf_tensor("v2t2", [128, 16], F32))
        vt2 = _e(nc.sbuf_tensor("vt2", [128, 16], F32))
        lt2 = _e(nc.sbuf_tensor("lt2", [128, 16], F32))
        rt2 = _e(nc.sbuf_tensor("rt2", [128, 16], F32))
        nmr2 = _e(nc.sbuf_tensor("nmr2", [128, 16], F32))
        tsm = _e(nc.sbuf_tensor("tsm", [128, 16], F32))
        psH0 = _e(nc.psum_tensor("psH0", [128, 1024], F32))
        psH1 = _e(nc.psum_tensor("psH1", [128, 1024], F32))
        psE0 = _e(nc.psum_tensor("psE0", [128, 512], BF16))
        psE1 = _e(nc.psum_tensor("psE1", [128, 512], BF16))
        psG = _e(nc.psum_tensor("psG", [128, 32], F32))
        g_w1a = _e(nc.semaphore("g_w1a"))
        g_w1b = _e(nc.semaphore("g_w1b"))
        g_sm = _e(nc.semaphore("g_sm"))
        g_gc = _e(nc.semaphore("g_gc"))
        g_w2 = _e(nc.semaphore("g_w2"))
        s_oT0 = _e(nc.semaphore("s_oT0"))
        s_oT1 = _e(nc.semaphore("s_oT1"))
        g_sm2 = _e(nc.semaphore("g_sm2"))
        st_g = _e(nc.semaphore("st_g"))
        s_ob0 = _e(nc.semaphore("s_ob0"))
        s_ob1 = _e(nc.semaphore("s_ob1"))
        s_x10 = _e(nc.semaphore("s_x10"))
        s_x11 = _e(nc.semaphore("s_x11"))
        s_x12 = _e(nc.semaphore("s_x12"))
        s_x13 = _e(nc.semaphore("s_x13"))
        po = _e(nc.semaphore("po"))
        pe = _e(nc.semaphore("pe"))
        acts = _e(nc.semaphore("acts"))
        ve = _e(nc.semaphore("ve"))
        st = _e(nc.semaphore("st"))
        block = _e(nc.Block())
        psH = [psH0, psH1]
        psE = [psE0, psE1]
        outb = [out0, out1]
        outT = [outT0, outT1]

        @block.gpsimd
        def _(g):
            g.dma_start(out=w1s[:, 0:4096].rearrange("p (o c) -> p o c", o=4),
                        in_=w1_in[0:4].rearrange("o p c -> p o c")).then_inc(g_w1a, 16)
            g.dma_start(out=w1s[:, 4096:8192].rearrange("p (o c) -> p o c", o=4),
                        in_=w1_in[4:8].rearrange("o p c -> p o c")).then_inc(g_w1b, 16)
            g.dma_start(out=gw1s[:, :].rearrange("p (hc d) -> p hc d", hc=8),
                        in_=gw1_in.rearrange("hc p d -> p hc d")).then_inc(g_gc, 16)
            g.dma_start(out=gw2s[:, :].rearrange("p (hc d) -> p hc d", hc=4),
                        in_=gw2_in.rearrange("hc p d -> p hc d")).then_inc(g_gc, 16)
            g.dma_start(out=gw3s[:, :], in_=gw3_in).then_inc(g_gc, 16)
            g.dma_start(out=idents[:, :], in_=ident_in).then_inc(g_gc, 16)
            g.dma_start(out=gcTs[:, :], in_=gcT_in).then_inc(g_gc, 16)
            g.dma_start(out=w2s[:, :].rearrange("p (hc d) -> p hc d", hc=8),
                        in_=w2_in.rearrange("hc p d -> p hc d")).then_inc(g_w2, 16)
            g.dma_start(out=w3s[:, :].rearrange("p (hc d) -> p hc d", hc=4),
                        in_=w3_in.rearrange("hc p d -> p hc d")).then_inc(g_w2, 16)
            g.dma_start(out=w4s[:, :], in_=w4_in).then_inc(g_w2, 16)
            for p in range(8):
                osl = outb[0][:, p * 1024:(p + 1) * 1024]
                g.wait_ge(ve, P["ve_twc", 0, 2, p])
                g.wait_ge([s_x10, s_x11, s_x12, s_x13][p % 4], P["sld_x1s", 0, p])
                if p > 0:
                    g.wait_ge(po, P["po_sq", 0, p - 1])
                g.tensor_tensor(osl, osl,
                                x1s[:, (p % 4) * 1024:(p % 4 + 1) * 1024],
                                ALU.add).then_inc(po, 1)
                if p >= 2:
                    g.wait_ge(ve, P["ve_vt", 0, p - 2])
                g.wait_ge(po, P["po_s2add", 0, p])
                g.tensor_tensor(sq2[:, (p % 2) * 1024:(p % 2 + 1) * 1024],
                                osl, osl, ALU.mult).then_inc(po, 1)

        @block.sync
        def _(sy):
            s_x1p = [s_x10, s_x11, s_x12, s_x13]
            sy.dma_start(out=sml_sb[:, :], in_=sml_in).then_inc(g_sm, 16)
            sy.dma_start(out=outT[0][:, :].rearrange("p (fc t) -> p fc t", fc=8),
                         in_=x1T_in[0].rearrange("fc p t -> p fc t")).then_inc(s_oT0, 16)
            sy.dma_start(out=outb[0][:, :].rearrange("ii (p f) -> ii p f", p=8),
                         in_=x1_in[0].rearrange("p ii f -> ii p f")).then_inc(s_ob0, 16)
            sy.dma_start(out=outT[1][:, :].rearrange("p (fc t) -> p fc t", fc=8),
                         in_=x1T_in[1].rearrange("fc p t -> p fc t")).then_inc(s_oT1, 16)
            sy.dma_start(out=outb[1][:, :].rearrange("ii (p f) -> ii p f", p=8),
                         in_=x1_in[1].rearrange("p ii f -> ii p f")).then_inc(s_ob1, 16)
            sy.dma_start(out=sc4[0:1, 0:1], in_=b4_in).then_inc(g_sm2, 16)
            sy.dma_start(out=sc4[0:1, 1:2], in_=ar_in).then_inc(g_sm2, 16)
            sy.dma_start(out=gb3s[0:1, 0:1], in_=gb3_in).then_inc(g_sm2, 16)

            def x1load(idx):
                b, p = divmod(idx, 8)
                if idx >= 4:
                    pb, pp = divmod(idx - 4, 8)
                    if pb == 0:
                        sy.wait_ge(po, P["po_s2add", 0, pp])
                    else:
                        sy.wait_ge(ve, P["ve_s2add1", pp])
                sy.dma_start(out=x1s[:, (idx % 4) * 1024:(idx % 4 + 1) * 1024],
                             in_=x1_in[b, p]).then_inc(s_x1p[idx % 4], 16)

            for idx in range(12):
                x1load(idx)
            sy.wait_ge(ve, P["ve_apply", 0, 7])
            sy.dma_start(out=x2[0].rearrange("p ii f -> ii p f"),
                         in_=outb[0][:, :].rearrange("ii (p f) -> ii p f", p=8)
                         ).then_inc(st, 16)
            for idx in range(12, 16):
                x1load(idx)
            for p in (0, 2, 4, 6):
                sy.wait_ge(ve, P["ve_apply", 1, p])
                sy.dma_start(out=x2[1, p],
                             in_=out1[:, p * 1024:(p + 1) * 1024]).then_inc(st, 16)
            sy.wait_ge(st, 16 * 5)
            sy.wait_ge(st_g, 16 * 4)

        @block.tensor
        def _(t):
            pshi = [0]

            def psh_wait():
                i = pshi[0]
                if i >= 2:
                    t.wait_ge(acts, P["psh_prev"][i - 2])
                pshi[0] = i + 1

            t.wait_ge(ve, P["ve_dm"])
            for _wu in range(110):
                t.matmul(psH0[:, 0:512], obf[:, 0:128], obf[:, 0:512],
                         start=True, stop=True)
            t.wait_ge(g_w1a, 16)
            t.wait_ge(s_oT0, 16)
            for oc in range(8):
                if oc == 4:
                    t.wait_ge(g_w1b, 16)
                psh_wait()
                last = None
                for ic in range(8):
                    for pc in (0, 512):
                        last = t.matmul(
                            psH[oc % 2][:, pc:pc + 512],
                            w1s[:, oc * 1024 + ic * 128:oc * 1024 + ic * 128 + 128],
                            outT[0][:, ic * 1024 + pc:ic * 1024 + pc + 512],
                            start=(ic == 0), stop=(ic == 7))
                last.then_inc(pe, 1)
            t.wait_ge(g_gc, 16 * 5)
            t.wait_ge(ve, P["ve_gc16b"])
            for b in range(B):
                for oc in range(4):
                    last = None
                    for ic in range(8):
                        last = t.matmul(psG[:, oc:oc + 1],
                                        gw1s[:, ic * 512 + oc * 128:ic * 512 + oc * 128 + 128],
                                        gc16b[:, b * 8 + ic:b * 8 + ic + 1],
                                        start=(ic == 0), stop=(ic == 7))
                    last.then_inc(pe, 1)
                for oc in range(2):
                    t.wait_ge(acts, P["acts_g1", b, 3])
                    last = None
                    for ic in range(4):
                        last = t.matmul(psG[:, 4 + oc:5 + oc],
                                        gw2s[:, ic * 256 + oc * 128:ic * 256 + oc * 128 + 128],
                                        g1[:, ic:ic + 1],
                                        start=(ic == 0), stop=(ic == 3))
                    last.then_inc(pe, 1)
                t.wait_ge(acts, P["acts_g2", b, 1])
                last = None
                for icc in range(2):
                    last = t.matmul(psG[0:1, 6 + b:7 + b],
                                    gw3s[:, icc:icc + 1], g2[:, icc:icc + 1],
                                    start=(icc == 0), stop=(icc == 1))
                last.then_inc(pe, 1)
            t.wait_ge(acts, P["acts_gf", 1])
            t.wait_ge(g_sm2, 16 * 3)
            t.wait_ge(ve, P["ve_ones"])
            t.matmul(psG[:, 8:12], ones1[0:1, :], sc4[0:1, :],
                     start=True, stop=True).then_inc(pe, 1)
            t.wait_ge(g_w2, 16 * 3)
            for l in range(3):
                for b in range(B):
                    if l > 0:
                        t.wait_ge(ve, P["ve_obf", b, l])
                        gi = 0
                        for fc in range(8):
                            for pg in range(2):
                                if gi >= 2:
                                    t.wait_ge(ve, P["ve_tpev", b, l, gi - 2])
                                last = None
                                for j in range(4):
                                    p = pg * 4 + j
                                    last = t.transpose(
                                        psE[gi % 2][:, j * 128:(j + 1) * 128],
                                        obf[:, p * 1024 + fc * 128:p * 1024 + fc * 128 + 128],
                                        idents[:, :])
                                last.then_inc(pe, 1)
                                gi += 1
                        t.wait_ge(ve, P["ve_tpev", b, l, 15])
                    else:
                        if b == 1:
                            t.wait_ge(s_oT1, 16)
                    if (b, l) != (0, 0):
                        for oc in range(8):
                            psh_wait()
                            last = None
                            for ic in range(8):
                                for pc in (0, 512):
                                    last = t.matmul(
                                        psH[oc % 2][:, pc:pc + 512],
                                        w1s[:, oc * 1024 + ic * 128:oc * 1024 + ic * 128 + 128],
                                        outT[b][:, ic * 1024 + pc:ic * 1024 + pc + 512],
                                        start=(ic == 0), stop=(ic == 7))
                            last.then_inc(pe, 1)
                    t.wait_ge(acts, P["acts_h1", b, l, 7])
                    for oc in range(4):
                        psh_wait()
                        last = None
                        for ic in range(8):
                            for pc in (0, 512):
                                last = t.matmul(
                                    psH[oc % 2][:, pc:pc + 512],
                                    w2s[:, ic * 512 + oc * 128:ic * 512 + oc * 128 + 128],
                                    h1T[:, ic * 1024 + pc:ic * 1024 + pc + 512],
                                    start=(ic == 0), stop=(ic == 7))
                        last.then_inc(pe, 1)
                    t.wait_ge(acts, P["acts_h2", b, l, 3])
                    for oc in range(2):
                        psh_wait()
                        last = None
                        for ic in range(4):
                            for pc in (0, 512):
                                last = t.matmul(
                                    psH[oc % 2][:, pc:pc + 512],
                                    w3s[:, ic * 256 + oc * 128:ic * 256 + oc * 128 + 128],
                                    h2T[:, ic * 1024 + pc:ic * 1024 + pc + 512],
                                    start=(ic == 0), stop=(ic == 3))
                        last.then_inc(pe, 1)
                    t.wait_ge(acts, P["acts_h3", b, l, 1])
                    if not (b == 0 and l == 0):
                        lb, ll = (0, l) if b == 1 else (1, l - 1)
                        t.wait_ge(acts, P["acts_tf", lb, ll])
                    last = None
                    for p in range(8):
                        for icc in range(2):
                            last = t.matmul(
                                psG[:, 12 + p:13 + p],
                                h3T[:, icc * 1024 + p * 128:icc * 1024 + p * 128 + 128],
                                w4s[:, icc:icc + 1],
                                start=(icc == 0), stop=(icc == 1))
                    last.then_inc(pe, 1)

        @block.scalar
        def _(s):
            sc = Ctr()

            def act(fn_out, fn_in, func, **kw):
                s.wait_ge(acts, sc.v)
                s.activation(fn_out, fn_in, func, **kw).then_inc(acts, 1)
                sc.inc()

            s.wait_ge(g_sm, 16)
            for oc in range(8):
                s.wait_ge(pe, P["pe_h1", 0, 0, oc])
                act(h1T[:, oc * 1024:(oc + 1) * 1024], psH[oc % 2][:, :],
                    AF.Gelu, bias=sml_sb[:, oc:oc + 1])
            for b in range(B):
                for oc in range(4):
                    s.wait_ge(pe, P["pe_g1", b, oc])
                    act(g1[:, oc:oc + 1], psG[:, oc:oc + 1], AF.Gelu,
                        bias=sml_sb[:, 14 + oc:15 + oc])
                for oc in range(2):
                    s.wait_ge(pe, P["pe_g2", b, oc])
                    act(g2[:, oc:oc + 1], psG[:, 4 + oc:5 + oc], AF.Gelu,
                        bias=sml_sb[:, 18 + oc:19 + oc])
            s.wait_ge(g_sm2, 16 * 3)
            for b in range(B):
                s.wait_ge(pe, P["pe_gf", b])
                act(sc4[0:1, 2 + b:3 + b], psG[0:1, 6 + b:7 + b], AF.Sigmoid,
                    bias=gb3s[0:1, 0:1])
            s.wait_ge(ve, P["ve_bc"])
            def hacts(b, l):
                if (b, l) != (0, 0):
                    for oc in range(8):
                        s.wait_ge(pe, P["pe_h1", b, l, oc])
                        act(h1T[:, oc * 1024:(oc + 1) * 1024], psH[oc % 2][:, :],
                            AF.Gelu, bias=sml_sb[:, oc:oc + 1])
                for oc in range(4):
                    s.wait_ge(pe, P["pe_h2", b, l, oc])
                    act(h2T[:, oc * 1024:(oc + 1) * 1024], psH[oc % 2][:, :],
                        AF.Gelu, bias=sml_sb[:, 8 + oc:9 + oc])
                for oc in range(2):
                    s.wait_ge(pe, P["pe_h3", b, l, oc])
                    act(h3T[:, oc * 1024:(oc + 1) * 1024], psH[oc % 2][:, :],
                        AF.Gelu, bias=sml_sb[:, 12 + oc:13 + oc])

            def sigact(b, l):
                s.wait_ge(pe, P["pe_tf", b, l])
                act(tfs[:, b * 8:(b + 1) * 8], psG[:, 12:20], AF.Sigmoid,
                    bias=bc[:, 0:1])

            def rstd2(b):
                s.wait_ge(ve, P["ve_vt2eps", b])
                act(lt2[:, b * 8:(b + 1) * 8], vt2[:, b * 8:(b + 1) * 8], AF.Ln,
                    scale=2.0 ** -20)
                act(rt2[:, b * 8:(b + 1) * 8], lt2[:, b * 8:(b + 1) * 8], AF.Exp,
                    scale=-0.5)

            for l in range(2):
                hacts(0, l); sigact(0, l)
                hacts(1, l); sigact(1, l)
            hacts(0, 2); sigact(0, 2)
            for oc in range(8):
                s.wait_ge(pe, P["pe_h1", 1, 2, oc])
                act(h1T[:, oc * 1024:(oc + 1) * 1024], psH[oc % 2][:, :],
                    AF.Gelu, bias=sml_sb[:, oc:oc + 1])
            rstd2(0)
            for oc in range(4):
                s.wait_ge(pe, P["pe_h2", 1, 2, oc])
                act(h2T[:, oc * 1024:(oc + 1) * 1024], psH[oc % 2][:, :],
                    AF.Gelu, bias=sml_sb[:, 8 + oc:9 + oc])
            for oc in range(2):
                s.wait_ge(pe, P["pe_h3", 1, 2, oc])
                act(h3T[:, oc * 1024:(oc + 1) * 1024], psH[oc % 2][:, :],
                    AF.Gelu, bias=sml_sb[:, 12 + oc:13 + oc])
            sigact(1, 2)
            def rstd1(p):
                s.wait_ge(ve, P["ve_vteps1", p])
                act(lt2[:, 8 + p:9 + p], vt2[:, 8 + p:9 + p], AF.Ln,
                    scale=2.0 ** -20)
                act(rt2[:, 8 + p:9 + p], lt2[:, 8 + p:9 + p], AF.Exp, scale=-0.5)

            for p in range(8):
                rstd1(p)
                if p >= 3 and p % 2 == 1:
                    pw = p - 2
                    s.wait_ge(ve, P["ve_apply", 1, pw])
                    s.dma_start(out=x2[1, pw],
                                in_=out1[:, pw * 1024:(pw + 1) * 1024]).then_inc(st_g, 16)
            s.wait_ge(ve, P["ve_apply", 1, 7])
            s.dma_start(out=x2[1, 7],
                        in_=out1[:, 7 * 1024:8 * 1024]).then_inc(st_g, 16)

        @block.vector
        def _(v):
            vc = Ctr()

            def op(fn, *args, **kw):
                v.wait_ge(ve, vc.v)
                fn(*args, **kw).then_inc(ve, 1)
                vc.inc()

            op(v.memset, obf[:, 0:512], 0.0)
            op(v.memset, ones1[0:1, :], 1.0)
            v.wait_ge(g_gc, 16 * 5)
            op(v.tensor_tensor, gc16[:, :], gcTs[:, 0:16], gcTs[:, 16:32], ALU.add)
            for cr in range(2, 8):
                op(v.tensor_tensor, gc16[:, :], gc16[:, :],
                   gcTs[:, cr * 16:(cr + 1) * 16], ALU.add)
            op(v.tensor_scalar, gc16[:, :], gc16[:, :], 1.0 / 8192.0, None, ALU.mult)
            op(v.tensor_copy, gc16b[:, :], gc16[:, :])
            assert vc.v == P["ve_gc16b"]
            v.wait_ge(pe, P["pe_bc"])
            op(v.tensor_copy, bc[:, :], psG[:, 8:12])
            assert vc.v == P["ve_bc"]
            for b in range(B):
                op(v.tensor_tensor, tAB[:, b:b + 1], bc[:, 1:2], bc[:, 2 + b:3 + b], ALU.mult)
            op(v.tensor_scalar, tAB[:, :], tAB[:, :], 1.4, None, ALU.mult)
            op(v.tensor_scalar, tsm[:, 0:1], bc[:, 1:2], -1.0, 0.1, ALU.mult, ALU.add)
            op(v.tensor_tensor, Ab[:, 0:1], tsm[:, 0:1], tAB[:, 0:1], ALU.add)
            op(v.tensor_tensor, Ab[:, 1:2], tsm[:, 0:1], tAB[:, 1:2], ALU.add)
            op(v.tensor_scalar, Bc[:, 0:1], bc[:, 1:2], 0.6, None, ALU.mult)
            assert vc.v == P["ve_AB"]

            def twist(b, l):
                if l == 0:
                    v.wait_ge([s_ob0, s_ob1][b], 16)
                v.wait_ge(acts, P["acts_tf", b, l])
                op(v.tensor_scalar, coup[:, b * 8:(b + 1) * 8],
                   tfs[:, b * 8:(b + 1) * 8], Bc[:, 0:1], Ab[:, b:b + 1],
                   ALU.mult, ALU.add)
                op(v.tensor_scalar, coupn[:, b * 8:(b + 1) * 8],
                   coup[:, b * 8:(b + 1) * 8], -1.0, None, ALU.mult)
                assert vc.v == P["ve_coup", b, l]
                w4 = wtmp[:, :].rearrange("p (a q) -> p a q", a=4)
                for p in range(8):
                    osl = outb[b][:, p * 1024:(p + 1) * 1024]
                    o4 = osl.rearrange("p (a q) -> p a q", a=4)
                    cp = coup[:, b * 8 + p:b * 8 + p + 1]
                    op(v.tensor_scalar, wtmp[:, :], osl, cp, None, ALU.mult)
                    op(v.scalar_tensor_tensor, o4[:, 0:4:3, :], w4[:, 2:0:-1, :],
                       1.0, o4[:, 0:4:3, :], ALU.mult, ALU.add)
                    op(v.scalar_tensor_tensor, o4[:, 1:3, :], w4[:, 3::-3, :],
                       -1.0, o4[:, 1:3, :], ALU.mult, ALU.add)
                assert vc.v == P["ve_twist", b, l]

            def ln2(b):
                assert b == 0
                bsl = slice(0, 8)
                for p in range(8):
                    col = slice(p, p + 1)
                    osl = outb[0][:, p * 1024:(p + 1) * 1024]
                    v.wait_ge(po, P["po_s2add", 0, p])
                    op(v.reduce_sum, mt2[:, col], osl, AX.X)
                    v.wait_ge(po, P["po_sq", 0, p])
                    op(v.reduce_sum, v2t2[:, col],
                       sq2[:, (p % 2) * 1024:(p % 2 + 1) * 1024], AX.X)
                    op(v.tensor_scalar, tsm[:, col], mt2[:, col], mt2[:, col],
                       RAWEPS, ALU.mult, ALU.subtract)
                    op(v.scalar_tensor_tensor, vt2[:, col], v2t2[:, col],
                       1024.0, tsm[:, col], ALU.mult, ALU.subtract)
                    assert vc.v == P["ve_vt", 0, p]
                assert vc.v == P["ve_vt2eps", 0]
                v.wait_ge(acts, P["acts_rstd2", 0])
                op(v.tensor_tensor, nmr2[:, bsl], mt2[:, bsl], rt2[:, bsl], ALU.mult)
                op(v.tensor_scalar, nmr2[:, bsl], nmr2[:, bsl], -1.0 / 1024.0, None, ALU.mult)
                assert vc.v == P["ve_nmrb", 0]
                for p in range(8):
                    col = slice(p, p + 1)
                    osl = outb[0][:, p * 1024:(p + 1) * 1024]
                    op(v.tensor_scalar, osl, osl, rt2[:, col], nmr2[:, col],
                       ALU.mult, ALU.add)
                    assert vc.v == P["ve_apply", 0, p]

            def ln2b1_head(p):
                idx = 8 + p
                col = slice(idx, idx + 1)
                osl = outb[1][:, p * 1024:(p + 1) * 1024]
                idx4 = [s_x10, s_x11, s_x12, s_x13][idx % 4]
                v.wait_ge(idx4, P["sld_x1s", 1, p])
                op(v.scalar_tensor_tensor, osl,
                   x1s[:, (idx % 4) * 1024:(idx % 4 + 1) * 1024], 1.0, osl,
                   ALU.mult, ALU.add, accum_out=mt2[:, col])
                assert vc.v == P["ve_s2add1", p]

            def ln2b1_var(p):
                idx = 8 + p
                col = slice(idx, idx + 1)
                osl = outb[1][:, p * 1024:(p + 1) * 1024]
                op(v.scalar_tensor_tensor, wtmp[:, :], osl, 1.0, osl,
                   ALU.mult, ALU.mult, accum_out=v2t2[:, col])
                op(v.tensor_scalar, tsm[:, col], mt2[:, col], mt2[:, col],
                   RAWEPS, ALU.mult, ALU.subtract)
                op(v.scalar_tensor_tensor, vt2[:, col], v2t2[:, col],
                   1024.0, tsm[:, col], ALU.mult, ALU.subtract)
                assert vc.v == P["ve_vt", 1, p]
                assert vc.v == P["ve_vteps1", p]

            def ln2b1_apply(p):
                idx = 8 + p
                col = slice(idx, idx + 1)
                osl = outb[1][:, p * 1024:(p + 1) * 1024]
                v.wait_ge(acts, P["acts_rstd1", p])
                op(v.tensor_scalar, nmr2[:, col], mt2[:, col], rt2[:, col],
                   -1.0 / 1024.0, ALU.mult, ALU.mult)
                op(v.tensor_scalar, osl, osl, rt2[:, col], nmr2[:, col],
                   ALU.mult, ALU.add)
                assert vc.v == P["ve_apply", 1, p]

            def ln2b1():
                for p in range(8):
                    ln2b1_head(p)
                    if p >= 1:
                        ln2b1_var(p - 1)
                    if p >= 2:
                        ln2b1_apply(p - 2)
                ln2b1_var(7)
                ln2b1_apply(6)
                ln2b1_apply(7)

            def ce(b, l):
                if (b, l) != (0, 1):
                    ob, ol = (1, l - 1) if b == 0 else (0, l)
                    v.wait_ge(pe, P["pe_tp", ob, ol, 15])
                op(v.tensor_copy, obf[:, :], outb[b][:, :])
                assert vc.v == P["ve_obf", b, l]
                for g in range(16):
                    v.wait_ge(pe, P["pe_tp", b, l, g])
                    fc, pg = divmod(g, 2)
                    op(v.tensor_copy,
                       outT[b][:, fc * 1024 + pg * 512:fc * 1024 + pg * 512 + 512],
                       psE[g % 2][:, :])
                    assert vc.v == P["ve_tpev", b, l, g]

            twist(0, 0)
            ce(0, 1)
            twist(1, 0)
            ce(1, 1)
            twist(0, 1)
            ce(0, 2)
            twist(1, 1)
            ce(1, 2)
            twist(0, 2)
            ln2(0)
            twist(1, 2)
            ln2b1()
    return nc


def l3_host_inputs(x1_full, x1bf, gcp, W):
    import ml_dtypes
    bf = ml_dtypes.bfloat16
    gcT = np.zeros((128, 128), dtype=np.float32)
    for cr in range(NC):
        for b in range(B):
            gcT[:, cr * 16 + b * 8:cr * 16 + (b + 1) * 8] = (
                gcp[cr, b].reshape(8, 128).T)
    common = {
        "gcT_in": gcT,
        "w1_in": np.ascontiguousarray(
            W["cn_w1"].reshape(8, 128, 8, 128).transpose(2, 1, 0, 3)
        ).reshape(8, 128, 1024),
        "w2_in": W["cn_w2"].reshape(8, 128, 512),
        "w3_in": W["cn_w3"].reshape(4, 128, 256),
        "w4_in": np.ascontiguousarray(W["cn_w4"].reshape(2, 128).T),
        "gw1_in": W["gc_w1"].reshape(8, 128, 512),
        "gw2_in": W["gc_w2"].reshape(4, 128, 256),
        "gw3_in": np.ascontiguousarray(W["gc_w3"].reshape(2, 128).T),
        "sml_in": np.concatenate(
            [W["cn_b1"].reshape(8, 128).T, W["cn_b2"].reshape(4, 128).T,
             W["cn_b3"].reshape(2, 128).T, W["gc_b1"].reshape(4, 128).T,
             W["gc_b2"].reshape(2, 128).T], axis=1),
        "b4_in": W["cn_b4"].reshape(1, 1),
        "gb3_in": W["gc_b3"].reshape(1, 1),
        "ar_in": W["adaptive_range"].reshape(1, 1),
        "ident_in": np.eye(128, dtype=bf),
    }
    common = {k: np.ascontiguousarray(v, dtype=v.dtype) for k, v in common.items()}
    maps = []
    for c in range(NC):
        rows = np.concatenate([np.arange(t * 128, (t + 1) * 128)
                               for t in snake_tiles(c)])
        x1c = np.ascontiguousarray(x1_full[:, rows, :]).reshape(B, 8, 128, 1024)
        x1Tc = np.ascontiguousarray(
            x1bf[:, rows, :].transpose(0, 2, 1)).reshape(B, 8, 128, 1024)
        maps.append({"x1_in": x1c, "x1T_in": x1Tc, **common})
    return maps


def l3_host_outputs(res):
    out = np.zeros((B, S, D), dtype=np.float32)
    for c in range(NC):
        r = res.results[c]
        for pi, t in enumerate(snake_tiles(c)):
            out[:, t * 128:(t + 1) * 128, :] = r["x2"][:, pi]
    return out



from concourse.bass_utils import run_bass_kernel_spmd

_NC_CACHE = {}


def _get(name, builder):
    if name not in _NC_CACHE:
        _NC_CACHE[name] = builder()
    return _NC_CACHE[name]


def kernel(L0_attn_x, L0_attn_Wq, L0_attn_Wk, L0_attn_Wv,
           L0_ffn_cn_w1, L0_ffn_cn_b1, L0_ffn_cn_w2, L0_ffn_cn_b2,
           L0_ffn_cn_w3, L0_ffn_cn_b3, L0_ffn_cn_w4, L0_ffn_cn_b4,
           L0_ffn_gc_w1, L0_ffn_gc_b1, L0_ffn_gc_w2, L0_ffn_gc_b2,
           L0_ffn_gc_w3, L0_ffn_gc_b3, L0_ffn_adaptive_range):
    x = np.asarray(L0_attn_x, dtype=np.float32)
    cores = list(range(NC))

    nc1 = _get("l1", build_l1)
    res1 = run_bass_kernel_spmd(
        nc1, l1_host_inputs(x, np.asarray(L0_attn_Wq), np.asarray(L0_attn_Wk),
                            np.asarray(L0_attn_Wv)), core_ids=cores)
    qT, kT, vT, usum = l1_host_outputs(res1)

    nc2 = _get("l2", build_l2)
    res2 = run_bass_kernel_spmd(nc2, l2_host_inputs(x, qT, kT, vT, usum),
                                core_ids=cores)
    x1, x1bf, gcp = l2_host_outputs(res2)

    W = {
        "cn_w1": np.asarray(L0_ffn_cn_w1), "cn_b1": np.asarray(L0_ffn_cn_b1),
        "cn_w2": np.asarray(L0_ffn_cn_w2), "cn_b2": np.asarray(L0_ffn_cn_b2),
        "cn_w3": np.asarray(L0_ffn_cn_w3), "cn_b3": np.asarray(L0_ffn_cn_b3),
        "cn_w4": np.asarray(L0_ffn_cn_w4), "cn_b4": np.asarray(L0_ffn_cn_b4),
        "gc_w1": np.asarray(L0_ffn_gc_w1), "gc_b1": np.asarray(L0_ffn_gc_b1),
        "gc_w2": np.asarray(L0_ffn_gc_w2), "gc_b2": np.asarray(L0_ffn_gc_b2),
        "gc_w3": np.asarray(L0_ffn_gc_w3), "gc_b3": np.asarray(L0_ffn_gc_b3),
        "adaptive_range": np.asarray(L0_ffn_adaptive_range),
    }
    nc3 = _get("l3", build_l3)
    res3 = run_bass_kernel_spmd(nc3, l3_host_inputs(x1, x1bf, gcp, W), core_ids=cores)
    return l3_host_outputs(res3)


# version 24 (final)
# speedup vs baseline: 1.0455x; 1.0021x over previous
import numpy as np
import concourse.bass as bass
import concourse.mybir as mybir

F32 = mybir.dt.float32
BF16 = mybir.dt.bfloat16
AF = mybir.ActivationFunctionType
ALU = mybir.AluOpType
AX = mybir.AxisListType

B = 2
S = 8192
D = 1024
NC = 8
TOK = 1024


class Ctr:
    def __init__(self):
        self.v = 0
    def inc(self, n=1):
        self.v += n
        return self.v



def build_l1():
    nc = bass.Bass("TRN2", target_bir_lowering=False)
    xT = nc.dram_tensor("xT", [B, 8, 128, TOK], F32, kind="ExternalInput").ap()
    wqT = nc.dram_tensor("wqT", [8, 128, D], F32, kind="ExternalInput").ap()
    wkT = nc.dram_tensor("wkT", [8, 128, D], F32, kind="ExternalInput").ap()
    wvT = nc.dram_tensor("wvT", [8, 128, D], F32, kind="ExternalInput").ap()
    qT = nc.dram_tensor("qT", [B, 8, 128, TOK], BF16, kind="ExternalOutput").ap()
    kT = nc.dram_tensor("kT", [B, 8, 128, TOK], BF16, kind="ExternalOutput").ap()
    vT = nc.dram_tensor("vT", [B, 8, 128, TOK], BF16, kind="ExternalOutput").ap()
    usum = nc.dram_tensor("usum", [128, 16], BF16, kind="ExternalOutput").ap()

    projs = [("k", wkT, kT), ("v", wvT, vT), ("q", wqT, wqT)]

    AC = {"wk0": 1, "x00": 2, "wk1": 3, "x10": 4, "x20": 5, "x30": 6,
          "wvc": 7, "wqc": 8}
    pe_c = Ctr()
    PE_AT = {}
    for pb in range(6):
        for th in range(2):
            for dc in range(8):
                pe_c.inc(); PE_AT[(pb, th, dc)] = pe_c.v
    ve_c = Ctr()
    ve_c.inc()
    VE_DM = ve_c.v
    EV = {}
    for pb in range(6):
        for th in range(2):
            for dc in range(8):
                ve_c.inc(); EV[(pb, th, dc)] = ve_c.v
                if pb in (2, 3) and th == 1:
                    ve_c.inc()
    ve_c.inc()
    VE_TOTAL = ve_c.v

    from contextlib import ExitStack
    with ExitStack() as _ctx:
        def _e(t):
            return _ctx.enter_context(t)
        x_sb = _e(nc.sbuf_tensor("x_sb", [128, B * 8 * TOK], BF16))
        x32 = _e(nc.sbuf_tensor("x32", [128, 8 * TOK], F32))
        w32a = _e(nc.sbuf_tensor("w32a", [128, 8 * D], F32))
        w32b = _e(nc.sbuf_tensor("w32b", [128, 8 * D], F32))
        w_sb0 = _e(nc.sbuf_tensor("w_sb0", [128, 8 * D], BF16))
        w_sb1 = _e(nc.sbuf_tensor("w_sb1", [128, 8 * D], BF16))
        st0 = _e(nc.sbuf_tensor("st0", [128, 8 * TOK], BF16))
        st1 = _e(nc.sbuf_tensor("st1", [128, 8 * TOK], BF16))
        us_sb = _e(nc.sbuf_tensor("us_sb", [128, 16], F32))
        usb_sb = _e(nc.sbuf_tensor("usb_sb", [128, 16], BF16))
        psb = [_e(nc.psum_tensor(f"psb{i}", [128, 512], F32)) for i in range(8)]
        g_x = [_e(nc.semaphore(f"g_x{i}")) for i in range(3)]
        s_wk0 = _e(nc.semaphore("s_wk0"))
        s_wk1 = _e(nc.semaphore("s_wk1"))
        s_wq = _e(nc.semaphore("s_wq"))
        s_w1 = _e(nc.semaphore("s_w1"))
        a_x = [_e(nc.semaphore(f"a_x{i}")) for i in range(2)]
        pe = _e(nc.semaphore("pe"))
        acts = _e(nc.semaphore("acts"))
        ev = _e(nc.semaphore("ev"))
        stA = _e(nc.semaphore("stA"))
        stB = _e(nc.semaphore("stB"))
        block = _e(nc.Block())
        w_sb = [w_sb0, w_sb1]
        stg = [st0, st1]
        stp = [stA, stB]
        XSLOT = {0: 0, 1: 1, 2: 0, 3: 1}
        XSEM = {0: a_x[0], 1: g_x[1], 2: a_x[1], 3: g_x[0]}
        XCNT = {0: 16, 1: 16, 2: 16, 3: 16}

        def xsrc(i):
            b, half = divmod(i, 2)
            return xT[b, half * 4:(half + 1) * 4].rearrange("hc p t -> p hc t")

        def xdst(i):
            s = XSLOT[i]
            return x32[:, s * 4 * TOK:(s + 1) * 4 * TOK].rearrange(
                "p (hc t) -> p hc t", hc=4)

        @block.gpsimd
        def _(g):
            g.dma_start(out=xdst(1), in_=xsrc(1)).then_inc(XSEM[1], 16)
            g.wait_ge(acts, AC["x10"])
            g.dma_start(out=xdst(3), in_=xsrc(3)).then_inc(XSEM[3], 16)

        @block.sync
        def _(sy):
            w32a3 = w32a[:, :].rearrange("p (hc d) -> p hc d", hc=8)
            w32b3 = w32b[:, :].rearrange("p (hc d) -> p hc d", hc=8)
            sy.dma_start(out=w32a[:, 0:4 * D].rearrange("p (hc d) -> p hc d", hc=4),
                         in_=wkT[0:4].rearrange("hc p d -> p hc d")).then_inc(s_wk0, 16)
            sy.dma_start(out=w32a[:, 4 * D:8 * D].rearrange("p (hc d) -> p hc d", hc=4),
                         in_=wkT[4:8].rearrange("hc p d -> p hc d")).then_inc(s_wk1, 16)
            sy.dma_start(out=w32b3,
                         in_=wvT.rearrange("hc p d -> p hc d")).then_inc(s_w1, 16)
            sy.wait_ge(acts, AC["wk1"])
            sy.dma_start(out=w32a3,
                         in_=wqT.rearrange("hc p d -> p hc d")).then_inc(s_wq, 16)
            npar = [0, 0]
            for pb in range(6):
                i, b = divmod(pb, 2)
                out = [kT, vT, qT][i]
                for h in range(4):
                    sy.wait_ge(ev, EV[(pb, 1, 2 * h + 1)])
                    sy.dma_start(out=out[b, 2 * h:2 * h + 2].rearrange("dc p t -> p dc t"),
                                 in_=stg[pb % 2][:, 2 * h * TOK:(2 * h + 2) * TOK].rearrange(
                                     "p (dc t) -> p dc t", dc=2)).then_inc(stp[pb % 2], 16)
                    npar[pb % 2] += 1
            sy.wait_ge(ev, VE_TOTAL)
            sy.dma_start(out=usum, in_=usb_sb[:, :]).then_inc(stA, 16)
            npar[0] += 1
            sy.wait_ge(stA, 16 * npar[0])
            sy.wait_ge(stB, 16 * npar[1])

        @block.scalar
        def _(s):
            sc = Ctr()

            def cp(dst, srcap):
                s.wait_ge(acts, sc.v)
                s.copy(dst, srcap).then_inc(acts, 1)
                sc.inc()

            def xcast(i):
                b, half = divmod(i, 2)
                sl = XSLOT[i]
                s.wait_ge(XSEM[i], XCNT[i])
                cp(x_sb[:, (b * 8 + half * 4) * TOK:(b * 8 + half * 4 + 4) * TOK],
                   x32[:, sl * 4 * TOK:(sl + 1) * 4 * TOK])

            s.dma_start(out=xdst(0), in_=xsrc(0)).then_inc(XSEM[0], 16)
            s.wait_ge(s_wk0, 16)
            cp(w_sb0[:, 0:4 * D], w32a[:, 0:4 * D])
            assert sc.v == AC["wk0"]
            xcast(0)
            assert sc.v == AC["x00"]
            s.wait_ge(acts, AC["x00"])
            s.dma_start(out=xdst(2), in_=xsrc(2)).then_inc(XSEM[2], 16)
            s.wait_ge(s_wk1, 16)
            cp(w_sb0[:, 4 * D:8 * D], w32a[:, 4 * D:8 * D])
            assert sc.v == AC["wk1"]
            xcast(1)
            xcast(2)
            xcast(3)
            assert sc.v == AC["x30"]
            s.wait_ge(s_w1, 16)
            cp(w_sb1[:, :], w32b[:, :])
            assert sc.v == AC["wvc"]
            s.wait_ge(pe, PE_AT[(1, 1, 7)])
            s.wait_ge(s_wq, 16)
            cp(w_sb0[:, :], w32a[:, :])
            assert sc.v == AC["wqc"]

        @block.vector
        def _(v):
            vc = Ctr()

            def op(fn, *args, **kw):
                v.wait_ge(ev, vc.v)
                fn(*args, **kw).then_inc(ev, 1)
                vc.inc()

            op(v.memset, st0[:, 0:512], 0.0)
            assert vc.v == VE_DM
            for pb in range(6):
                i, b = divmod(pb, 2)
                if pb >= 2:
                    v.wait_ge(stp[pb % 2], 64 * (pb // 2))
                for th in range(2):
                    for dc in range(8):
                        v.wait_ge(pe, PE_AT[(pb, th, dc)])
                        sgn = -1.0 if (i == 2 and b == 1) else 1.0
                        op(v.tensor_scalar,
                           stg[pb % 2][:, dc * TOK + th * 512:dc * TOK + th * 512 + 512],
                           psb[dc][:, :], sgn, None, ALU.mult)
                        assert vc.v == EV[(pb, th, dc)]
                        if i == 1 and th == 1:
                            op(v.reduce_sum, us_sb[:, b * 8 + dc:b * 8 + dc + 1],
                               stg[pb % 2][:, dc * TOK:(dc + 1) * TOK], AX.X)
            op(v.tensor_copy, usb_sb[:, :], us_sb[:, :])
            assert vc.v == VE_TOTAL

        @block.tensor
        def _(t):
            t.wait_ge(ev, VE_DM)
            for _wu in range(55):
                t.matmul(psb[0][:, :], st0[:, 0:128], st0[:, 0:512],
                         start=True, stop=True)
            for pb in range(6):
                i, b = divmod(pb, 2)
                w_ok = [AC["wk0"], AC["wk1"]] if i == 0 else (
                    [AC["wvc"]] * 2 if i == 1 else [AC["wqc"]] * 2)
                x_ok = [[AC["x00"], AC["x10"]], [AC["x20"], AC["x30"]]][b]
                for th in range(2):
                    for ph in range(2):
                        t.wait_ge(acts, max(w_ok[ph], x_ok[ph]))
                        for dc in range(8):
                            if ph == 0:
                                if (pb, th) != (0, 0):
                                    ppb, pth = (pb, 0) if th == 1 else (pb - 1, 1)
                                    t.wait_ge(ev, EV[(ppb, pth, dc)])
                            last = None
                            for hc in range(4 * ph, 4 * ph + 4):
                                last = t.matmul(
                                    psb[dc][:, :],
                                    w_sb[i % 2][:, hc * D + dc * 128:hc * D + dc * 128 + 128],
                                    x_sb[:, (b * 8 + hc) * TOK + th * 512:
                                         (b * 8 + hc) * TOK + th * 512 + 512],
                                    start=(hc == 0), stop=(hc == 7),
                                )
                            if ph == 1:
                                last.then_inc(pe, 1)
    return nc



def snake_tiles(c):
    return [c, 15 - c, 16 + c, 31 - c, 32 + c, 47 - c, 48 + c, 63 - c]


def l1_host_inputs(x, Wq, Wk, Wv):
    maps = []
    wqT = np.ascontiguousarray(Wq.T).reshape(8, 128, D)
    wkT = np.ascontiguousarray(Wk.T).reshape(8, 128, D)
    wvT = np.ascontiguousarray(Wv.T).reshape(8, 128, D)
    for c in range(NC):
        xs = x[:, c * TOK:(c + 1) * TOK, :]
        xT = np.ascontiguousarray(xs.transpose(0, 2, 1)).reshape(B, 8, 128, TOK)
        maps.append({"xT": xT, "wqT": wqT, "wkT": wkT, "wvT": wvT})
    return maps


def l1_host_outputs(res):
    import ml_dtypes
    qT = np.zeros((B, D, S), dtype=ml_dtypes.bfloat16)
    kT = np.zeros((B, D, S), dtype=ml_dtypes.bfloat16)
    vT = np.zeros((B, D, S), dtype=ml_dtypes.bfloat16)
    usum = np.zeros((B, NC, D), dtype=ml_dtypes.bfloat16)
    for c in range(NC):
        r = res.results[c]
        sl = slice(c * TOK, (c + 1) * TOK)
        qT[:, :, sl] = r["qT"].reshape(B, D, TOK)
        kT[:, :, sl] = r["kT"].reshape(B, D, TOK)
        vT[:, :, sl] = r["vT"].reshape(B, D, TOK)
        u = r["usum"]
        for b in range(B):
            usum[b, c] = u[:, b * 8:(b + 1) * 8].T.reshape(D)
    return qT, kT, vT, usum



def _pieces(w):
    out = []
    o = 0
    while o < w:
        c = min(512, w - o)
        out.append((o, c))
        o += c
    return out


def plan_l2():
    pe = Ctr(); acts = Ctr(); ve = Ctr(); st = Ctr()
    P = dict(pe_S={}, pe_V={}, pe_gc={}, acts_sig={}, acts_sq={}, acts_rstd={},
             acts_b16={}, ve_ready={}, ve_ready0={}, ve_m0={}, ve_m1={},
             ve_evac={}, ve_xadd={}, ve_vt={},
             ve_vteps={}, ve_nmr={}, ve_apply={}, ve_b16={}, ve_gcev={},
             sld_K={}, sld_V={}, gld_x={}, st_x={}, st_b16={}, st_gc={})

    xc = [Ctr(), Ctr()]
    for p in range(8):
        for b in range(B):
            idx = p * 2 + b
            xc[idx % 2].inc(16)
            P["gld_x"][(b, p)] = xc[idx % 2].v

    for a in range(8):
        for rc in range(8):
            P["sld_K"][(a, rc)] = 16 * (a + 1)
        P["sld_V"][(a, 0)] = 16 * a
        P["sld_V"][(a, 1)] = 16 * a

    ve.inc(); P["ve_dm"] = ve.v
    ve.inc(); P["ve_ones"] = ve.v

    for a in range(8):
        w = (8 - a) * 128
        for rc in range(8):
            pe.inc(); P["pe_S"][(a, rc)] = pe.v
        for rc in range(8):
            acts.inc(); P["acts_sig"][(a, rc)] = acts.v
        acts.inc(2); P["acts_rstd"][a] = acts.v
        for rc in range(8):
            ve.inc(2)
            P["ve_m0"][(a, rc)] = ve.v
            ve.inc(2)
            if w > 128:
                ve.inc()
            P["ve_m1"][(a, rc)] = ve.v
        P["ve_ready0"][a] = P["ve_m0"][(a, 7)]
        P["ve_ready"][a] = ve.v
        for b in range(B):
            for p in range(a, 8):
                pe.inc(); P["pe_V"][(a, b, p)] = pe.v
                ve.inc(); P["ve_evac"][(a, b, p)] = ve.v
                if p == a:
                    ve.inc(); P["ve_xadd"][(b, p)] = ve.v
                    ve.inc()
                    ve.inc(2)
                    P["ve_vt"][(b, p)] = ve.v
                    if b == 1:
                        P["ve_vteps"][a] = ve.v
        for b in range(B):
            ve.inc(); P["ve_nmr"][(b, a)] = ve.v
            ve.inc(); P["ve_apply"][(b, a)] = ve.v
    for p in range(8):
        ve.inc(); P["ve_b16"][(1, p)] = ve.v
    for p in range(8):
        acts.inc(); P["acts_b16"][(0, p)] = acts.v
    for b in range(B):
        pe.inc(); P["pe_gc"][b] = pe.v
        ve.inc(); P["ve_gcev"][b] = ve.v
    for a in range(8):
        for b in range(B):
            st.inc(16); P["st_x"][(b, a)] = st.v
    for b in range(B):
        st.inc(16); P["st_b16"][b] = st.v
    for b in range(B):
        st.inc(16); P["st_gc"][b] = st.v
    P["st_total"] = st.v
    P["sig_seq"] = [P["acts_sig"][(a, rc)] for a in range(8) for rc in range(8)]
    return P


def build_l2():
    nc = bass.Bass("TRN2", target_bir_lowering=False)
    qT_in = nc.dram_tensor("qT_in", [B, 8, 128, 1024], BF16, kind="ExternalInput").ap()
    K_in = nc.dram_tensor("K_in", [8, 8, B, 128, 8, 128], BF16, kind="ExternalInput").ap()
    V_in = nc.dram_tensor("V_in", [B, 8, 8, 128, 1024], BF16, kind="ExternalInput").ap()
    us_in = nc.dram_tensor("us_in", [8, B * 1024], BF16, kind="ExternalInput").ap()
    c19_in = nc.dram_tensor("c19_in", [8, 1024], BF16, kind="ExternalInput").ap()
    m1_in = nc.dram_tensor("m1_in", [128, 2048], BF16, kind="ExternalInput").ap()
    m2_in = nc.dram_tensor("m2_in", [128, 2048], BF16, kind="ExternalInput").ap()

    xq_in = nc.dram_tensor("xq_in", [B, 8, 128, 1024], F32, kind="ExternalInput").ap()
    x1 = nc.dram_tensor("x1", [B, 8, 128, 1024], F32, kind="ExternalOutput").ap()
    x1b = nc.dram_tensor("x1b", [B, 8, 128, 1024], BF16, kind="ExternalOutput").ap()
    gcp = nc.dram_tensor("gcp", [B, 1024], F32, kind="ExternalOutput").ap()

    P = plan_l2()
    EPS = 1e-5
    RAWEPS = EPS * 1024.0 * 1024.0

    from contextlib import ExitStack
    with ExitStack() as _ctx:
        def _e(x):
            return _ctx.enter_context(x)
        qT_sb = _e(nc.sbuf_tensor("qT_sb", [128, B * 8 * 1024], BF16))
        K_rg = _e(nc.sbuf_tensor("K_rg", [128, 10 * 1024], BF16))
        V_sb = _e(nc.sbuf_tensor("V_sb", [128, 2 * 8 * 1024], BF16))
        at0 = _e(nc.sbuf_tensor("at0", [128, 8 * 1024], BF16))
        at1 = _e(nc.sbuf_tensor("at1", [128, 8 * 1024], BF16))
        acc = _e(nc.sbuf_tensor("acc", [128, B * 8 * 1024], F32))
        m1m = _e(nc.sbuf_tensor("m1m", [128, 2048], BF16))
        m2m = _e(nc.sbuf_tensor("m2m", [128, 2048], BF16))

        c19_sb = _e(nc.sbuf_tensor("c19_sb", [8, 1024], BF16))
        us_sb = _e(nc.sbuf_tensor("us_sb", [8, B * 1024], BF16))
        xres = _e(nc.sbuf_tensor("xres", [128, 2 * 1024], F32))
        sq = _e(nc.sbuf_tensor("sq", [128, 1024], F32))
        mt = _e(nc.sbuf_tensor("mt", [128, 16], F32))
        v2t = _e(nc.sbuf_tensor("v2t", [128, 16], F32))
        vt = _e(nc.sbuf_tensor("vt", [128, 16], F32))
        lt = _e(nc.sbuf_tensor("lt", [128, 16], F32))
        rt = _e(nc.sbuf_tensor("rt", [128, 16], F32))
        nmr = _e(nc.sbuf_tensor("nmr", [128, 16], F32))
        tmp1 = _e(nc.sbuf_tensor("tmp1", [128, 16], F32))
        ones = _e(nc.sbuf_tensor("ones", [128, 1], F32))
        psS0 = _e(nc.psum_tensor("psS0", [128, 1024], F32))
        psS1 = _e(nc.psum_tensor("psS1", [128, 1024], F32))
        psO0 = _e(nc.psum_tensor("psO0", [128, 1024], F32))
        psO1 = _e(nc.psum_tensor("psO1", [128, 1024], F32))
        s_Krc = [_e(nc.semaphore(f"s_K{i}")) for i in range(8)]
        s_V0 = _e(nc.semaphore("s_V0"))
        s_V1 = _e(nc.semaphore("s_V1"))
        g_V00 = _e(nc.semaphore("g_V00"))
        g_V01 = _e(nc.semaphore("g_V01"))
        g_qT0 = _e(nc.semaphore("g_qT0"))
        g_qT1 = _e(nc.semaphore("g_qT1"))
        g_cst = _e(nc.semaphore("g_cst"))
        g_x0 = _e(nc.semaphore("g_x0"))
        g_x1 = _e(nc.semaphore("g_x1"))
        pe = _e(nc.semaphore("pe"))
        acts = _e(nc.semaphore("acts"))
        ve = _e(nc.semaphore("ve"))
        st = _e(nc.semaphore("st"))
        block = _e(nc.Block())
        psS = [psS0, psS1]
        psO = [psO0, psO1]

        @block.gpsimd
        def _(g):
            g_V0p = [g_V00, g_V01]
            g_xp = [g_x0, g_x1]

            def kload(a, rc):
                grp = a * 8 + rc
                if grp >= 5:
                    pa, prc = divmod(grp - 5, 8)
                    g.wait_ge(pe, P["pe_S"][(pa, prc)])
                g.dma_start(
                    out=K_rg[:, (grp % 5) * 2048:(grp % 5 + 1) * 2048].rearrange(
                        "p (b ic rr) -> p b ic rr", b=2, ic=8),
                    in_=K_in[a, rc].rearrange("b p ic rr -> p b ic rr"),
                ).then_inc(s_Krc[rc], 16)

            def xload(p, b):
                idx = p * 2 + b
                if idx >= 2:
                    pb, bb = divmod(idx - 2, 2)
                    g.wait_ge(ve, P["ve_xadd"][(bb, pb)])
                g.dma_start(out=xres[:, (idx % 2) * 1024:(idx % 2 + 1) * 1024],
                            in_=xq_in[b, p]).then_inc(g_xp[idx % 2], 16)

            g.dma_start(
                out=V_sb[:, 0:8192].rearrange("p (jc f) -> p jc f", jc=8),
                in_=V_in[0, 0].rearrange("jc p f -> p jc f")).then_inc(g_V0p[0], 16)
            for rc in (1, 3, 5, 7):
                kload(0, rc)
            xload(0, 0)
            xload(0, 1)
            g.dma_start(
                out=V_sb[:, 8192:16384].rearrange("p (jc f) -> p jc f", jc=8),
                in_=V_in[1, 0].rearrange("jc p f -> p jc f")).then_inc(g_V0p[1], 16)
            for a in range(1, 8):
                for rc in (1, 3, 5, 7):
                    kload(a, rc)
                xload(a, 0)
                xload(a, 1)

        @block.sync
        def _(sy):
            s_Vp = [s_V0, s_V1]
            sy.dma_start(out=m1m[:, :], in_=m1_in).then_inc(g_cst, 16)
            sy.dma_start(out=m2m[:, :], in_=m2_in).then_inc(g_cst, 16)
            sy.dma_start(out=us_sb[:, :], in_=us_in).then_inc(g_cst, 16)
            sy.dma_start(out=c19_sb[:, :], in_=c19_in).then_inc(g_cst, 16)
            for a in range(8):
                for rc in (0, 2, 4, 6):
                    grp = a * 8 + rc
                    if grp >= 5:
                        pa, prc = divmod(grp - 5, 8)
                        sy.wait_ge(pe, P["pe_S"][(pa, prc)])
                    sy.dma_start(
                        out=K_rg[:, (grp % 5) * 2048:(grp % 5 + 1) * 2048].rearrange(
                            "p (b ic rr) -> p b ic rr", b=2, ic=8),
                        in_=K_in[a, rc].rearrange("b p ic rr -> p b ic rr"),
                    ).then_inc(s_Krc[rc], 16)
                if a >= 1:
                    for b in range(B):
                        sy.wait_ge(pe, P["pe_V"][(a - 1, b, 7)])
                        sy.dma_start(
                            out=V_sb[:, b * 8192:(b + 1) * 8192].rearrange(
                                "p (jc f) -> p jc f", jc=8),
                            in_=V_in[b, a].rearrange("jc p f -> p jc f")
                        ).then_inc(s_Vp[b], 16)
            sy.wait_ge(st, P["st_total"])

        @block.tensor
        def _(t):
            t.wait_ge(ve, P["ve_dm"])
            for _wu in range(110):
                t.matmul(psO0[:, 0:512], at0[:, 0:128], at0[:, 0:512],
                         start=True, stop=True)
            t.wait_ge(g_qT0, 16)
            t.wait_ge(g_cst, 16 * 4)
            s_Vp = [s_V0, s_V1]
            vgrp = 0
            for a in range(8):
                w = (8 - a) * 128
                for rc in range(8):
                    gidx = a * 8 + rc
                    if gidx >= 2:
                        t.wait_ge(acts, P["sig_seq"][gidx - 2])
                    t.wait_ge(s_Krc[rc], P["sld_K"][(a, rc)])
                    grp = a * 8 + rc
                    p2 = psS[rc % 2]
                    last = None
                    for b in range(B):
                        if gidx == 0 and b == 1:
                            t.wait_ge(g_qT1, 16)
                        for (off, pw) in _pieces(w):
                            for ic in range(8):
                                last = t.matmul(
                                    p2[:, off:off + pw],
                                    K_rg[:, (grp % 5) * 2048 + b * 1024 + ic * 128:
                                         (grp % 5) * 2048 + b * 1024 + ic * 128 + 128],
                                    qT_sb[:, (b * 8 + ic) * 1024 + a * 128 + off:
                                          (b * 8 + ic) * 1024 + a * 128 + off + pw],
                                    start=(b == 0 and ic == 0),
                                    stop=(b == 1 and ic == 7),
                                )
                    last.then_inc(pe, 1)
                for b in range(B):
                    if a == 0:
                        t.wait_ge([g_V00, g_V01][b], 16)
                    else:
                        t.wait_ge(s_Vp[b], P["sld_V"][(a, b)])
                    mk = P["ve_m0"] if b == 0 else P["ve_m1"]
                    for p in range(a, 8):
                        if vgrp >= 2:
                            t.wait_ge(ve, _ve_evac_by_idx(P, vgrp - 2))
                        po = psO[vgrp % 2]
                        att = at0 if b == 0 else at1
                        last = None
                        for rc in range(8):
                            if p == a:
                                t.wait_ge(ve, mk[(a, rc)])
                            for foff in (0, 512):
                                last = t.matmul(
                                    po[:, foff:foff + 512],
                                    att[:, rc * 1024 + (p - a) * 128:
                                        rc * 1024 + (p - a) * 128 + 128],
                                    V_sb[:, (b * 8 + rc) * 1024 + foff:
                                         (b * 8 + rc) * 1024 + foff + 512],
                                    start=(rc == 0), stop=(rc == 7 and a != 0),
                                )
                        if a == 0:
                            for foff in (0, 512):
                                last = t.matmul(
                                    po[:, foff:foff + 512],
                                    c19_sb[:, p * 128:(p + 1) * 128],
                                    us_sb[:, b * 1024 + foff:b * 1024 + foff + 512],
                                    start=False, stop=True,
                                )
                        last.then_inc(pe, 1)
                        vgrp += 1
            t.wait_ge(ve, P["ve_evac"][(7, 0, 7)])
            for b in range(B):
                if b >= 1:
                    t.wait_ge(ve, P["ve_gcev"][0])
                last = None
                for p in range(8):
                    t.wait_ge(ve, P["ve_apply"][(b, p)])
                    for foff in (0, 512):
                        last = t.matmul(
                            psO0[0:1, foff:foff + 512],
                            ones[:, 0:1],
                            acc[:, (b * 8 + p) * 1024 + foff:(b * 8 + p) * 1024 + foff + 512],
                            start=(p == 0), stop=(p == 7),
                        )
                last.then_inc(pe, 1)

        @block.scalar
        def _(s):
            s.dma_start(out=qT_sb[:, 0:8192].rearrange("p (ic m) -> p ic m", ic=8),
                        in_=qT_in[0].rearrange("ic p m -> p ic m")).then_inc(g_qT0, 16)
            s.dma_start(out=qT_sb[:, 8192:16384].rearrange("p (ic m) -> p ic m", ic=8),
                        in_=qT_in[1].rearrange("ic p m -> p ic m")).then_inc(g_qT1, 16)
            sc = Ctr()
            for a in range(8):
                w = (8 - a) * 128
                for rc in range(8):
                    s.wait_ge(acts, sc.v)
                    s.wait_ge(pe, P["pe_S"][(a, rc)])
                    s.activation(at0[:, rc * 1024:rc * 1024 + w],
                                 psS[rc % 2][:, 0:w], AF.Sigmoid,
                                 scale=1.0 / 32.0).then_inc(acts, 1)
                    sc.inc()
                s.wait_ge(acts, sc.v)
                s.wait_ge(ve, P["ve_vteps"][a])
                s.activation(lt[:, 2 * a:2 * a + 2], vt[:, 2 * a:2 * a + 2],
                             AF.Ln, scale=2.0 ** -20).then_inc(acts, 1)
                sc.inc()
                s.wait_ge(acts, sc.v)
                s.activation(rt[:, 2 * a:2 * a + 2], lt[:, 2 * a:2 * a + 2],
                             AF.Exp, scale=-0.5).then_inc(acts, 1)
                sc.inc()
                for b in range(B):
                    s.wait_ge(ve, P["ve_apply"][(b, a)])
                    s.dma_start(out=x1[b, a],
                                in_=acc[:, (b * 8 + a) * 1024:(b * 8 + a + 1) * 1024]
                                ).then_inc(st, 16)
            s.wait_ge(pe, P["pe_V"][(7, 0, 7)])
            for p in range(8):
                s.wait_ge(acts, sc.v)
                s.wait_ge(ve, P["ve_apply"][(0, p)])
                s.copy(at0[:, p * 1024:(p + 1) * 1024],
                       acc[:, p * 1024:(p + 1) * 1024]).then_inc(acts, 1)
                sc.inc()
            s.wait_ge(acts, sc.v)
            s.dma_start(out=x1b[0].rearrange("p ii f -> ii p f"),
                        in_=at0[:, :].rearrange("ii (p f) -> ii p f", p=8)
                        ).then_inc(st, 16)
            s.wait_ge(ve, P["ve_b16"][(1, 7)])
            s.dma_start(out=x1b[1].rearrange("p ii f -> ii p f"),
                        in_=at1[:, :].rearrange("ii (p f) -> ii p f", p=8)
                        ).then_inc(st, 16)
            for b in range(B):
                s.wait_ge(ve, P["ve_gcev"][b])
                s.dma_start(out=gcp[b:b + 1, :],
                            in_=sq[0:1, 0:1024]).then_inc(st, 16)

        @block.vector
        def _(v):
            vc = Ctr()

            def op(fn, *args, **kw):
                v.wait_ge(ve, vc.v)
                fn(*args, **kw).then_inc(ve, 1)
                vc.inc()

            op(v.memset, at0[:, 0:512], 0.0)
            op(v.memset, ones[:, :], 1.0)
            v.wait_ge(g_cst, 16 * 4)
            g_xp = [g_x0, g_x1]
            for a in range(8):
                w = (8 - a) * 128
                par = a % 2
                for rc in range(8):
                    v.wait_ge(acts, P["acts_sig"][(a, rc)])
                    sl0 = at0[:, rc * 1024:rc * 1024 + 128]
                    msl = slice(par * 1024 + rc * 128, par * 1024 + rc * 128 + 128)
                    op(v.tensor_tensor, sl0, sl0, m1m[:, msl], ALU.mult)
                    op(v.tensor_tensor, sl0, sl0, m2m[:, msl], ALU.add)
                    assert vc.v == P["ve_m0"][(a, rc)]
                    op(v.tensor_tensor, at1[:, rc * 1024:rc * 1024 + 128],
                       m1m[:, msl], sl0, ALU.subtract)
                    op(v.scalar_tensor_tensor, at1[:, rc * 1024:rc * 1024 + 128],
                       m2m[:, msl], 2.0, at1[:, rc * 1024:rc * 1024 + 128],
                       ALU.mult, ALU.add)
                    if w > 128:
                        op(v.tensor_scalar, at1[:, rc * 1024 + 128:rc * 1024 + w],
                       at0[:, rc * 1024 + 128:rc * 1024 + w], -1.0, 1.0,
                       ALU.mult, ALU.add)
                assert vc.v == P["ve_ready"][a]
                for b in range(B):
                    for p in range(a, 8):
                        v.wait_ge(pe, P["pe_V"][(a, b, p)])
                        asl = acc[:, (b * 8 + p) * 1024:(b * 8 + p + 1) * 1024]
                        po = psO[_vgrp_idx(a, b, p) % 2]
                        if a == 0:
                            op(v.tensor_copy, asl, po[:, :])
                        else:
                            op(v.tensor_tensor, asl, asl, po[:, :], ALU.add)
                        assert vc.v == P["ve_evac"][(a, b, p)]
                        if p == a:
                            idx = p * 2 + b
                            col = slice(2 * a + b, 2 * a + b + 1)
                            v.wait_ge(g_xp[idx % 2], P["gld_x"][(b, p)])
                            op(v.scalar_tensor_tensor, asl,
                               xres[:, (idx % 2) * 1024:(idx % 2 + 1) * 1024],
                               1.0, asl, ALU.mult, ALU.add, accum_out=mt[:, col])
                            assert vc.v == P["ve_xadd"][(b, p)]
                            op(v.scalar_tensor_tensor, sq[:, :], asl, 1.0, asl,
                               ALU.mult, ALU.mult, accum_out=v2t[:, col])
                            op(v.tensor_scalar, tmp1[:, col], mt[:, col],
                               mt[:, col], RAWEPS, ALU.mult, ALU.subtract)
                            op(v.scalar_tensor_tensor, vt[:, col], v2t[:, col],
                               1024.0, tmp1[:, col], ALU.mult, ALU.subtract)
                            assert vc.v == P["ve_vt"][(b, p)]
                            if b == 1:
                                assert vc.v == P["ve_vteps"][a]
                v.wait_ge(acts, P["acts_rstd"][a])
                for b in range(B):
                    col = slice(2 * a + b, 2 * a + b + 1)
                    op(v.tensor_scalar, nmr[:, col], mt[:, col], rt[:, col],
                       -1.0 / 1024.0, ALU.mult, ALU.mult)
                    assert vc.v == P["ve_nmr"][(b, a)]
                    asl = acc[:, (b * 8 + a) * 1024:(b * 8 + a + 1) * 1024]
                    op(v.tensor_scalar, asl, asl, rt[:, col], nmr[:, col],
                       ALU.mult, ALU.add)
                    assert vc.v == P["ve_apply"][(b, a)]
            for p in range(8):
                op(v.tensor_copy, at1[:, p * 1024:(p + 1) * 1024],
                   acc[:, (8 + p) * 1024:(9 + p) * 1024])
                assert vc.v == P["ve_b16"][(1, p)]
            for b in range(B):
                v.wait_ge(pe, P["pe_gc"][b])
                if b >= 1:
                    v.wait_ge(st, P["st_gc"][0])
                op(v.tensor_copy, sq[0:1, 0:1024], psO0[0:1, :])
                assert vc.v == P["ve_gcev"][b]
    return nc


def _vgrp_idx(a, b, p):
    i = 0
    for aa in range(a):
        i += 2 * (8 - aa)
    return i + b * (8 - a) + (p - a)


def _ve_evac_by_idx(P, vgrp):
    for (a, b, p), val in P["ve_evac"].items():
        if _vgrp_idx(a, b, p) == vgrp:
            return val
    raise KeyError(vgrp)


def l2_host_inputs(x, qT, kT, vT, usum):
    import ml_dtypes
    bf = ml_dtypes.bfloat16
    k_full = np.ascontiguousarray(qT[0:0]).reshape(0,)
    k_tok = np.ascontiguousarray(kT.transpose(0, 2, 1))
    v_tok = np.ascontiguousarray(vT.transpose(0, 2, 1))
    K_in = np.zeros((8, 8, B, 128, 8, 128), dtype=bf)
    for a in range(8):
        ks = k_tok[:, a::8, :].reshape(B, 8, 128, 8, 128)
        K_in[a] = ks.transpose(3, 0, 2, 1, 4)
    V_in = v_tok.reshape(B, 8, 8, 128, 1024)
    us_in = np.ascontiguousarray(usum.transpose(1, 0, 2)).reshape(8, B * 1024)
    c19 = np.zeros((8, 1024), dtype=bf)
    for A in range(8):
        for p in range(8):
            if A > p:
                c19[A, p * 128:(p + 1) * 128] = 1e-9
    maps = []
    for c in range(NC):
        rows = np.concatenate([np.arange(t * 128, (t + 1) * 128)
                               for t in snake_tiles(c)])
        qTc = np.ascontiguousarray(qT[:, :, rows]).reshape(B, 8, 128, 1024)
        xqc = np.ascontiguousarray(x[:, rows, :]).reshape(B, 8, 128, 1024)
        m1 = np.zeros((128, 2048), dtype=np.float32)
        for par, tm8 in ((0, c), (1, 7 - c)):
            for rc in range(8):
                slc = slice(par * 1024 + rc * 128, par * 1024 + rc * 128 + 128)
                if rc < tm8:
                    m1[:, slc] = 1.0
                elif rc == tm8:
                    jj = np.arange(128)[:, None]
                    mm = np.arange(128)[None, :]
                    m1[:, slc] = (jj <= mm).astype(np.float32)
        m2 = 1e-9 * (1.0 - m1)
        maps.append({
            "qT_in": qTc, "K_in": K_in, "V_in": V_in, "us_in": us_in,
            "c19_in": c19, "m1_in": m1.astype(bf), "m2_in": m2.astype(bf),
            "xq_in": xqc,
        })
    return maps


def l2_host_outputs(res):
    import ml_dtypes
    x1 = np.zeros((B, S, D), dtype=np.float32)
    x1bf = np.zeros((B, S, D), dtype=ml_dtypes.bfloat16)
    gcp = np.zeros((NC, B, D), dtype=np.float32)
    for c in range(NC):
        r = res.results[c]
        for pi, t in enumerate(snake_tiles(c)):
            x1[:, t * 128:(t + 1) * 128, :] = r["x1"][:, pi]
            x1bf[:, t * 128:(t + 1) * 128, :] = r["x1b"][:, pi]
        gcp[c] = r["gcp"].reshape(B, D)
    return x1, x1bf, gcp



def plan_l3():
    pe = Ctr(); acts = Ctr(); ve = Ctr(); gld = Ctr(); sld = Ctr(); st = Ctr()
    P = {}
    xp = [Ctr(), Ctr(), Ctr(), Ctr()]
    for b in range(B):
        for p in range(8):
            idx = b * 8 + p
            xp[idx % 4].inc(16)
            P["sld_x1s", b, p] = xp[idx % 4].v

    ve.inc(1); P["ve_dm"] = ve.v
    ve.inc(1); P["ve_ones"] = ve.v
    ve.inc(9); P["ve_gc16b"] = ve.v

    for oc in range(8):
        pe.inc(); P["pe_h1", 0, 0, oc] = pe.v
    for b in range(B):
        for oc in range(4):
            pe.inc(); P["pe_g1", b, oc] = pe.v
        for oc in range(2):
            pe.inc(); P["pe_g2", b, oc] = pe.v
        pe.inc(); P["pe_gf", b] = pe.v
    pe.inc(); P["pe_bc"] = pe.v
    for l in range(3):
        for b in range(B):
            if l > 0:
                for g in range(16):
                    pe.inc(); P["pe_tp", b, l, g] = pe.v
            for oc in range(8):
                if (b, l) != (0, 0):
                    pe.inc(); P["pe_h1", b, l, oc] = pe.v
            for oc in range(4):
                pe.inc(); P["pe_h2", b, l, oc] = pe.v
            for oc in range(2):
                pe.inc(); P["pe_h3", b, l, oc] = pe.v
            pe.inc(); P["pe_tf", b, l] = pe.v
    for oc in range(8):
        acts.inc(); P["acts_h1", 0, 0, oc] = acts.v
    for b in range(B):
        for oc in range(4):
            acts.inc(); P["acts_g1", b, oc] = acts.v
        for oc in range(2):
            acts.inc(); P["acts_g2", b, oc] = acts.v
    for b in range(B):
        acts.inc(); P["acts_gf", b] = acts.v
    ve.inc(); P["ve_bc"] = ve.v
    ve.inc(7); P["ve_AB"] = ve.v
    def _plan_hacts(b, l):
        for oc in range(8):
            if (b, l) != (0, 0):
                acts.inc(); P["acts_h1", b, l, oc] = acts.v
        for oc in range(4):
            acts.inc(); P["acts_h2", b, l, oc] = acts.v
        for oc in range(2):
            acts.inc(); P["acts_h3", b, l, oc] = acts.v

    def _plan_sig(b, l):
        acts.inc(); P["acts_tf", b, l] = acts.v

    for l in range(2):
        _plan_hacts(0, l); _plan_sig(0, l)
        _plan_hacts(1, l); _plan_sig(1, l)
    _plan_hacts(0, 2); _plan_sig(0, 2)
    for oc in range(8):
        acts.inc(); P["acts_h1", 1, 2, oc] = acts.v
    acts.inc(2); P["acts_rstd2", 0] = acts.v
    for oc in range(4):
        acts.inc(); P["acts_h2", 1, 2, oc] = acts.v
    for oc in range(2):
        acts.inc(); P["acts_h3", 1, 2, oc] = acts.v
    _plan_sig(1, 2)
    for p in range(8):
        acts.inc(2); P["acts_rstd1", p] = acts.v
    def _plan_CE(b, l):
        ve.inc(); P["ve_obf", b, l] = ve.v
        for g in range(16):
            ve.inc(); P["ve_tpev", b, l, g] = ve.v

    def _plan_T(b, l):
        ve.inc(2); P["ve_coup", b, l] = ve.v
        for p in range(8):
            ve.inc(3); P["ve_twc", b, l, p] = ve.v
        P["ve_twist", b, l] = ve.v

    po = Ctr()
    P["po"] = po

    def _plan_LN2(b):
        if b == 0:
            for p in range(8):
                po.inc(); P["po_s2add", 0, p] = po.v
                po.inc(); P["po_sq", 0, p] = po.v
                ve.inc()
                ve.inc(3)
                P["ve_vt", 0, p] = ve.v
            P["ve_vt2eps", 0] = ve.v
            ve.inc(2)
            P["ve_nmrb", 0] = ve.v
            for p in range(8):
                ve.inc(); P["ve_apply", 0, p] = ve.v
        else:
            for p in range(8):
                ve.inc(); P["ve_s2add1", p] = ve.v
                if p >= 1:
                    ve.inc(3)
                    P["ve_vt", 1, p - 1] = ve.v
                    P["ve_vteps1", p - 1] = ve.v
                if p >= 2:
                    ve.inc(2)
                    P["ve_apply", 1, p - 2] = ve.v
            ve.inc(3); P["ve_vt", 1, 7] = ve.v
            P["ve_vteps1", 7] = ve.v
            ve.inc(2); P["ve_apply", 1, 6] = ve.v
            ve.inc(2); P["ve_apply", 1, 7] = ve.v

    _plan_T(0, 0)
    _plan_CE(0, 1)
    _plan_T(1, 0)
    _plan_CE(1, 1)
    _plan_T(0, 1)
    _plan_CE(0, 2)
    _plan_T(1, 1)
    _plan_CE(1, 2)
    _plan_T(0, 2)
    _plan_LN2(0)
    _plan_T(1, 2)
    _plan_LN2(1)
    st.inc(16 * 9)
    P["st_total"] = st.v
    seq = [("acts_h1", 0, 0, oc) for oc in range(8)]
    for l in range(3):
        for b in range(B):
            if (b, l) != (0, 0):
                seq += [("acts_h1", b, l, oc) for oc in range(8)]
            seq += [("acts_h2", b, l, oc) for oc in range(4)]
            seq += [("acts_h3", b, l, oc) for oc in range(2)]
    P["psh_prev"] = [P[k] for k in seq]
    return P


def build_l3():
    nc = bass.Bass("TRN2", target_bir_lowering=False)
    x1_in = nc.dram_tensor("x1_in", [B, 8, 128, 1024], F32, kind="ExternalInput").ap()
    x1T_in = nc.dram_tensor("x1T_in", [B, 8, 128, 1024], BF16, kind="ExternalInput").ap()
    gcT_in = nc.dram_tensor("gcT_in", [128, 128], F32, kind="ExternalInput").ap()
    w1_in = nc.dram_tensor("w1_in", [8, 128, 1024], F32, kind="ExternalInput").ap()
    w2_in = nc.dram_tensor("w2_in", [8, 128, 512], F32, kind="ExternalInput").ap()
    w3_in = nc.dram_tensor("w3_in", [4, 128, 256], F32, kind="ExternalInput").ap()
    w4_in = nc.dram_tensor("w4_in", [128, 2], F32, kind="ExternalInput").ap()
    gw1_in = nc.dram_tensor("gw1_in", [8, 128, 512], F32, kind="ExternalInput").ap()
    gw2_in = nc.dram_tensor("gw2_in", [4, 128, 256], F32, kind="ExternalInput").ap()
    gw3_in = nc.dram_tensor("gw3_in", [128, 2], F32, kind="ExternalInput").ap()
    sml_in = nc.dram_tensor("sml_in", [128, 20], F32, kind="ExternalInput").ap()
    b4_in = nc.dram_tensor("b4_in", [1, 1], F32, kind="ExternalInput").ap()
    gb3_in = nc.dram_tensor("gb3_in", [1, 1], F32, kind="ExternalInput").ap()
    ar_in = nc.dram_tensor("ar_in", [1, 1], F32, kind="ExternalInput").ap()
    ident_in = nc.dram_tensor("ident_in", [128, 128], BF16, kind="ExternalInput").ap()
    x2 = nc.dram_tensor("x2", [B, 8, 128, 1024], F32, kind="ExternalOutput").ap()

    P = plan_l3()
    EPS = 1e-5
    RAWEPS = EPS * 1024.0 * 1024.0
    from contextlib import ExitStack
    with ExitStack() as _ctx:
        def _e(x):
            return _ctx.enter_context(x)
        w1s = _e(nc.sbuf_tensor("w1s", [128, 8 * 1024], BF16))
        w2s = _e(nc.sbuf_tensor("w2s", [128, 8 * 512], BF16))
        w3s = _e(nc.sbuf_tensor("w3s", [128, 4 * 256], BF16))
        w4s = _e(nc.sbuf_tensor("w4s", [128, 2], BF16))
        gw1s = _e(nc.sbuf_tensor("gw1s", [128, 8 * 512], BF16))
        gw2s = _e(nc.sbuf_tensor("gw2s", [128, 4 * 256], BF16))
        gw3s = _e(nc.sbuf_tensor("gw3s", [128, 2], BF16))
        sml_sb = _e(nc.sbuf_tensor("sml_sb", [128, 20], F32))
        gb3s = _e(nc.sbuf_tensor("gb3s", [1, 1], F32))
        gcTs = _e(nc.sbuf_tensor("gcTs", [128, 128], F32))
        gc16 = _e(nc.sbuf_tensor("gc16", [128, 16], F32))
        gc16b = _e(nc.sbuf_tensor("gc16b", [128, 16], BF16))
        idents = _e(nc.sbuf_tensor("idents", [128, 128], BF16))
        sc4 = _e(nc.sbuf_tensor("sc4", [1, 4], F32))
        ones1 = _e(nc.sbuf_tensor("ones1", [1, 128], F32))
        bc = _e(nc.sbuf_tensor("bc", [128, 4], F32))
        Ab = _e(nc.sbuf_tensor("Ab", [128, 2], F32))
        Bc = _e(nc.sbuf_tensor("Bc", [128, 1], F32))
        tAB = _e(nc.sbuf_tensor("tAB", [128, 2], F32))
        g1 = _e(nc.sbuf_tensor("g1", [128, 4], BF16))
        g2 = _e(nc.sbuf_tensor("g2", [128, 2], BF16))
        out0 = _e(nc.sbuf_tensor("out0", [128, 8 * 1024], F32))
        out1 = _e(nc.sbuf_tensor("out1", [128, 8 * 1024], F32))
        outT0 = _e(nc.sbuf_tensor("outT0", [128, 8 * 1024], BF16))
        outT1 = _e(nc.sbuf_tensor("outT1", [128, 8 * 1024], BF16))
        obf = _e(nc.sbuf_tensor("obf", [128, 8 * 1024], BF16))
        h1T = _e(nc.sbuf_tensor("h1T", [128, 8 * 1024], BF16))
        h2T = _e(nc.sbuf_tensor("h2T", [128, 4 * 1024], BF16))
        h3T = _e(nc.sbuf_tensor("h3T", [128, 2 * 1024], BF16))
        x1s = _e(nc.sbuf_tensor("x1s", [128, 4 * 1024], F32))
        tfs = _e(nc.sbuf_tensor("tfs", [128, 2 * 8], F32))
        coup = _e(nc.sbuf_tensor("coup", [128, 2 * 8], F32))
        coupn = _e(nc.sbuf_tensor("coupn", [128, 2 * 8], F32))
        wtmp = _e(nc.sbuf_tensor("wtmp", [128, 1024], F32))
        sq2 = _e(nc.sbuf_tensor("sq2", [128, 2 * 1024], F32))
        mt2 = _e(nc.sbuf_tensor("mt2", [128, 16], F32))
        v2t2 = _e(nc.sbuf_tensor("v2t2", [128, 16], F32))
        vt2 = _e(nc.sbuf_tensor("vt2", [128, 16], F32))
        lt2 = _e(nc.sbuf_tensor("lt2", [128, 16], F32))
        rt2 = _e(nc.sbuf_tensor("rt2", [128, 16], F32))
        nmr2 = _e(nc.sbuf_tensor("nmr2", [128, 16], F32))
        tsm = _e(nc.sbuf_tensor("tsm", [128, 16], F32))
        psH0 = _e(nc.psum_tensor("psH0", [128, 1024], F32))
        psH1 = _e(nc.psum_tensor("psH1", [128, 1024], F32))
        psE0 = _e(nc.psum_tensor("psE0", [128, 512], BF16))
        psE1 = _e(nc.psum_tensor("psE1", [128, 512], BF16))
        psG = _e(nc.psum_tensor("psG", [128, 32], F32))
        g_w1a = _e(nc.semaphore("g_w1a"))
        g_w1b = _e(nc.semaphore("g_w1b"))
        g_sm = _e(nc.semaphore("g_sm"))
        g_gc = _e(nc.semaphore("g_gc"))
        g_w2 = _e(nc.semaphore("g_w2"))
        s_oT0 = _e(nc.semaphore("s_oT0"))
        s_oT1 = _e(nc.semaphore("s_oT1"))
        g_sm2 = _e(nc.semaphore("g_sm2"))
        st_g = _e(nc.semaphore("st_g"))
        s_ob0 = _e(nc.semaphore("s_ob0"))
        s_ob1 = _e(nc.semaphore("s_ob1"))
        s_x10 = _e(nc.semaphore("s_x10"))
        s_x11 = _e(nc.semaphore("s_x11"))
        s_x12 = _e(nc.semaphore("s_x12"))
        s_x13 = _e(nc.semaphore("s_x13"))
        po = _e(nc.semaphore("po"))
        pe = _e(nc.semaphore("pe"))
        acts = _e(nc.semaphore("acts"))
        ve = _e(nc.semaphore("ve"))
        st = _e(nc.semaphore("st"))
        block = _e(nc.Block())
        psH = [psH0, psH1]
        psE = [psE0, psE1]
        outb = [out0, out1]
        outT = [outT0, outT1]

        @block.gpsimd
        def _(g):
            g.dma_start(out=w1s[:, 0:4096].rearrange("p (o c) -> p o c", o=4),
                        in_=w1_in[0:4].rearrange("o p c -> p o c")).then_inc(g_w1a, 16)
            g.dma_start(out=w1s[:, 4096:8192].rearrange("p (o c) -> p o c", o=4),
                        in_=w1_in[4:8].rearrange("o p c -> p o c")).then_inc(g_w1b, 16)
            g.dma_start(out=gw1s[:, :].rearrange("p (hc d) -> p hc d", hc=8),
                        in_=gw1_in.rearrange("hc p d -> p hc d")).then_inc(g_gc, 16)
            g.dma_start(out=gw2s[:, :].rearrange("p (hc d) -> p hc d", hc=4),
                        in_=gw2_in.rearrange("hc p d -> p hc d")).then_inc(g_gc, 16)
            g.dma_start(out=gw3s[:, :], in_=gw3_in).then_inc(g_gc, 16)
            g.dma_start(out=idents[:, :], in_=ident_in).then_inc(g_gc, 16)
            g.dma_start(out=gcTs[:, :], in_=gcT_in).then_inc(g_gc, 16)
            g.dma_start(out=w2s[:, :].rearrange("p (hc d) -> p hc d", hc=8),
                        in_=w2_in.rearrange("hc p d -> p hc d")).then_inc(g_w2, 16)
            g.dma_start(out=w3s[:, :].rearrange("p (hc d) -> p hc d", hc=4),
                        in_=w3_in.rearrange("hc p d -> p hc d")).then_inc(g_w2, 16)
            g.dma_start(out=w4s[:, :], in_=w4_in).then_inc(g_w2, 16)
            for p in range(8):
                osl = outb[0][:, p * 1024:(p + 1) * 1024]
                g.wait_ge(ve, P["ve_twc", 0, 2, p])
                g.wait_ge([s_x10, s_x11, s_x12, s_x13][p % 4], P["sld_x1s", 0, p])
                if p > 0:
                    g.wait_ge(po, P["po_sq", 0, p - 1])
                g.tensor_tensor(osl, osl,
                                x1s[:, (p % 4) * 1024:(p % 4 + 1) * 1024],
                                ALU.add).then_inc(po, 1)
                if p >= 2:
                    g.wait_ge(ve, P["ve_vt", 0, p - 2])
                g.wait_ge(po, P["po_s2add", 0, p])
                g.tensor_tensor(sq2[:, (p % 2) * 1024:(p % 2 + 1) * 1024],
                                osl, osl, ALU.mult).then_inc(po, 1)

        @block.sync
        def _(sy):
            s_x1p = [s_x10, s_x11, s_x12, s_x13]
            sy.dma_start(out=sml_sb[:, :], in_=sml_in).then_inc(g_sm, 16)
            sy.dma_start(out=outT[0][:, :].rearrange("p (fc t) -> p fc t", fc=8),
                         in_=x1T_in[0].rearrange("fc p t -> p fc t")).then_inc(s_oT0, 16)
            sy.dma_start(out=outb[0][:, :].rearrange("ii (p f) -> ii p f", p=8),
                         in_=x1_in[0].rearrange("p ii f -> ii p f")).then_inc(s_ob0, 16)
            sy.dma_start(out=outT[1][:, :].rearrange("p (fc t) -> p fc t", fc=8),
                         in_=x1T_in[1].rearrange("fc p t -> p fc t")).then_inc(s_oT1, 16)
            sy.dma_start(out=outb[1][:, :].rearrange("ii (p f) -> ii p f", p=8),
                         in_=x1_in[1].rearrange("p ii f -> ii p f")).then_inc(s_ob1, 16)
            sy.dma_start(out=sc4[0:1, 0:1], in_=b4_in).then_inc(g_sm2, 16)
            sy.dma_start(out=sc4[0:1, 1:2], in_=ar_in).then_inc(g_sm2, 16)
            sy.dma_start(out=gb3s[0:1, 0:1], in_=gb3_in).then_inc(g_sm2, 16)

            def x1load(idx):
                b, p = divmod(idx, 8)
                if idx >= 4:
                    pb, pp = divmod(idx - 4, 8)
                    if pb == 0:
                        sy.wait_ge(po, P["po_s2add", 0, pp])
                    else:
                        sy.wait_ge(ve, P["ve_s2add1", pp])
                sy.dma_start(out=x1s[:, (idx % 4) * 1024:(idx % 4 + 1) * 1024],
                             in_=x1_in[b, p]).then_inc(s_x1p[idx % 4], 16)

            for idx in range(12):
                x1load(idx)
            sy.wait_ge(ve, P["ve_apply", 0, 7])
            sy.dma_start(out=x2[0].rearrange("p ii f -> ii p f"),
                         in_=outb[0][:, :].rearrange("ii (p f) -> ii p f", p=8)
                         ).then_inc(st, 16)
            for idx in range(12, 16):
                x1load(idx)
            for p in (0, 2, 4, 6):
                sy.wait_ge(ve, P["ve_apply", 1, p])
                sy.dma_start(out=x2[1, p],
                             in_=out1[:, p * 1024:(p + 1) * 1024]).then_inc(st, 16)
            sy.wait_ge(st, 16 * 5)
            sy.wait_ge(st_g, 16 * 4)

        @block.tensor
        def _(t):
            pshi = [0]

            def psh_wait():
                i = pshi[0]
                if i >= 2:
                    t.wait_ge(acts, P["psh_prev"][i - 2])
                pshi[0] = i + 1

            t.wait_ge(ve, P["ve_dm"])
            for _wu in range(45):
                t.matmul(psH0[:, 0:512], obf[:, 0:128], obf[:, 0:512],
                         start=True, stop=True)
            t.wait_ge(g_w1a, 16)
            t.wait_ge(s_oT0, 16)
            for oc in range(8):
                if oc == 4:
                    t.wait_ge(g_w1b, 16)
                psh_wait()
                last = None
                for ic in range(8):
                    for pc in (0, 512):
                        last = t.matmul(
                            psH[oc % 2][:, pc:pc + 512],
                            w1s[:, oc * 1024 + ic * 128:oc * 1024 + ic * 128 + 128],
                            outT[0][:, ic * 1024 + pc:ic * 1024 + pc + 512],
                            start=(ic == 0), stop=(ic == 7))
                last.then_inc(pe, 1)
            t.wait_ge(g_gc, 16 * 5)
            t.wait_ge(ve, P["ve_gc16b"])
            for b in range(B):
                for oc in range(4):
                    last = None
                    for ic in range(8):
                        last = t.matmul(psG[:, oc:oc + 1],
                                        gw1s[:, ic * 512 + oc * 128:ic * 512 + oc * 128 + 128],
                                        gc16b[:, b * 8 + ic:b * 8 + ic + 1],
                                        start=(ic == 0), stop=(ic == 7))
                    last.then_inc(pe, 1)
                for oc in range(2):
                    t.wait_ge(acts, P["acts_g1", b, 3])
                    last = None
                    for ic in range(4):
                        last = t.matmul(psG[:, 4 + oc:5 + oc],
                                        gw2s[:, ic * 256 + oc * 128:ic * 256 + oc * 128 + 128],
                                        g1[:, ic:ic + 1],
                                        start=(ic == 0), stop=(ic == 3))
                    last.then_inc(pe, 1)
                t.wait_ge(acts, P["acts_g2", b, 1])
                last = None
                for icc in range(2):
                    last = t.matmul(psG[0:1, 6 + b:7 + b],
                                    gw3s[:, icc:icc + 1], g2[:, icc:icc + 1],
                                    start=(icc == 0), stop=(icc == 1))
                last.then_inc(pe, 1)
            t.wait_ge(acts, P["acts_gf", 1])
            t.wait_ge(g_sm2, 16 * 3)
            t.wait_ge(ve, P["ve_ones"])
            t.matmul(psG[:, 8:12], ones1[0:1, :], sc4[0:1, :],
                     start=True, stop=True).then_inc(pe, 1)
            t.wait_ge(g_w2, 16 * 3)
            for l in range(3):
                for b in range(B):
                    if l > 0:
                        t.wait_ge(ve, P["ve_obf", b, l])
                        gi = 0
                        for fc in range(8):
                            for pg in range(2):
                                if gi >= 2:
                                    t.wait_ge(ve, P["ve_tpev", b, l, gi - 2])
                                last = None
                                for j in range(4):
                                    p = pg * 4 + j
                                    last = t.transpose(
                                        psE[gi % 2][:, j * 128:(j + 1) * 128],
                                        obf[:, p * 1024 + fc * 128:p * 1024 + fc * 128 + 128],
                                        idents[:, :])
                                last.then_inc(pe, 1)
                                gi += 1
                        t.wait_ge(ve, P["ve_tpev", b, l, 15])
                    else:
                        if b == 1:
                            t.wait_ge(s_oT1, 16)
                    if (b, l) != (0, 0):
                        for oc in range(8):
                            psh_wait()
                            last = None
                            for ic in range(8):
                                for pc in (0, 512):
                                    last = t.matmul(
                                        psH[oc % 2][:, pc:pc + 512],
                                        w1s[:, oc * 1024 + ic * 128:oc * 1024 + ic * 128 + 128],
                                        outT[b][:, ic * 1024 + pc:ic * 1024 + pc + 512],
                                        start=(ic == 0), stop=(ic == 7))
                            last.then_inc(pe, 1)
                    t.wait_ge(acts, P["acts_h1", b, l, 7])
                    for oc in range(4):
                        psh_wait()
                        last = None
                        for ic in range(8):
                            for pc in (0, 512):
                                last = t.matmul(
                                    psH[oc % 2][:, pc:pc + 512],
                                    w2s[:, ic * 512 + oc * 128:ic * 512 + oc * 128 + 128],
                                    h1T[:, ic * 1024 + pc:ic * 1024 + pc + 512],
                                    start=(ic == 0), stop=(ic == 7))
                        last.then_inc(pe, 1)
                    t.wait_ge(acts, P["acts_h2", b, l, 3])
                    for oc in range(2):
                        psh_wait()
                        last = None
                        for ic in range(4):
                            for pc in (0, 512):
                                last = t.matmul(
                                    psH[oc % 2][:, pc:pc + 512],
                                    w3s[:, ic * 256 + oc * 128:ic * 256 + oc * 128 + 128],
                                    h2T[:, ic * 1024 + pc:ic * 1024 + pc + 512],
                                    start=(ic == 0), stop=(ic == 3))
                        last.then_inc(pe, 1)
                    t.wait_ge(acts, P["acts_h3", b, l, 1])
                    if not (b == 0 and l == 0):
                        lb, ll = (0, l) if b == 1 else (1, l - 1)
                        t.wait_ge(acts, P["acts_tf", lb, ll])
                    last = None
                    for p in range(8):
                        for icc in range(2):
                            last = t.matmul(
                                psG[:, 12 + p:13 + p],
                                h3T[:, icc * 1024 + p * 128:icc * 1024 + p * 128 + 128],
                                w4s[:, icc:icc + 1],
                                start=(icc == 0), stop=(icc == 1))
                    last.then_inc(pe, 1)

        @block.scalar
        def _(s):
            sc = Ctr()

            def act(fn_out, fn_in, func, **kw):
                s.wait_ge(acts, sc.v)
                s.activation(fn_out, fn_in, func, **kw).then_inc(acts, 1)
                sc.inc()

            s.wait_ge(g_sm, 16)
            for oc in range(8):
                s.wait_ge(pe, P["pe_h1", 0, 0, oc])
                act(h1T[:, oc * 1024:(oc + 1) * 1024], psH[oc % 2][:, :],
                    AF.Gelu, bias=sml_sb[:, oc:oc + 1])
            for b in range(B):
                for oc in range(4):
                    s.wait_ge(pe, P["pe_g1", b, oc])
                    act(g1[:, oc:oc + 1], psG[:, oc:oc + 1], AF.Gelu,
                        bias=sml_sb[:, 14 + oc:15 + oc])
                for oc in range(2):
                    s.wait_ge(pe, P["pe_g2", b, oc])
                    act(g2[:, oc:oc + 1], psG[:, 4 + oc:5 + oc], AF.Gelu,
                        bias=sml_sb[:, 18 + oc:19 + oc])
            s.wait_ge(g_sm2, 16 * 3)
            for b in range(B):
                s.wait_ge(pe, P["pe_gf", b])
                act(sc4[0:1, 2 + b:3 + b], psG[0:1, 6 + b:7 + b], AF.Sigmoid,
                    bias=gb3s[0:1, 0:1])
            s.wait_ge(ve, P["ve_bc"])
            def hacts(b, l):
                if (b, l) != (0, 0):
                    for oc in range(8):
                        s.wait_ge(pe, P["pe_h1", b, l, oc])
                        act(h1T[:, oc * 1024:(oc + 1) * 1024], psH[oc % 2][:, :],
                            AF.Gelu, bias=sml_sb[:, oc:oc + 1])
                for oc in range(4):
                    s.wait_ge(pe, P["pe_h2", b, l, oc])
                    act(h2T[:, oc * 1024:(oc + 1) * 1024], psH[oc % 2][:, :],
                        AF.Gelu, bias=sml_sb[:, 8 + oc:9 + oc])
                for oc in range(2):
                    s.wait_ge(pe, P["pe_h3", b, l, oc])
                    act(h3T[:, oc * 1024:(oc + 1) * 1024], psH[oc % 2][:, :],
                        AF.Gelu, bias=sml_sb[:, 12 + oc:13 + oc])

            def sigact(b, l):
                s.wait_ge(pe, P["pe_tf", b, l])
                act(tfs[:, b * 8:(b + 1) * 8], psG[:, 12:20], AF.Sigmoid,
                    bias=bc[:, 0:1])

            def rstd2(b):
                s.wait_ge(ve, P["ve_vt2eps", b])
                act(lt2[:, b * 8:(b + 1) * 8], vt2[:, b * 8:(b + 1) * 8], AF.Ln,
                    scale=2.0 ** -20)
                act(rt2[:, b * 8:(b + 1) * 8], lt2[:, b * 8:(b + 1) * 8], AF.Exp,
                    scale=-0.5)

            for l in range(2):
                hacts(0, l); sigact(0, l)
                hacts(1, l); sigact(1, l)
            hacts(0, 2); sigact(0, 2)
            for oc in range(8):
                s.wait_ge(pe, P["pe_h1", 1, 2, oc])
                act(h1T[:, oc * 1024:(oc + 1) * 1024], psH[oc % 2][:, :],
                    AF.Gelu, bias=sml_sb[:, oc:oc + 1])
            rstd2(0)
            for oc in range(4):
                s.wait_ge(pe, P["pe_h2", 1, 2, oc])
                act(h2T[:, oc * 1024:(oc + 1) * 1024], psH[oc % 2][:, :],
                    AF.Gelu, bias=sml_sb[:, 8 + oc:9 + oc])
            for oc in range(2):
                s.wait_ge(pe, P["pe_h3", 1, 2, oc])
                act(h3T[:, oc * 1024:(oc + 1) * 1024], psH[oc % 2][:, :],
                    AF.Gelu, bias=sml_sb[:, 12 + oc:13 + oc])
            sigact(1, 2)
            def rstd1(p):
                s.wait_ge(ve, P["ve_vteps1", p])
                act(lt2[:, 8 + p:9 + p], vt2[:, 8 + p:9 + p], AF.Ln,
                    scale=2.0 ** -20)
                act(rt2[:, 8 + p:9 + p], lt2[:, 8 + p:9 + p], AF.Exp, scale=-0.5)

            for p in range(8):
                rstd1(p)
                if p >= 3 and p % 2 == 1:
                    pw = p - 2
                    s.wait_ge(ve, P["ve_apply", 1, pw])
                    s.dma_start(out=x2[1, pw],
                                in_=out1[:, pw * 1024:(pw + 1) * 1024]).then_inc(st_g, 16)
            s.wait_ge(ve, P["ve_apply", 1, 7])
            s.dma_start(out=x2[1, 7],
                        in_=out1[:, 7 * 1024:8 * 1024]).then_inc(st_g, 16)

        @block.vector
        def _(v):
            vc = Ctr()

            def op(fn, *args, **kw):
                v.wait_ge(ve, vc.v)
                fn(*args, **kw).then_inc(ve, 1)
                vc.inc()

            op(v.memset, obf[:, 0:512], 0.0)
            op(v.memset, ones1[0:1, :], 1.0)
            v.wait_ge(g_gc, 16 * 5)
            op(v.tensor_tensor, gc16[:, :], gcTs[:, 0:16], gcTs[:, 16:32], ALU.add)
            for cr in range(2, 8):
                op(v.tensor_tensor, gc16[:, :], gc16[:, :],
                   gcTs[:, cr * 16:(cr + 1) * 16], ALU.add)
            op(v.tensor_scalar, gc16[:, :], gc16[:, :], 1.0 / 8192.0, None, ALU.mult)
            op(v.tensor_copy, gc16b[:, :], gc16[:, :])
            assert vc.v == P["ve_gc16b"]
            v.wait_ge(pe, P["pe_bc"])
            op(v.tensor_copy, bc[:, :], psG[:, 8:12])
            assert vc.v == P["ve_bc"]
            for b in range(B):
                op(v.tensor_tensor, tAB[:, b:b + 1], bc[:, 1:2], bc[:, 2 + b:3 + b], ALU.mult)
            op(v.tensor_scalar, tAB[:, :], tAB[:, :], 1.4, None, ALU.mult)
            op(v.tensor_scalar, tsm[:, 0:1], bc[:, 1:2], -1.0, 0.1, ALU.mult, ALU.add)
            op(v.tensor_tensor, Ab[:, 0:1], tsm[:, 0:1], tAB[:, 0:1], ALU.add)
            op(v.tensor_tensor, Ab[:, 1:2], tsm[:, 0:1], tAB[:, 1:2], ALU.add)
            op(v.tensor_scalar, Bc[:, 0:1], bc[:, 1:2], 0.6, None, ALU.mult)
            assert vc.v == P["ve_AB"]

            def twist(b, l):
                if l == 0:
                    v.wait_ge([s_ob0, s_ob1][b], 16)
                v.wait_ge(acts, P["acts_tf", b, l])
                op(v.tensor_scalar, coup[:, b * 8:(b + 1) * 8],
                   tfs[:, b * 8:(b + 1) * 8], Bc[:, 0:1], Ab[:, b:b + 1],
                   ALU.mult, ALU.add)
                op(v.tensor_scalar, coupn[:, b * 8:(b + 1) * 8],
                   coup[:, b * 8:(b + 1) * 8], -1.0, None, ALU.mult)
                assert vc.v == P["ve_coup", b, l]
                w4 = wtmp[:, :].rearrange("p (a q) -> p a q", a=4)
                for p in range(8):
                    osl = outb[b][:, p * 1024:(p + 1) * 1024]
                    o4 = osl.rearrange("p (a q) -> p a q", a=4)
                    cp = coup[:, b * 8 + p:b * 8 + p + 1]
                    op(v.tensor_scalar, wtmp[:, :], osl, cp, None, ALU.mult)
                    op(v.scalar_tensor_tensor, o4[:, 0:4:3, :], w4[:, 2:0:-1, :],
                       1.0, o4[:, 0:4:3, :], ALU.mult, ALU.add)
                    op(v.scalar_tensor_tensor, o4[:, 1:3, :], w4[:, 3::-3, :],
                       -1.0, o4[:, 1:3, :], ALU.mult, ALU.add)
                assert vc.v == P["ve_twist", b, l]

            def ln2(b):
                assert b == 0
                bsl = slice(0, 8)
                for p in range(8):
                    col = slice(p, p + 1)
                    osl = outb[0][:, p * 1024:(p + 1) * 1024]
                    v.wait_ge(po, P["po_s2add", 0, p])
                    op(v.reduce_sum, mt2[:, col], osl, AX.X)
                    v.wait_ge(po, P["po_sq", 0, p])
                    op(v.reduce_sum, v2t2[:, col],
                       sq2[:, (p % 2) * 1024:(p % 2 + 1) * 1024], AX.X)
                    op(v.tensor_scalar, tsm[:, col], mt2[:, col], mt2[:, col],
                       RAWEPS, ALU.mult, ALU.subtract)
                    op(v.scalar_tensor_tensor, vt2[:, col], v2t2[:, col],
                       1024.0, tsm[:, col], ALU.mult, ALU.subtract)
                    assert vc.v == P["ve_vt", 0, p]
                assert vc.v == P["ve_vt2eps", 0]
                v.wait_ge(acts, P["acts_rstd2", 0])
                op(v.tensor_tensor, nmr2[:, bsl], mt2[:, bsl], rt2[:, bsl], ALU.mult)
                op(v.tensor_scalar, nmr2[:, bsl], nmr2[:, bsl], -1.0 / 1024.0, None, ALU.mult)
                assert vc.v == P["ve_nmrb", 0]
                for p in range(8):
                    col = slice(p, p + 1)
                    osl = outb[0][:, p * 1024:(p + 1) * 1024]
                    op(v.tensor_scalar, osl, osl, rt2[:, col], nmr2[:, col],
                       ALU.mult, ALU.add)
                    assert vc.v == P["ve_apply", 0, p]

            def ln2b1_head(p):
                idx = 8 + p
                col = slice(idx, idx + 1)
                osl = outb[1][:, p * 1024:(p + 1) * 1024]
                idx4 = [s_x10, s_x11, s_x12, s_x13][idx % 4]
                v.wait_ge(idx4, P["sld_x1s", 1, p])
                op(v.scalar_tensor_tensor, osl,
                   x1s[:, (idx % 4) * 1024:(idx % 4 + 1) * 1024], 1.0, osl,
                   ALU.mult, ALU.add, accum_out=mt2[:, col])
                assert vc.v == P["ve_s2add1", p]

            def ln2b1_var(p):
                idx = 8 + p
                col = slice(idx, idx + 1)
                osl = outb[1][:, p * 1024:(p + 1) * 1024]
                op(v.scalar_tensor_tensor, wtmp[:, :], osl, 1.0, osl,
                   ALU.mult, ALU.mult, accum_out=v2t2[:, col])
                op(v.tensor_scalar, tsm[:, col], mt2[:, col], mt2[:, col],
                   RAWEPS, ALU.mult, ALU.subtract)
                op(v.scalar_tensor_tensor, vt2[:, col], v2t2[:, col],
                   1024.0, tsm[:, col], ALU.mult, ALU.subtract)
                assert vc.v == P["ve_vt", 1, p]
                assert vc.v == P["ve_vteps1", p]

            def ln2b1_apply(p):
                idx = 8 + p
                col = slice(idx, idx + 1)
                osl = outb[1][:, p * 1024:(p + 1) * 1024]
                v.wait_ge(acts, P["acts_rstd1", p])
                op(v.tensor_scalar, nmr2[:, col], mt2[:, col], rt2[:, col],
                   -1.0 / 1024.0, ALU.mult, ALU.mult)
                op(v.tensor_scalar, osl, osl, rt2[:, col], nmr2[:, col],
                   ALU.mult, ALU.add)
                assert vc.v == P["ve_apply", 1, p]

            def ln2b1():
                for p in range(8):
                    ln2b1_head(p)
                    if p >= 1:
                        ln2b1_var(p - 1)
                    if p >= 2:
                        ln2b1_apply(p - 2)
                ln2b1_var(7)
                ln2b1_apply(6)
                ln2b1_apply(7)

            def ce(b, l):
                if (b, l) != (0, 1):
                    ob, ol = (1, l - 1) if b == 0 else (0, l)
                    v.wait_ge(pe, P["pe_tp", ob, ol, 15])
                op(v.tensor_copy, obf[:, :], outb[b][:, :])
                assert vc.v == P["ve_obf", b, l]
                for g in range(16):
                    v.wait_ge(pe, P["pe_tp", b, l, g])
                    fc, pg = divmod(g, 2)
                    op(v.tensor_copy,
                       outT[b][:, fc * 1024 + pg * 512:fc * 1024 + pg * 512 + 512],
                       psE[g % 2][:, :])
                    assert vc.v == P["ve_tpev", b, l, g]

            twist(0, 0)
            ce(0, 1)
            twist(1, 0)
            ce(1, 1)
            twist(0, 1)
            ce(0, 2)
            twist(1, 1)
            ce(1, 2)
            twist(0, 2)
            ln2(0)
            twist(1, 2)
            ln2b1()
    return nc


def l3_host_inputs(x1_full, x1bf, gcp, W):
    import ml_dtypes
    bf = ml_dtypes.bfloat16
    gcT = np.zeros((128, 128), dtype=np.float32)
    for cr in range(NC):
        for b in range(B):
            gcT[:, cr * 16 + b * 8:cr * 16 + (b + 1) * 8] = (
                gcp[cr, b].reshape(8, 128).T)
    common = {
        "gcT_in": gcT,
        "w1_in": np.ascontiguousarray(
            W["cn_w1"].reshape(8, 128, 8, 128).transpose(2, 1, 0, 3)
        ).reshape(8, 128, 1024),
        "w2_in": W["cn_w2"].reshape(8, 128, 512),
        "w3_in": W["cn_w3"].reshape(4, 128, 256),
        "w4_in": np.ascontiguousarray(W["cn_w4"].reshape(2, 128).T),
        "gw1_in": W["gc_w1"].reshape(8, 128, 512),
        "gw2_in": W["gc_w2"].reshape(4, 128, 256),
        "gw3_in": np.ascontiguousarray(W["gc_w3"].reshape(2, 128).T),
        "sml_in": np.concatenate(
            [W["cn_b1"].reshape(8, 128).T, W["cn_b2"].reshape(4, 128).T,
             W["cn_b3"].reshape(2, 128).T, W["gc_b1"].reshape(4, 128).T,
             W["gc_b2"].reshape(2, 128).T], axis=1),
        "b4_in": W["cn_b4"].reshape(1, 1),
        "gb3_in": W["gc_b3"].reshape(1, 1),
        "ar_in": W["adaptive_range"].reshape(1, 1),
        "ident_in": np.eye(128, dtype=bf),
    }
    common = {k: np.ascontiguousarray(v, dtype=v.dtype) for k, v in common.items()}
    maps = []
    for c in range(NC):
        rows = np.concatenate([np.arange(t * 128, (t + 1) * 128)
                               for t in snake_tiles(c)])
        x1c = np.ascontiguousarray(x1_full[:, rows, :]).reshape(B, 8, 128, 1024)
        x1Tc = np.ascontiguousarray(
            x1bf[:, rows, :].transpose(0, 2, 1)).reshape(B, 8, 128, 1024)
        maps.append({"x1_in": x1c, "x1T_in": x1Tc, **common})
    return maps


def l3_host_outputs(res):
    out = np.zeros((B, S, D), dtype=np.float32)
    for c in range(NC):
        r = res.results[c]
        for pi, t in enumerate(snake_tiles(c)):
            out[:, t * 128:(t + 1) * 128, :] = r["x2"][:, pi]
    return out



from concourse.bass_utils import run_bass_kernel_spmd

_NC_CACHE = {}


def _get(name, builder):
    if name not in _NC_CACHE:
        _NC_CACHE[name] = builder()
    return _NC_CACHE[name]


def kernel(L0_attn_x, L0_attn_Wq, L0_attn_Wk, L0_attn_Wv,
           L0_ffn_cn_w1, L0_ffn_cn_b1, L0_ffn_cn_w2, L0_ffn_cn_b2,
           L0_ffn_cn_w3, L0_ffn_cn_b3, L0_ffn_cn_w4, L0_ffn_cn_b4,
           L0_ffn_gc_w1, L0_ffn_gc_b1, L0_ffn_gc_w2, L0_ffn_gc_b2,
           L0_ffn_gc_w3, L0_ffn_gc_b3, L0_ffn_adaptive_range):
    x = np.asarray(L0_attn_x, dtype=np.float32)
    cores = list(range(NC))

    nc1 = _get("l1", build_l1)
    res1 = run_bass_kernel_spmd(
        nc1, l1_host_inputs(x, np.asarray(L0_attn_Wq), np.asarray(L0_attn_Wk),
                            np.asarray(L0_attn_Wv)), core_ids=cores)
    qT, kT, vT, usum = l1_host_outputs(res1)

    nc2 = _get("l2", build_l2)
    res2 = run_bass_kernel_spmd(nc2, l2_host_inputs(x, qT, kT, vT, usum),
                                core_ids=cores)
    x1, x1bf, gcp = l2_host_outputs(res2)

    W = {
        "cn_w1": np.asarray(L0_ffn_cn_w1), "cn_b1": np.asarray(L0_ffn_cn_b1),
        "cn_w2": np.asarray(L0_ffn_cn_w2), "cn_b2": np.asarray(L0_ffn_cn_b2),
        "cn_w3": np.asarray(L0_ffn_cn_w3), "cn_b3": np.asarray(L0_ffn_cn_b3),
        "cn_w4": np.asarray(L0_ffn_cn_w4), "cn_b4": np.asarray(L0_ffn_cn_b4),
        "gc_w1": np.asarray(L0_ffn_gc_w1), "gc_b1": np.asarray(L0_ffn_gc_b1),
        "gc_w2": np.asarray(L0_ffn_gc_w2), "gc_b2": np.asarray(L0_ffn_gc_b2),
        "gc_w3": np.asarray(L0_ffn_gc_w3), "gc_b3": np.asarray(L0_ffn_gc_b3),
        "adaptive_range": np.asarray(L0_ffn_adaptive_range),
    }
    nc3 = _get("l3", build_l3)
    res3 = run_bass_kernel_spmd(nc3, l3_host_inputs(x1, x1bf, gcp, W), core_ids=cores)
    return l3_host_outputs(res3)
